# version 08
# baseline (speedup 1.0000x reference)
import contextlib
import numpy as np
import concourse.bass as bass
import concourse.mybir as mybir
from concourse.bass_utils import run_bass_kernel_spmd

F32 = mybir.dt.float32
F32R = mybir.dt.float32r
PDT = mybir.dt.float32
BF16 = mybir.dt.bfloat16
AF = mybir.ActivationFunctionType
ALU = mybir.AluOpType
AX = mybir.AxisListType

S = 4096
E = 1024
NT = S // 128
NC = E // 128
NEG = -1.0e30
SCALE8 = 0.125
SCALE96 = float(1.0 / np.sqrt(96.0))

ENGINES = ("tensor", "vector", "scalar", "gpsimd", "sync")
QUEUES = ("sync", "scalar", "gpsimd")

PHASES = (0, 1, 2, 3)
DBG_X1 = False
DBG_U = False
P0_PARTS = "abc"
P0A_STEP = 99
P1_PARTS = "abcd"
P1A_STEP = 9


class Tok:
    __slots__ = ("engine", "idx")
    def __init__(self, engine, idx):
        self.engine = engine; self.idx = idx


class Buf:
    __slots__ = ("w", "wl", "r")
    def __init__(self):
        self.w = None; self.wl = []; self.r = []
    def rd(self):
        return ([self.w] if self.w is not None else []) + self.wl
    def wr(self):
        return self.rd() + self.r


class Em:
    NPOOL = 8

    def __init__(self):
        self.ops = {e: [] for e in ENGINES}
        self.barriers = []

    def _emit(self, engine, fn, deps, risky=False, is_dma=False,
              reads=(), writes=(), writes_also=()):
        dd = []
        for b in reads:
            dd += b.rd()
        for b in writes:
            dd += b.wr()
        dd += [d for d in deps if d is not None]
        rec = dict(fn=fn, deps=dd, marked=False, risky=risky,
                   is_dma=is_dma, inc_val=None, dma_idx=None)
        self.ops[engine].append(rec)
        tok = Tok(engine, len(self.ops[engine]) - 1)
        for b in reads:
            b.r.append(tok)
        for b in writes:
            b.w = tok; b.wl = []; b.r = []
        for b in writes_also:
            b.wl.append(tok)
        return tok

    def t(self, fn, deps=(), risky=False, reads=(), writes=(), writes_also=()):
        return self._emit("tensor", fn, list(deps), risky, False, reads, writes,
                          writes_also)
    def v(self, fn, deps=(), risky=False, reads=(), writes=(), writes_also=()):
        return self._emit("vector", fn, list(deps), risky, False, reads, writes,
                          writes_also)
    def s(self, fn, deps=(), risky=False, reads=(), writes=(), writes_also=()):
        return self._emit("scalar", fn, list(deps), risky, False, reads, writes,
                          writes_also)
    def g(self, fn, deps=(), risky=False, reads=(), writes=(), writes_also=()):
        return self._emit("gpsimd", fn, list(deps), risky, False, reads, writes,
                          writes_also)
    def dma(self, queue, fn, deps=(), reads=(), writes=(), writes_also=()):
        return self._emit(queue, fn, list(deps), False, True, reads, writes,
                          writes_also)

    def barrier(self):
        self.barriers.append({e: len(self.ops[e]) for e in ENGINES})

    def finalize(self, block, sems, dsems):
        for q in QUEUES:
            n = 0
            for rec in self.ops[q]:
                if rec["is_dma"]:
                    rec["dma_idx"] = n; n += 1
        for e in ENGINES:
            for rec in self.ops[e]:
                for d in rec["deps"]:
                    dep = self.ops[d.engine][d.idx]
                    if not dep["is_dma"]:
                        dep["marked"] = True
        for bar in self.barriers:
            for e in ENGINES:
                if bar[e] > 0:
                    self.ops[e][bar[e] - 1]["marked"] = True
        for e in ENGINES:
            cnt = 0
            for rec in self.ops[e]:
                if (not rec["is_dma"]) and rec["marked"]:
                    cnt += 1; rec["inc_val"] = cnt
            assert cnt < 60000, (e, cnt)
            ndma = sum(1 for r in self.ops[e] if r["is_dma"])
            assert (ndma // self.NPOOL + 1) * 16 < 60000, (e, ndma)
        bar_targets = []
        for bar in self.barriers:
            tg = {}
            for e in ENGINES:
                mc = dc = 0
                for rec in self.ops[e][:bar[e]]:
                    if rec["is_dma"]:
                        dc += 1
                    elif rec["marked"]:
                        mc += 1
                tg[e] = (mc, dc)
            bar_targets.append(tg)

        def make_body(eng_name):
            def body(eng):
                wm = {}
                bars = [(self.barriers[i][eng_name], bar_targets[i])
                        for i in range(len(self.barriers))]
                bi = 0
                for i, rec in enumerate(self.ops[eng_name]):
                    while bi < len(bars) and bars[bi][0] <= i:
                        for oe in ENGINES:
                            mc, dc = bars[bi][1][oe]
                            if (oe != eng_name and mc > 0
                                    and wm.get(("c", oe), 0) < mc):
                                eng.wait_ge(sems[oe], mc); wm[("c", oe)] = mc
                            if oe in QUEUES and dc > 0:
                                for sl in range(self.NPOOL):
                                    nsl = (dc - sl + self.NPOOL - 1) // self.NPOOL
                                    if nsl > 0 and wm.get(("d", oe, sl), 0) < nsl:
                                        eng.wait_ge(dsems[oe][sl], 16 * nsl)
                                        wm[("d", oe, sl)] = nsl
                        bi += 1
                    for d in rec["deps"]:
                        dep = self.ops[d.engine][d.idx]
                        if dep["is_dma"]:
                            sl = dep["dma_idx"] % self.NPOOL
                            tgt = dep["dma_idx"] // self.NPOOL + 1
                            if wm.get(("d", d.engine, sl), 0) < tgt:
                                eng.wait_ge(dsems[d.engine][sl], 16 * tgt)
                                wm[("d", d.engine, sl)] = tgt
                        else:
                            if d.engine == eng_name == "tensor":
                                continue
                            tgt = dep["inc_val"]
                            if wm.get(("c", d.engine), 0) < tgt:
                                eng.wait_ge(sems[d.engine], tgt)
                                wm[("c", d.engine)] = tgt
                    if rec["is_dma"] and rec["dma_idx"] >= self.NPOOL:
                        sl = rec["dma_idx"] % self.NPOOL
                        tgt = rec["dma_idx"] // self.NPOOL
                        if wm.get(("d", eng_name, sl), 0) < tgt:
                            eng.wait_ge(dsems[eng_name][sl], 16 * tgt)
                            wm[("d", eng_name, sl)] = tgt
                    inst = rec["fn"](eng)
                    if rec["is_dma"]:
                        inst.then_inc(dsems[eng_name][rec["dma_idx"] % self.NPOOL], 16)
                    elif rec["marked"]:
                        inst.then_inc(sems[eng_name], 1)
                if eng_name == "gpsimd":
                    for q in QUEUES:
                        tot = sum(1 for r in self.ops[q] if r["is_dma"])
                        for sl in range(self.NPOOL):
                            nsl = (tot - sl + self.NPOOL - 1) // self.NPOOL
                            if nsl > 0 and wm.get(("d", q, sl), 0) < nsl:
                                eng.wait_ge(dsems[q][sl], 16 * nsl)
            return body

        block.tensor(make_body("tensor"))
        block.vector(make_body("vector"))
        block.scalar(make_body("scalar"))
        block.gpsimd(make_body("gpsimd"))
        block.sync(make_body("sync"))


class Ctx:
    pass



def ln_to(em, c, x_ap, x_buf, out_ap, out_buf, st, n, eps6=False, rms=False):
    sq, sqb = st["sq"]
    stat, stb = st["stat"]
    s1 = stat[:, 0:1]; s2 = stat[:, 1:2]; rstd = stat[:, 2:3]
    negmr = stat[:, 3:4]; tmp = stat[:, 4:5]
    t_sq = em.s(lambda e: e.activation(sq, x_ap, AF.Square, bias=0.0,
                                       scale=1.0, accum_out=s2),
                risky=True, reads=[x_buf], writes=[sqb, stb])
    if not rms:
        t_r = em.v(lambda e: e.reduce_sum(s1, x_ap, axis=AX.X),
                   deps=[t_sq], risky=True, reads=[x_buf], writes=[stb])
        t_a = em.v(lambda e: e.tensor_scalar(s1, s1, 1.0 / n, None, ALU.mult),
                   deps=[t_r], risky=True, writes=[stb])
        t_b = em.v(lambda e: e.tensor_scalar(s2, s2, 1.0 / n, None, ALU.mult),
                   deps=[t_sq, t_a], risky=True, writes=[stb])
        t_c = em.v(lambda e: e.tensor_tensor(tmp, s1, s1, ALU.mult),
                   deps=[t_b], risky=True, writes=[stb])
        last = em.v(lambda e: e.tensor_tensor(s2, s2, tmp, ALU.subtract),
                    deps=[t_c], risky=True, writes=[stb])
    else:
        last = em.v(lambda e: e.tensor_scalar(s2, s2, 1.0 / n, None, ALU.mult),
                    deps=[t_sq], risky=True, writes=[stb])
    eps_ap = c.eps6[:, 0:1] if eps6 else c.eps5[:, 0:1]
    t_ln = em.s(lambda e: e.activation(tmp, s2, AF.Ln, bias=eps_ap, scale=1.0),
                deps=[last], risky=True, reads=[c.constb], writes=[stb])
    t_ex = em.s(lambda e: e.activation(rstd, tmp, AF.Exp, bias=0.0, scale=-0.5),
                deps=[t_ln], risky=True, writes=[stb])
    if not rms:
        t_nm = em.v(lambda e: e.scalar_tensor_tensor(negmr, s1, -1.0, rstd,
                                                     ALU.mult, ALU.mult),
                    deps=[t_ex], risky=True, writes=[stb])
        t_out = em.s(lambda e: e.activation(out_ap, x_ap, AF.Identity,
                                            bias=negmr, scale=rstd),
                     deps=[t_nm, t_ex], reads=[x_buf], writes=[out_buf])
    else:
        t_out = em.s(lambda e: e.activation(out_ap, x_ap, AF.Identity,
                                            bias=0.0, scale=rstd),
                     deps=[t_ex], reads=[x_buf], writes=[out_buf])
    return t_out



def ln_compose(em, c, x_ap, x_buf, out_ap, out_buf, st, n, eps6_2=False):
    sq, sqb = st["sq"]
    stat, stb = st["stat"]
    s1 = stat[:, 0:1]; s2 = stat[:, 1:2]; rstd = stat[:, 2:3]
    negmr = stat[:, 3:4]; tmp = stat[:, 4:5]; v2 = stat[:, 5:6]
    r2 = stat[:, 6:7]; rr = stat[:, 7:8]
    t_sq = em.s(lambda e: e.activation(sq, x_ap, AF.Square, bias=0.0,
                                       scale=1.0, accum_out=s2),
                risky=True, reads=[x_buf], writes=[sqb, stb])
    t_r = em.v(lambda e: e.reduce_sum(s1, x_ap, axis=AX.X),
               deps=[t_sq], risky=True, reads=[x_buf], writes=[stb])
    t_a = em.v(lambda e: e.tensor_scalar(s1, s1, 1.0 / n, None, ALU.mult),
               deps=[t_r], risky=True, writes=[stb])
    t_b = em.v(lambda e: e.tensor_scalar(s2, s2, 1.0 / n, None, ALU.mult),
               deps=[t_sq, t_a], risky=True, writes=[stb])
    t_c = em.v(lambda e: e.tensor_tensor(tmp, s1, s1, ALU.mult),
               deps=[t_b], risky=True, writes=[stb])
    t_d = em.v(lambda e: e.tensor_tensor(s2, s2, tmp, ALU.subtract),
               deps=[t_c], risky=True, writes=[stb])
    t_ln = em.s(lambda e: e.activation(tmp, s2, AF.Ln, bias=c.eps5[:, 0:1],
                                       scale=1.0),
                deps=[t_d], risky=True, reads=[c.constb], writes=[stb])
    t_ex = em.s(lambda e: e.activation(rstd, tmp, AF.Exp, bias=0.0, scale=-0.5),
                deps=[t_ln], risky=True, writes=[stb])
    t_v1 = em.v(lambda e: e.tensor_tensor(v2, s2, rstd, ALU.mult),
                deps=[t_ex], risky=True, writes=[stb])
    t_v2 = em.v(lambda e: e.tensor_tensor(v2, v2, rstd, ALU.mult),
                deps=[t_v1], risky=True, writes=[stb])
    eps_ap = c.eps6[:, 0:1] if eps6_2 else c.eps5[:, 0:1]
    t_l2 = em.s(lambda e: e.activation(tmp, v2, AF.Ln, bias=eps_ap, scale=1.0),
                deps=[t_v2], risky=True, reads=[c.constb], writes=[stb])
    t_e2 = em.s(lambda e: e.activation(r2, tmp, AF.Exp, bias=0.0, scale=-0.5),
                deps=[t_l2], risky=True, writes=[stb])
    t_rr = em.v(lambda e: e.tensor_tensor(rr, rstd, r2, ALU.mult),
                deps=[t_e2], risky=True, writes=[stb])
    t_nm = em.v(lambda e: e.scalar_tensor_tensor(negmr, s1, -1.0, rr,
                                                 ALU.mult, ALU.mult),
                deps=[t_rr], risky=True, writes=[stb])
    t_out = em.s(lambda e: e.activation(out_ap, x_ap, AF.Identity,
                                        bias=negmr, scale=rr),
                 deps=[t_nm, t_rr], reads=[x_buf], writes=[out_buf])
    return t_out


def transpose_128(em, c, src_ap, src_bufs, dst_ap, dst_bufs, pt_sel, deps=(),
                  f32=False, split_dst=None, final=True):
    pt, ptb = (c.ptA, c.ptAb) if pt_sel % 2 == 0 else (c.ptB, c.ptBb)
    pt_ap = pt[:, 0:128]
    m = src_ap.shape[1]
    if f32:
        t1 = em.t(lambda e: e.transpose(pt_ap[0:m, :], src_ap, c.ident_f32[:]),
                  deps=deps, reads=list(src_bufs) + [c.constb], writes=[ptb])
    else:
        t1 = em.t(lambda e: e.matmul(pt_ap[0:m, :], src_ap, c.ident_bf[:],
                                     start=True, stop=True),
                  deps=deps, reads=list(src_bufs) + [c.constb], writes=[ptb])
    dst_dt = getattr(getattr(dst_ap if split_dst is None else split_dst[0][0],
                             "tensor", None), "dtype", None)
    if dst_dt == F32R:
        use_s = True
    else:
        use_s = pt_sel % 2 != 0
    ev = em.s if use_s else em.v
    if split_dst is not None:
        out_toks = []
        for u, (dtens, i, dbuf) in enumerate(split_dst):
            out_toks.append(ev(lambda e, u=u, dtens=dtens, i=i: e.copy(
                dtens[:, i * 128:(i + 1) * 128], pt_ap[u * 64:(u + 1) * 64, :])
                if use_s else e.tensor_copy(
                dtens[:, i * 128:(i + 1) * 128], pt_ap[u * 64:(u + 1) * 64, :]),
                deps=[t1], reads=[ptb], writes=[dbuf]))
        return out_toks[-1]
    t2 = ev(lambda e: e.copy(dst_ap, pt_ap[0:m, :]) if use_s
            else e.tensor_copy(dst_ap, pt_ap[0:m, :]),
            deps=[t1], reads=[ptb], writes_also=list(dst_bufs))
    return t2


def rope_all(em, c, ps_ap, ps_buf, nh, cs_ap, sc_ap, cs_buf, sc_buf,
             out_ap, out_buf, tmpA, tmpAb, tmpB, tmpBb, scale, extra_deps=()):
    w = nh * 64
    tA = em.v(lambda e: e.tensor_tensor(tmpA[:, 0:w], ps_ap, cs_ap, ALU.mult),
              deps=list(extra_deps), reads=[ps_buf, cs_buf], writes=[tmpAb])
    tB = em.v(lambda e: e.tensor_tensor(tmpB[:, 0:w], ps_ap, sc_ap, ALU.mult),
              deps=list(extra_deps), reads=[ps_buf, sc_buf], writes=[tmpBb])
    vA = tmpA[:, 0:w].rearrange("p (h x) -> p h x", h=nh)
    vB = tmpB[:, 0:w].rearrange("p (h x) -> p h x", h=nh)
    t1 = em.v(lambda e: e.tensor_tensor(vA[:, :, 0:32], vA[:, :, 0:32],
                                        vA[:, :, 32:64], ALU.subtract),
              deps=[tA], writes=[tmpAb])
    t2 = em.v(lambda e: e.tensor_tensor(vA[:, :, 32:64], vB[:, :, 0:32],
                                        vB[:, :, 32:64], ALU.add),
              deps=[tB, t1], reads=[tmpBb], writes=[tmpAb])
    if out_ap is None:
        return t2
    tc = em.s(lambda e: e.activation(out_ap, tmpA[:, 0:w], AF.Copy,
                                     bias=0.0, scale=scale),
              deps=[t1, t2], reads=[tmpAb], writes=[out_buf])
    return tc


def replicate_row(em, c, row_ap, row_buf, out_ap, out_buf, n, ps, psb,
                  deps=()):
    toks = []
    for o in range(0, n, 512):
        w = min(512, n - o)
        toks.append(em.t(lambda e, o=o, w=w: e.matmul(
            ps[:, o:o + w], c.ones_f32[0:1, :], row_ap[0:1, o:o + w],
            start=True, stop=True),
            deps=list(deps) + (psb.wr() if o == 0 else []),
            reads=[row_buf, c.constb]))
    psb.w = toks[-1]; psb.wl = []; psb.r = []
    t2 = em.s(lambda e: e.copy(out_ap, ps[:, 0:n]), deps=toks, reads=[psb],
              writes=[out_buf])
    return t2



def _colform(v, p=128):
    n = np.asarray(v).shape[0]
    return np.ascontiguousarray(np.asarray(v, np.float32).reshape(n // p, p).T)


def host_prep(inputs):
    d = {}
    g = lambda k: np.asarray(inputs[k], dtype=np.float32)

    def ef_perm(nheads, hd=64):
        p = []
        for h in range(nheads):
            p += [h * hd + i for i in range(0, hd, 2)]
            p += [h * hd + i for i in range(1, hd, 2)]
        return np.array(p)

    d["wqa"] = g("L0_attn_Wqa")
    d["wqb"] = np.ascontiguousarray(g("L0_attn_Wqb")[:, ef_perm(16)])
    d["wkva"] = g("L0_attn_Wkva")
    wkvb = g("L0_attn_Wkvb")
    cols = []
    for h in range(4):
        cols += [128 * h + 4 * i for i in range(32)]
        cols += [128 * h + 4 * i + 2 for i in range(32)]
    for h in range(4):
        cols += [128 * h + 2 * dd + 1 for dd in range(64)]
    d["wkvb"] = np.ascontiguousarray(wkvb[:, np.array(cols)])
    d["wout"] = g("L0_attn_Wout")
    d["bout_row"] = np.ascontiguousarray(g("L0_attn_bout").reshape(1, E))

    d["wqT0f"] = np.ascontiguousarray(g("L0_ffn_Wq").T)
    d["wdkvT"] = np.ascontiguousarray(g("L0_ffn_Wdkv").T)
    d["wukT"] = np.ascontiguousarray(g("L0_ffn_Wuk").T)
    d["wuvT"] = np.ascontiguousarray(g("L0_ffn_Wuv").T)
    d["woT0f"] = np.ascontiguousarray(g("L0_ffn_Wo").T)
    d["w1ncol"] = _colform(g("L0_ffn_norm1_w"))
    d["w2ncol"] = _colform(g("L0_ffn_norm2_w"))
    d["lngcol"] = _colform(g("L0_ffn_ln_g"))
    d["lnbcol"] = _colform(g("L0_ffn_ln_b"))
    d["gwT"] = np.ascontiguousarray(g("L0_ffn_gate_w").T)
    d["gbrow"] = np.ascontiguousarray(g("L0_ffn_gate_b").reshape(1, 4))
    d["w1T"] = np.ascontiguousarray(g("L0_ffn_W1").transpose(0, 2, 1))
    d["w2T"] = np.ascontiguousarray(g("L0_ffn_W2").transpose(0, 2, 1))
    d["b1col"] = np.stack([_colform(g("L0_ffn_b1")[e]) for e in range(4)])
    d["b2row"] = np.ascontiguousarray(g("L0_ffn_b2").reshape(4, 1, E))

    d["wq2"] = g("L1_attn_Wq"); d["wk2"] = g("L1_attn_Wk")
    d["wv2"] = g("L1_attn_Wv"); d["wr2"] = g("L1_attn_Wr")
    d["wo2"] = g("L1_attn_Wo")
    d["bq2col"] = _colform(g("L1_attn_bq")); d["bk2col"] = _colform(g("L1_attn_bk"))
    d["br2col"] = np.ascontiguousarray(g("L1_attn_br").reshape(8, 64).T)
    d["bv2row"] = np.ascontiguousarray(g("L1_attn_bv").reshape(1, E))
    d["bo2row"] = np.ascontiguousarray(g("L1_attn_bo").reshape(1, E))

    d["l1w1T"] = np.ascontiguousarray(g("L1_ffn_w1").T)
    d["l1w2T"] = np.ascontiguousarray(g("L1_ffn_w2").T)
    d["l1g1col"] = _colform(g("L1_ffn_ln1_g"))
    d["l1b1lncol"] = _colform(g("L1_ffn_ln1_b"))
    d["l1b1row"] = np.ascontiguousarray(g("L1_ffn_b1").reshape(1, 4096))
    d["l1g2col"] = _colform(g("L1_ffn_ln2_g"))
    d["l1b2lncol"] = _colform(g("L1_ffn_ln2_b"))
    d["l1b2row"] = np.ascontiguousarray(g("L1_ffn_b2").reshape(1, E))

    t = np.arange(S, dtype=np.float64)
    inv = 10000.0 ** (-np.arange(32, dtype=np.float64) / 32.0)
    ang = np.outer(t, inv)
    cos = np.cos(ang).astype(np.float32); sin = np.sin(ang).astype(np.float32)
    cs = np.concatenate([cos, sin], axis=1)
    sc = np.concatenate([sin, cos], axis=1)
    d["cs16"] = np.ascontiguousarray(np.tile(cs, (1, 16)))
    d["sc16"] = np.ascontiguousarray(np.tile(sc, (1, 16)))
    msk = np.zeros((4, 128, 512), np.float32)
    for a in range(4):
        kk = np.arange(128)[:, None] + 128 * a
        qq = np.arange(512)[None, :]
        msk[a] = np.where(kk > qq, NEG, 0.0)
    d["masks"] = np.ascontiguousarray(msk.reshape(4 * 128, 512))
    d["ident"] = np.eye(128, dtype=np.float32)
    return d


HOST_KEYS = ["wqa", "wqb", "wkva", "wkvb", "wout", "bout_row",
             "wqT0f", "wdkvT", "wukT", "wuvT", "woT0f", "w1ncol", "w2ncol",
             "lngcol", "lnbcol", "gwT", "gbrow", "w1T", "w2T", "b1col", "b2row",
             "wq2", "wk2", "wv2", "wr2", "wo2", "bq2col", "bk2col", "br2col",
             "bv2row", "bo2row",
             "l1w1T", "l1w2T", "l1g1col", "l1b1lncol", "l1b1row",
             "l1g2col", "l1b2lncol", "l1b2row",
             "cs16", "sc16", "masks", "ident"]



def build(phases=None):
    if phases is None:
        phases = PHASES
    nc = bass.Bass("TRN2", target_bir_lowering=False, debug=False)
    c = Ctx(); c.nc = nc
    em = Em(); c.em = em

    def din(name, shape, dt=F32):
        return nc.dram_tensor(name, list(shape), dt, kind="ExternalInput").ap()

    x_in = din("x", [S, E])
    W = {}
    shapes = dict(
        wqa=[E, 64], wqb=[64, E], wkva=[E, 32], wkvb=[32, 512], wout=[E, E],
        bout_row=[1, E],
        wqT0f=[E, E], wdkvT=[E, 512], wukT=[512, E], wuvT=[512, E],
        woT0f=[E, E], w1ncol=[128, 8], w2ncol=[128, 8], lngcol=[128, 4],
        lnbcol=[128, 4], gwT=[E, 4], gbrow=[1, 4], w1T=[4, E, 4096],
        w2T=[4, 4096, E], b1col=[4, 128, 32], b2row=[4, 1, E],
        wq2=[E, E], wk2=[E, E], wv2=[E, E], wr2=[E, 512], wo2=[E, E],
        bq2col=[128, 8], bk2col=[128, 8], br2col=[64, 8], bv2row=[1, E],
        bo2row=[1, E],
        l1w1T=[E, 4096], l1w2T=[4096, E], l1g1col=[128, 8], l1b1lncol=[128, 8],
        l1b1row=[1, 4096], l1g2col=[128, 32], l1b2lncol=[128, 32],
        l1b2row=[1, E],
        cs16=[S, E], sc16=[S, E], masks=[4 * 128, 512], ident=[128, 128],
    )
    for kk, sh in shapes.items():
        W[kk] = din(kk, sh)

    out = nc.dram_tensor("out", [S, E], F32, kind="ExternalOutput").ap()

    def dscratch(name, shape, dt):
        return nc.dram_tensor(name, list(shape), dt, kind="Internal").ap()

    X1 = dscratch("X1", [S, E], F32)
    X2 = dscratch("X2", [S, E], F32)
    X3 = dscratch("X3", [S, E], F32)
    QTD = dscratch("QTD", [E, S], PDT)
    KTD = dscratch("KTD", [E, S], PDT)
    OTD = dscratch("OTD", [E, S], PDT)
    U = dscratch("U", [S, E], F32)
    N2 = dscratch("N2", [E, S], BF16)
    XN = dscratch("XN", [S, E], BF16)
    YACC = dscratch("YACC", [S, E], F32)
    VTD = dscratch("VTD", [S, 16 * 65], PDT)
    OTD2 = dscratch("OTD2", [E, S], BF16)
    HTD = dscratch("HTD", [NT * 32 * 128, 128], BF16)

    with contextlib.ExitStack() as stack:
        ent = stack.enter_context
        c.ident_bf = ent(nc.sbuf_tensor("ident_bf", [128, 128], BF16))
        c.ident_f32 = ent(nc.sbuf_tensor("ident_f32", [128, 128], F32))
        c.ones_bf = ent(nc.sbuf_tensor("ones_bf", [128, 128], BF16))
        c.ones_f32 = ent(nc.sbuf_tensor("ones_f32", [128, 128], F32))
        c.ones_f32r = ent(nc.sbuf_tensor("ones_f32r", [1, 128],
                                         PDT if PDT != F32 else F32R))
        c.eps5 = ent(nc.sbuf_tensor("eps5", [128, 1], F32))
        c.eps6 = ent(nc.sbuf_tensor("eps6", [128, 1], F32))

        c.psA = ent(nc.psum_tensor("psA", [128, 1024], F32)); c.psAb = Buf()
        c.psB = ent(nc.psum_tensor("psB", [128, 1024], F32)); c.psBb = Buf()
        c.psC = ent(nc.psum_tensor("psC", [128, 512], F32)); c.psCb = Buf()
        c.psD = ent(nc.psum_tensor("psD", [128, 512], F32)); c.psDb = Buf()
        c.ptA = ent(nc.psum_tensor("ptA", [128, 512], F32)); c.ptAb = Buf()
        c.ptB = ent(nc.psum_tensor("ptB", [128, 512], F32)); c.ptBb = Buf()
        c.psDT = [c.ptAb, c.ptBb]
        c.psAh = [Buf(), Buf()]
        c.psBh = [Buf(), Buf()]

        cb = Buf()
        em.dma("sync", lambda e: e.dma_start(out=c.ident_f32[:], in_=W["ident"]),
               writes=[cb])
        em.v(lambda e: e.tensor_copy(c.ident_bf[:], c.ident_f32[:]), reads=[cb],
             writes_also=[cb])
        em.v(lambda e: e.memset(c.ones_bf[:], 1.0), writes_also=[cb])
        em.v(lambda e: e.memset(c.ones_f32[:], 1.0), writes_also=[cb])
        em.s(lambda e: e.copy(c.ones_f32r[:], c.ones_f32[0:1, :]), reads=[cb],
             writes_also=[cb])
        c.ones_pdt = c.ones_f32 if PDT == F32 else c.ones_f32r
        em.v(lambda e: e.memset(c.eps5[:], 1e-5), writes_also=[cb])
        em.v(lambda e: e.memset(c.eps6[:], 1e-6), writes_also=[cb])
        c.constb = cb

        c.sq = [ent(nc.sbuf_tensor(f"sq1k{k}", [128, 1024], F32))
                for k in range(2)]
        c.sqb = [Buf(), Buf()]
        c.stat = [ent(nc.sbuf_tensor(f"stat{k}", [128, 12], F32)) for k in range(2)]
        c.statb = [Buf(), Buf()]
        c.xt = [ent(nc.sbuf_tensor(f"xt{k}", [128, E], F32)) for k in range(2)]
        c.xtb = [Buf(), Buf()]
        c.tmpA = [ent(nc.sbuf_tensor(f"tmpA{k}", [128, E], F32)) for k in range(2)]
        c.tmpAb = [Buf(), Buf()]
        c.tmpB = [ent(nc.sbuf_tensor(f"tmpB{k}", [128, E], F32)) for k in range(2)]
        c.tmpBb = [Buf(), Buf()]
        c.W = W

        if 0 in phases:
            with contextlib.ExitStack() as pstack:
                phase0(c, pstack, W, x_in, X1, QTD, KTD, VTD, OTD,
                       out_final=(out if max(phases) == 0 else None))
            em.barrier()
        if DBG_X1:
            dbg = nc.dram_tensor("dbg", [S, E], F32, kind="ExternalOutput").ap()
            em.dma("gpsimd", lambda e: e.dma_start(out=dbg, in_=X1))
            em.barrier()
        if 1 in phases:
            xin1 = x_in if 0 not in phases else X1
            with contextlib.ExitStack() as pstack:
                phase1(c, pstack, W, xin1, X2, QTD, KTD, OTD, U, N2, YACC, VTD,
                       HTD, out_final=(out if max(phases) == 1 else None))
            em.barrier()
            if DBG_U:
                dbg = nc.dram_tensor("dbg", [S, E], F32, kind="ExternalOutput").ap()
                em.dma("gpsimd", lambda e: e.dma_start(out=dbg, in_=U))
                dbg2 = nc.dram_tensor("dbg2", [128, NT * 4], F32,
                                      kind="ExternalOutput").ap()
                em.dma("gpsimd", lambda e: e.dma_start(out=dbg2, in_=c.MSK[:]),
                       reads=[c.MSKb])
                em.barrier()
        if 2 in phases:
            xin2 = x_in if 1 not in phases else X2
            with contextlib.ExitStack() as pstack:
                phase2(c, pstack, W, xin2, X3, XN, OTD2,
                       out_final=(out if max(phases) == 2 else None))
            em.barrier()
        if 3 in phases:
            xin3 = x_in if 2 not in phases else X3
            with contextlib.ExitStack() as pstack:
                phase3(c, pstack, W, xin3, out)
            em.barrier()

        sems = {}; dsems = {}
        for e in ENGINES:
            sems[e] = ent(nc.semaphore("sem_" + e))
        for q in QUEUES:
            dsems[q] = [ent(nc.semaphore(f"dsem_{q}{j}")) for j in range(Em.NPOOL)]
        blk = ent(nc.Block())
        em.finalize(blk, sems, dsems)
    return nc



def vtm_war(b):
    return b.wr()


def stream_tables(em, c, i, k):
    cs_dst = c.cs_t[k][:]
    sc_dst = c.sc_t[k][:]
    cs_src = c.W["cs16"][i * 128:(i + 1) * 128, :]
    sc_src = c.W["sc16"][i * 128:(i + 1) * 128, :]
    em.dma("sync", lambda e, d=cs_dst, s=cs_src: e.dma_start(out=d, in_=s),
        writes=[c.cs_tb[k]])
    em.dma("sync", lambda e, d=sc_dst, s=sc_src: e.dma_start(out=d, in_=s),
        writes=[c.sc_tb[k]])


def qts_store(em, c, qts, qtsb, tgt, i, queue="scalar"):
    em.dma(queue, lambda e, i=i: e.dma_start(
        out=tgt.rearrange("(cc p) t -> p cc t", p=128)[:, :, i * 128:(i + 1) * 128],
        in_=qts[:].rearrange("p (cc t) -> p cc t", cc=NC)),
        reads=[qtsb])


def flash_head(c, em, h, qb, kq, KT_ap, KT_b, V_ap, V_b, QT_ap, QT_b,
               cfg, OTD, causal, Kd):
    Rc = lambda ap: ap
    P_sb = cfg["P_sb"]; P_sbb = cfg["P_sbb"]
    den = cfg["den"]; denb = cfg["denb"]
    rcrep = cfg["rcrep"]; rcrepb = cfg["rcrepb"]
    ot = cfg["ot"][kq]; otb = cfg["otb"][kq]
    masks_sb = cfg["masks_sb"]; masksb = cfg["masksb"]

    njt = 4 * (qb + 1) if causal else NT
    slots = [
        [(c.psA[:, 0:512], c.psAh[0]), (c.psA[:, 512:1024], c.psAh[1])],
        [(c.psB[:, 0:512], c.psBh[0]), (c.psB[:, 512:1024], c.psBh[1])],
        [(c.ptA[:, 0:512], c.ptAb), (c.ptB[:, 0:512], c.ptBb)],
    ]
    pairs = [[j for j in (jp, jp + 1) if j < njt] for jp in range(0, njt, 2)]
    tes = {}

    def emit_scores(n):
        sl = n % 3
        for u, j in enumerate(pairs[n]):
            ap_u, buf_u = slots[sl][u]
            pb = P_sbb[sl][u]
            mmu = em.t(lambda e, j=j, ap_u=ap_u: e.matmul(
                ap_u, Rc(KT_ap[0:Kd, j * 128:(j + 1) * 128]), Rc(QT_ap[0:Kd, :]),
                start=True, stop=True),
                reads=[KT_b, QT_b], deps=buf_u.wr())
            buf_u.w = mmu; buf_u.wl = []; buf_u.r = []
            toks = [mmu]
            if causal:
                a = j - 4 * qb
                if 0 <= a <= 3:
                    toks.append(em.v(lambda e, a=a, ap_u=ap_u: e.tensor_tensor(
                        ap_u, ap_u, masks_sb[:, a * 512:(a + 1) * 512], ALU.add),
                        deps=[mmu], reads=[masksb], writes_also=[buf_u]))
            te = em.s(lambda e, ap_u=ap_u, sl=sl, u=u: e.activation(
                P_sb[sl][:, u * 512:(u + 1) * 512], ap_u, AF.Exp, bias=0.0,
                scale=1.0),
                deps=toks + pb.wr(), reads=[buf_u])
            pb.w = te; pb.wl = []; pb.r = []
            tes[(n, u)] = te

    pv_toks = []
    DEPTH = 2
    for n in range(min(DEPTH, len(pairs))):
        emit_scores(n)
    if cfg.get("pending_tail"):
        cfg["pending_tail"]()
        cfg["pending_tail"] = None
    for n, pair in enumerate(pairs):
        sl = n % 3
        for u, j in enumerate(pair):
            pb = P_sbb[sl][u]
            pv_toks.append(em.t(lambda e, j=j, sl=sl, u=u: e.matmul(
                c.psC[0:65, 0:512], Rc(V_ap[:, j * 65:(j + 1) * 65]),
                Rc(P_sb[sl][:, u * 512:(u + 1) * 512]),
                start=(j == 0), stop=(j == njt - 1)),
                deps=[tes[(n, u)]] + (c.psCb.wr() if j == 0 else []),
                reads=[V_b, pb]))
        if n + DEPTH < len(pairs):
            emit_scores(n + DEPTH)
    c.psCb.w = pv_toks[-1]; c.psCb.wl = []; c.psCb.r = []
    t1 = em.s(lambda e: e.copy(den[:], c.psC[64:65, 0:512]),
              reads=[c.psCb], writes=[denb])
    t2 = em.v(lambda e: e.reciprocal(den[:], den[:]), deps=[t1], risky=True,
              writes=[denb])

    def tail():
        t3 = em.t(lambda e: e.matmul(c.psD[0:64, 0:512], c.ones_f32[0:1, 0:64],
                                     den[0:1, :], start=True, stop=True),
                  deps=[t2] + c.psDb.wr(), reads=[denb, c.constb])
        c.psDb.w = t3; c.psDb.wl = []; c.psDb.r = []
        t4 = em.s(lambda e: e.copy(rcrep[:], c.psD[0:64, 0:512]), deps=[t3],
                  reads=[c.psDb], writes=[rcrepb])
        if cfg.get("f32") and PDT != F32:
            t5a = em.v(lambda e: e.tensor_tensor(rcrep[:], c.psC[0:64, 0:512],
                                                 rcrep[:], ALU.mult),
                       deps=[t4], reads=[c.psCb], writes_also=[rcrepb])
            t5 = em.s(lambda e: e.copy(ot[:], rcrep[:]),
                      deps=[t5a], reads=[rcrepb], writes=[otb])
        else:
            t5 = em.v(lambda e: e.tensor_tensor(ot[:], c.psC[0:64, 0:512],
                                                rcrep[:], ALU.mult),
                      deps=[t4], reads=[c.psCb, rcrepb], writes=[otb])
        em.dma("scalar", lambda e, h=h, qb=qb: e.dma_start(
            out=OTD[h * 64:(h + 1) * 64, qb * 512:(qb + 1) * 512], in_=ot[:]),
            reads=[otb])

    cfg["pending_tail"] = tail


def flash_fork(c):
    for hb, cb in ((c.psAh, c.psAb), (c.psBh, c.psBb)):
        for b in hb:
            b.w = None; b.wl = list(cb.rd()); b.r = list(cb.r)


def flash_merge(c):
    for hb, cb in ((c.psAh, c.psAb), (c.psBh, c.psBb)):
        toks = []
        rs = []
        for b in hb:
            toks += b.rd()
            rs += b.r
            b.w = None; b.wl = []; b.r = []
        cb.w = None; cb.wl = toks; cb.r = rs


def proj_residual(c, em, pstack, OTD, x_src, x_dst, wsb, wsbb, bias_row,
                  bias_rowb, tag, f32=False):
    ent = pstack.enter_context
    nc = c.nc
    dt_ = PDT if f32 else BF16
    Rc = lambda ap: ap
    ones_row = c.ones_pdt if f32 else c.ones_bf
    otc = [ent(nc.sbuf_tensor(f"{tag}_otc{k}", [128, NC * 128], dt_))
           for k in range(2)]
    otcb = [Buf(), Buf()]
    ysb = [ent(nc.sbuf_tensor(f"{tag}_ysb{k}", [128, E], F32)) for k in range(2)]
    ysbb = [Buf(), Buf()]
    lnb = [ent(nc.sbuf_tensor(f"{tag}_lnb{k}", [128, E], F32)) for k in range(2)]
    lnbb = [Buf(), Buf()]
    for i in range(NT):
        k = i % 2
        st = dict(sq=(c.sq[k][:, 0:E], c.sqb[k]), stat=(c.stat[k][:], c.statb[k]))
        em.dma("sync", lambda e, i=i, k=k: e.dma_start(
            out=c.xt[k][:], in_=x_src[i * 128:(i + 1) * 128, :]),
            writes=[c.xtb[k]])
        em.dma("scalar", lambda e, i=i, k=k: e.dma_start(
            out=otc[k][:].rearrange("p (cc t) -> p cc t", cc=NC),
            in_=OTD.rearrange("(cc p) t -> p cc t", p=128)[:, :,
                i * 128:(i + 1) * 128]),
            writes=[otcb[k]])
        tm = []
        for cc in range(NC):
            for half in range(2):
                tm.append(em.t(lambda e, cc=cc, half=half, k=k: e.matmul(
                    c.psA[:, half * 512:(half + 1) * 512],
                    Rc(otc[k][:, cc * 128:(cc + 1) * 128]),
                    Rc(wsb[:, cc * E + half * 512:cc * E + (half + 1) * 512]),
                    start=(cc == 0), stop=False),
                    reads=[otcb[k], wsbb],
                    deps=(c.psAb.wr() if (cc == 0 and half == 0) else [])))
        for half in range(2):
            tm.append(em.t(lambda e, half=half: e.matmul(
                c.psA[:, half * 512:(half + 1) * 512],
                Rc(ones_row[0:1, :]),
                Rc(bias_row[0:1, half * 512:(half + 1) * 512]),
                start=False, stop=True),
                reads=[bias_rowb, c.constb]))
        c.psAb.w = tm[-1]; c.psAb.wl = []; c.psAb.r = []
        tev = em.s(lambda e, k=k: e.copy(ysb[k][:], c.psA[:]),
                   reads=[c.psAb], writes=[ysbb[k]])
        tln = ln_to(em, c, ysb[k][:], ysbb[k], lnb[k][:], lnbb[k], st, E)
        tadd = em.v(lambda e, k=k: e.tensor_tensor(ysb[k][:], lnb[k][:],
                                                   c.xt[k][:], ALU.add),
                    deps=[tln], reads=[lnbb[k], c.xtb[k]], writes=[ysbb[k]])
        em.dma("gpsimd", lambda e, i=i, k=k: e.dma_start(
            out=x_dst[i * 128:(i + 1) * 128, :], in_=ysb[k][:]),
            reads=[ysbb[k]])



def phase0(c, pstack, W, x_in, X1, QTD, KTD, VTD, OTD, out_final=None):
    nc = c.nc; em = c.em
    R = lambda ap: ap

    scope_ab = contextlib.ExitStack()
    ent = scope_ab.enter_context

    masks_sb = ent(nc.sbuf_tensor("p0_masks", [128, 4 * 512], F32)); masksb = Buf()
    em.dma("sync", lambda e: e.dma_start(
        out=masks_sb[:].rearrange("p (a q) -> p a q", a=4),
        in_=W["masks"].rearrange("(a p) q -> p a q", p=128)), writes=[masksb])

    c.cs_t = [ent(nc.sbuf_tensor(f"p0cs_t{k}", [128, E], F32)) for k in range(2)]
    c.cs_tb = [Buf(), Buf()]
    c.sc_t = [ent(nc.sbuf_tensor(f"p0sc_t{k}", [128, E], F32)) for k in range(2)]
    c.sc_tb = [Buf(), Buf()]

    wqa_sb = ent(nc.sbuf_tensor("p0_wqa", [128, NC * 64], PDT)); wqab = Buf()
    wkva_sb = ent(nc.sbuf_tensor("p0_wkva", [128, NC * 32], PDT)); wkvab = Buf()
    wqb_sb = ent(nc.sbuf_tensor("p0_wqb", [64, E], PDT)); wqbb = Buf()
    wkvb_sb = ent(nc.sbuf_tensor("p0_wkvb", [32, 512], PDT)); wkvbb = Buf()

    em.dma("gpsimd", lambda e: e.dma_start(
        out=wqa_sb[:].rearrange("p (cc m) -> p cc m", cc=NC),
        in_=W["wqa"].rearrange("(cc p) m -> p cc m", p=128)), writes=[wqab])
    em.dma("gpsimd", lambda e: e.dma_start(
        out=wkva_sb[:].rearrange("p (cc m) -> p cc m", cc=NC),
        in_=W["wkva"].rearrange("(cc p) m -> p cc m", p=128)), writes=[wkvab])
    em.dma("gpsimd", lambda e: e.dma_start(out=wqb_sb[:], in_=W["wqb"]),
           writes=[wqbb])
    em.dma("gpsimd", lambda e: e.dma_start(out=wkvb_sb[:], in_=W["wkvb"]),
           writes=[wkvbb])
    em.v(lambda e: e.tensor_scalar_mul(wqb_sb[:], wqb_sb[:], SCALE8),
         reads=[wqbb], writes_also=[wqbb])

    kts0 = [ent(nc.sbuf_tensor(f"p0_kts0{k}", [128, 256], PDT)) for k in range(2)]
    kts0b = [Buf(), Buf()]
    vtm0 = [ent(nc.sbuf_tensor(f"p0_vtm0{k}", [128, 4 * 65], PDT)) for k in range(2)]
    vtm0b = [Buf(), Buf()]
    for k in range(2):
        em.s(lambda e, k=k: e.copy(
            vtm0[k][:].rearrange("p (h m) -> p h m", h=4)[:, :, 64:65],
            c.ones_f32[:, 0:4].rearrange("p (h m) -> p h m", m=1)),
            reads=[c.constb], writes=[vtm0b[k]])

    xn = [ent(nc.sbuf_tensor(f"p0_xn{k}", [128, E], F32)) for k in range(2)]
    xnb = [Buf(), Buf()]
    xnT = [ent(nc.sbuf_tensor(f"p0_xnT{k}", [128, NC * 128], PDT))
           for k in range(2)]
    xnTb = [Buf(), Buf()]
    kvl = [ent(nc.sbuf_tensor(f"p0_kvl{k}", [32, 128], PDT)) for k in range(2)]
    kvlb = [Buf(), Buf()]
    ql_t = [ent(nc.sbuf_tensor(f"p0_ql{k}", [64, 128], PDT)) for k in range(2)]
    ql_tb = [Buf(), Buf()]
    kr = [ent(nc.sbuf_tensor(f"p0_kr{k}", [128, 256], F32)) for k in range(2)]
    krb = [Buf(), Buf()]
    vtmp = [ent(nc.sbuf_tensor(f"p0_vtmp{k}", [128, 256], F32)) for k in range(2)]
    vtmpb = [Buf(), Buf()]
    qts = [ent(nc.sbuf_tensor(f"p0_qts{k}", [128, NC * 128], PDT)) for k in range(2)]
    qtsb = [Buf(), Buf()]

    for i in (range(NT) if "a" in P0_PARTS else []):
        k = i % 2
        st = dict(sq=(c.sq[k][:, 0:E], c.sqb[k]), stat=(c.stat[k][:], c.statb[k]))
        em.dma("sync", lambda e, i=i, k=k: e.dma_start(
            out=c.xt[k][:], in_=x_in[i * 128:(i + 1) * 128, :]), writes=[c.xtb[k]])
        stream_tables(em, c, i, k)
        ln_to(em, c, c.xt[k][:], c.xtb[k], xn[k][:], xnb[k], st, E)
        if P0A_STEP < 2:
            continue
        for cc in range(NC):
            transpose_128(em, c, xn[k][:, cc * 128:(cc + 1) * 128], [xnb[k]],
                          xnT[k][:, cc * 128:(cc + 1) * 128],
                          [xnTb[k]], cc % 2, f32=True,
                          deps=(xnTb[k].wr() if cc == 0 else []), final=(cc == NC - 1))
        if P0A_STEP < 21:
            continue
        tq = []
        for cc in range(NC):
            tq.append(em.t(lambda e, cc=cc, k=k: e.matmul(
                c.psC[0:64, 0:128], R(wqa_sb[:, cc * 64:(cc + 1) * 64]),
                R(xnT[k][:, cc * 128:(cc + 1) * 128]),
                start=(cc == 0), stop=(cc == NC - 1)),
                reads=[wqab, xnTb[k]],
                deps=(c.psCb.wr() if cc == 0 else [])))
        c.psCb.w = tq[-1]; c.psCb.wl = []; c.psCb.r = []
        em.s(lambda e, k=k: e.copy(ql_t[k][:], c.psC[0:64, 0:128]),
             reads=[c.psCb], writes=[ql_tb[k]])
        if P0A_STEP < 22:
            continue
        tkv = []
        for cc in range(NC):
            tkv.append(em.t(lambda e, cc=cc, k=k: e.matmul(
                c.psD[0:32, 0:128], R(wkva_sb[:, cc * 32:(cc + 1) * 32]),
                R(xnT[k][:, cc * 128:(cc + 1) * 128]),
                start=(cc == 0), stop=(cc == NC - 1)),
                reads=[wkvab, xnTb[k]],
                deps=(c.psDb.wr() if cc == 0 else [])))
        c.psDb.w = tkv[-1]; c.psDb.wl = []; c.psDb.r = []
        em.s(lambda e, k=k: e.copy(kvl[k][:], c.psD[0:32, 0:128]),
             reads=[c.psDb], writes=[kvlb[k]])
        if P0A_STEP < 3:
            continue
        tkvmm = em.t(lambda e, k=k: e.matmul(c.psB[:, 0:512], R(kvl[k][:]),
                                             R(wkvb_sb[:]), start=True, stop=True),
                     reads=[kvlb[k], wkvbb], deps=c.psBb.wr())
        c.psBb.w = tkvmm; c.psBb.wl = []; c.psBb.r = []
        tA = em.v(lambda e, k=k, _cs=c.cs_t[k][:, 0:256]: e.tensor_tensor(
            c.tmpA[k][:, 0:256], c.psB[:, 0:256], _cs, ALU.mult),
            reads=[c.psBb, c.cs_tb[k]], writes=[c.tmpAb[k]])
        tB = em.v(lambda e, k=k, _sc=c.sc_t[k][:, 0:256]: e.tensor_tensor(
            c.tmpB[k][:, 0:256], c.psB[:, 0:256], _sc, ALU.mult),
            reads=[c.psBb, c.sc_tb[k]], writes=[c.tmpBb[k]])
        vA = lambda k: c.tmpA[k][:, 0:256].rearrange("p (h x) -> p h x", h=4)
        vB = lambda k: c.tmpB[k][:, 0:256].rearrange("p (h x) -> p h x", h=4)
        vK = lambda k: kr[k][:].rearrange("p (h x) -> p h x", h=4)
        t1 = em.v(lambda e, k=k: e.tensor_tensor(
            vK(k)[:, :, 0:32], vA(k)[:, :, 0:32], vA(k)[:, :, 32:64],
            ALU.subtract), deps=[tA], reads=[c.tmpAb[k]], writes=[krb[k]])
        t2 = em.v(lambda e, k=k: e.tensor_tensor(
            vK(k)[:, :, 32:64], vB(k)[:, :, 0:32], vB(k)[:, :, 32:64],
            ALU.add), deps=[tB], reads=[c.tmpBb[k]], writes_also=[krb[k]])
        if P0A_STEP < 4:
            continue
        em.v(lambda e, k=k: e.tensor_copy(vtmp[k][:], c.psB[:, 256:512]),
             reads=[c.psBb], writes=[vtmpb[k]])
        em.s(lambda e, k=k: e.copy(
            vtm0[k][:].rearrange("p (h m) -> p h m", h=4)[:, :, 0:64],
            vtmp[k][:].rearrange("p (h m) -> p h m", h=4)),
            deps=vtm0b[k].wr(), reads=[vtmpb[k]], writes_also=[vtm0b[k]])
        em.dma("gpsimd", lambda e, i=i, k=k: e.dma_start(
            out=VTD[i * 128:(i + 1) * 128, 0:260], in_=vtm0[k][:]),
            reads=[vtm0b[k]])
        if P0A_STEP < 5:
            continue
        for blkk in range(2):
            transpose_128(em, c, kr[k][:, blkk * 128:(blkk + 1) * 128],
                          [krb[k]],
                          kts0[k][:, blkk * 128:(blkk + 1) * 128],
                          [kts0b[k]], blkk, f32=True,
                          deps=(kts0b[k].wr() if blkk == 0 else []), final=(blkk == 1))
        em.dma("scalar", lambda e, i=i, k=k: e.dma_start(
            out=KTD.rearrange("(cc p) t -> p cc t", p=128)[
                :, 0:2, i * 128:(i + 1) * 128],
            in_=kts0[k][:].rearrange("p (cc t) -> p cc t", cc=2)),
            reads=[kts0b[k]])
        if P0A_STEP < 6:
            continue
        tqa = []
        for half in range(2):
            tqa.append(em.t(lambda e, half=half, k=k: e.matmul(
                c.psA[:, half * 512:(half + 1) * 512], R(ql_t[k][:]),
                R(wqb_sb[:, half * 512:(half + 1) * 512]), start=True, stop=True),
                reads=[ql_tb[k], wqbb],
                deps=(c.psAb.wr() if half == 0 else [])))
        c.psAb.w = tqa[-1]; c.psAb.wl = []; c.psAb.r = []
        rope_all(em, c, c.psA[:, 0:1024], c.psAb, 16,
                 c.cs_t[k][:], c.sc_t[k][:], c.cs_tb[k], c.sc_tb[k],
                 None, None, c.tmpA[k], c.tmpAb[k], c.tmpB[k],
                 c.tmpBb[k], None)
        for cc in range(NC):
            transpose_128(em, c, c.tmpA[k][:, cc * 128:(cc + 1) * 128],
                          [c.tmpAb[k]],
                          qts[k][:, cc * 128:(cc + 1) * 128],
                          [qtsb[k]], cc % 2, f32=True,
                          deps=(qtsb[k].wr() if cc == 0 else []), final=(cc == NC - 1))
        qts_store(em, c, qts[k], qtsb[k], QTD, i)

    em.barrier()

    QT = [ent(nc.sbuf_tensor(f"p0_QT{k}", [64, 512], PDT)) for k in range(2)]
    QTb = [Buf(), Buf()]
    KT = [ent(nc.sbuf_tensor(f"p0_KTh{k}", [64, S], PDT)) for k in range(2)]
    KTb = [Buf(), Buf()]
    Vh = [ent(nc.sbuf_tensor(f"p0_Vh{k}", [128, NT * 65], PDT)) for k in range(2)]
    Vhb = [Buf(), Buf()]
    P_sb = [ent(nc.sbuf_tensor(f"p0_P{k}", [128, 1024], PDT)) for k in range(3)]
    P_sbb = [[Buf(), Buf()] for _ in range(3)]
    den = ent(nc.sbuf_tensor("p0_den", [1, 512], F32)); denb = Buf()
    rcrep = ent(nc.sbuf_tensor("p0_rcrep", [64, 512], F32)); rcrepb = Buf()
    ot = [ent(nc.sbuf_tensor(f"p0_ot{k}", [64, 512], PDT)) for k in range(2)]
    otb = [Buf(), Buf()]
    cfg = dict(P_sb=P_sb, P_sbb=P_sbb, den=den, denb=denb, rcrep=rcrep,
               rcrepb=rcrepb, ot=ot, otb=otb, masks_sb=masks_sb, masksb=masksb,
               f32=True)

    flash_fork(c)
    for h in (range(16) if "b" in P0_PARTS else []):
        kv = h // 4
        kk = kv % 2
        if h % 4 == 0:
            em.dma("sync", lambda e, kv=kv, kk=kk: e.dma_start(
                out=KT[kk][:], in_=KTD[kv * 64:(kv + 1) * 64, :]),
                writes=[KTb[kk]])
            em.dma("gpsimd", lambda e, kv=kv, kk=kk: e.dma_start(
                out=Vh[kk][:].rearrange("p (i m) -> p i m", i=NT),
                in_=VTD.rearrange("(i p) m -> p i m", p=128)[
                    :, :, kv * 65:(kv + 1) * 65]), writes=[Vhb[kk]])
        for qb in range(8):
            kq = (h * 8 + qb) % 2
            em.dma("sync", lambda e, h=h, qb=qb, kq=kq: e.dma_start(
                out=QT[kq][:], in_=QTD[h * 64:(h + 1) * 64,
                                       qb * 512:(qb + 1) * 512]),
                writes=[QTb[kq]])
            flash_head(c, em, h, qb, kq, KT[kk][:], KTb[kk], Vh[kk][:], Vhb[kk],
                       QT[kq][:], QTb[kq], cfg, OTD, causal=True, Kd=64)
    if cfg.get("pending_tail"):
        cfg["pending_tail"]()
        cfg["pending_tail"] = None
    flash_merge(c)

    em.barrier()
    scope_ab.close()

    with contextlib.ExitStack() as ps2:
        ent2 = ps2.enter_context
        wout_sb = ent2(nc.sbuf_tensor("p0_wout", [128, NC * E], PDT))
        woutb = Buf()
        boutr_sb = ent2(nc.sbuf_tensor("p0_boutr", [1, E], PDT)); boutb = Buf()
        em.dma("gpsimd", lambda e: e.dma_start(
            out=wout_sb[:].rearrange("p (cc m) -> p cc m", cc=NC),
            in_=W["wout"].rearrange("(cc p) m -> p cc m", p=128)), writes=[woutb])
        em.dma("gpsimd", lambda e: e.dma_start(out=boutr_sb[:], in_=W["bout_row"]),
               writes=[boutb])
        tgt = out_final if out_final is not None else X1
        if "c" in P0_PARTS:
            proj_residual(c, em, ps2, OTD, x_in, tgt, wout_sb[:], woutb,
                          boutr_sb, boutb, "p0c", f32=True)


def phase1(c, pstack, W, Xin, X2, QTD, KTD, OTD, U, N2, YACC, VTD, HTD,
           out_final=None):
    nc = c.nc; em = c.em
    R = lambda ap: ap
    ent_outer = pstack.enter_context
    w2ncol = ent_outer(nc.sbuf_tensor("p1_w2ncol", [128, 8], F32))
    c.w2ncol = w2ncol
    w2ncolb = Buf(); c.w2ncolb = w2ncolb
    c.MSK = ent_outer(nc.sbuf_tensor("p1_MSK", [128, NT * 4], F32))
    c.MSKb = Buf()
    em.dma("sync", lambda e: e.dma_start(out=w2ncol[:], in_=W["w2ncol"]),
           writes=[w2ncolb])

    scope_a = contextlib.ExitStack()
    ent = scope_a.enter_context
    c.cs_t = [ent(nc.sbuf_tensor(f"p1cs_t{k}", [128, E], F32)) for k in range(2)]
    c.cs_tb = [Buf(), Buf()]
    c.sc_t = [ent(nc.sbuf_tensor(f"p1sc_t{k}", [128, E], F32)) for k in range(2)]
    c.sc_tb = [Buf(), Buf()]

    wq_sb = ent(nc.sbuf_tensor("p1_wq", [128, NC * E], PDT)); wqb_ = Buf()
    wdkv_sb = ent(nc.sbuf_tensor("p1_wdkv", [128, NC * 512], PDT)); wdkvb = Buf()
    wuk_sb = ent(nc.sbuf_tensor("p1_wuk", [128, 4 * E], PDT)); wukb = Buf()
    wuv_sb = ent(nc.sbuf_tensor("p1_wuv", [128, 4 * E], PDT)); wuvb = Buf()
    w1ncol = ent(nc.sbuf_tensor("p1_w1ncol", [128, 8], F32)); w1ncolb = Buf()
    lngcol = ent(nc.sbuf_tensor("p1_lngcol", [128, 4], F32)); lngcolb = Buf()
    lnbcol = ent(nc.sbuf_tensor("p1_lnbcol", [128, 4], PDT)); lnbcolb = Buf()

    em.dma("gpsimd", lambda e: e.dma_start(
        out=wq_sb[:].rearrange("p (cc m) -> p cc m", cc=NC),
        in_=W["wqT0f"].rearrange("(cc p) m -> p cc m", p=128)), writes=[wqb_])
    em.dma("gpsimd", lambda e: e.dma_start(
        out=wdkv_sb[:].rearrange("p (cc m) -> p cc m", cc=NC),
        in_=W["wdkvT"].rearrange("(cc p) m -> p cc m", p=128)), writes=[wdkvb])
    em.dma("gpsimd", lambda e: e.dma_start(
        out=wuk_sb[:].rearrange("p (cc m) -> p cc m", cc=4),
        in_=W["wukT"].rearrange("(cc p) m -> p cc m", p=128)), writes=[wukb])
    em.dma("gpsimd", lambda e: e.dma_start(
        out=wuv_sb[:].rearrange("p (cc m) -> p cc m", cc=4),
        in_=W["wuvT"].rearrange("(cc p) m -> p cc m", p=128)), writes=[wuvb])
    em.dma("sync", lambda e: e.dma_start(out=w1ncol[:], in_=W["w1ncol"]),
           writes=[w1ncolb])
    em.dma("sync", lambda e: e.dma_start(out=lngcol[:], in_=W["lngcol"]),
           writes=[lngcolb])
    em.dma("gpsimd", lambda e: e.dma_start(out=lnbcol[:], in_=W["lnbcol"]),
           writes=[lnbcolb])

    bvk_row = ent(nc.sbuf_tensor("p1_bvkrow", [1, E], F32)); bvkrb = Buf()
    bvv_row = ent(nc.sbuf_tensor("p1_bvvrow", [1, E], F32)); bvvrb = Buf()
    bvk = ent(nc.sbuf_tensor("p1_bvk", [128, E], F32)); bvkb = Buf()
    bvv = ent(nc.sbuf_tensor("p1_bvv", [128, E], F32)); bvvb = Buf()
    for (wsb, wsbb, row, rowb, rep, repb) in (
            (wuk_sb, wukb, bvk_row, bvkrb, bvk, bvkb),
            (wuv_sb, wuvb, bvv_row, bvvrb, bvv, bvvb)):
        for half in range(2):
            tmm = []
            for lc in range(4):
                tmm.append(em.t(lambda e, lc=lc, half=half, wsb=wsb: e.matmul(
                    c.psC[0:1, 0:512], R(lnbcol[:, lc:lc + 1]),
                    R(wsb[:, lc * E + half * 512:lc * E + (half + 1) * 512]),
                    start=(lc == 0), stop=(lc == 3)),
                    reads=[lnbcolb, wsbb],
                    deps=(c.psCb.wr() if lc == 0 else [])))
            c.psCb.w = tmm[-1]; c.psCb.wl = []; c.psCb.r = []
            em.s(lambda e, half=half, row=row: e.copy(
                row[0:1, half * 512:(half + 1) * 512], c.psC[0:1, 0:512]),
                reads=[c.psCb], writes=([rowb]),
                writes_also=([rowb] if half == 0 else []))
        replicate_row(em, c, row[:], rowb, rep[:], repb, E, c.psB, c.psBb)
    for cc in range(NC):
        em.v(lambda e, cc=cc: e.tensor_scalar(
            wq_sb[:, cc * E:(cc + 1) * E], wq_sb[:, cc * E:(cc + 1) * E],
            w1ncol[:, cc:cc + 1], SCALE8, ALU.mult, ALU.mult),
            deps=wqb_.wr(), reads=[w1ncolb], writes_also=[wqb_])
        em.v(lambda e, cc=cc: e.tensor_scalar_mul(
            wdkv_sb[:, cc * 512:(cc + 1) * 512], wdkv_sb[:, cc * 512:(cc + 1) * 512],
            w1ncol[:, cc:cc + 1]), deps=wdkvb.wr(), reads=[w1ncolb],
            writes_also=[wdkvb])
    for lc in range(4):
        em.v(lambda e, lc=lc: e.tensor_scalar_mul(
            wuk_sb[:, lc * E:(lc + 1) * E], wuk_sb[:, lc * E:(lc + 1) * E],
            lngcol[:, lc:lc + 1]), deps=wukb.wr(), reads=[lngcolb],
            writes_also=[wukb])
        em.v(lambda e, lc=lc: e.tensor_scalar_mul(
            wuv_sb[:, lc * E:(lc + 1) * E], wuv_sb[:, lc * E:(lc + 1) * E],
            lngcol[:, lc:lc + 1]), deps=wuvb.wr(), reads=[lngcolb],
            writes_also=[wuvb])

    xr1 = [ent(nc.sbuf_tensor(f"p1_xr1{k}", [128, E], F32)) for k in range(2)]
    xr1b = [Buf(), Buf()]
    xnT = [ent(nc.sbuf_tensor(f"p1_xnT{k}", [128, NC * 128], PDT)) for k in range(2)]
    xnTb = [Buf(), Buf()]
    latn = [ent(nc.sbuf_tensor(f"p1_latn{k}", [128, 512], F32)) for k in range(2)]
    latnb = [Buf(), Buf()]
    latT = [ent(nc.sbuf_tensor(f"p1_latT{k}", [128, 4 * 128], PDT)) for k in range(2)]
    latTb = [Buf(), Buf()]
    qts = [ent(nc.sbuf_tensor(f"p1_qts{k}", [128, NC * 128], PDT)) for k in range(2)]
    qtsb = [Buf(), Buf()]
    kts = [ent(nc.sbuf_tensor(f"p1_kts{k}", [128, NC * 128], PDT)) for k in range(2)]
    ktsb = [Buf(), Buf()]
    vtm = [ent(nc.sbuf_tensor(f"p1_vtm{k}", [128, 16 * 65], PDT)) for k in range(2)]
    vtmb = [Buf(), Buf()]
    for k in range(2):
        em.s(lambda e, k=k: e.copy(
            vtm[k][:].rearrange("p (h m) -> p h m", h=16)[:, :, 64:65],
            c.ones_f32[:, 0:16].rearrange("p (h m) -> p h m", m=1)),
            reads=[c.constb], writes=[vtmb[k]])

    for i in range(NT):
        k = i % 2
        st = dict(sq=(c.sq[k][:, 0:E], c.sqb[k]), stat=(c.stat[k][:], c.statb[k]))
        em.dma("sync", lambda e, i=i, k=k: e.dma_start(
            out=c.xt[k][:], in_=Xin[i * 128:(i + 1) * 128, :]), writes=[c.xtb[k]])
        stream_tables(em, c, i, k)
        ln_compose(em, c, c.xt[k][:], c.xtb[k], xr1[k][:], xr1b[k], st, E,
                   eps6_2=True)
        for cc in range(NC):
            transpose_128(em, c, xr1[k][:, cc * 128:(cc + 1) * 128], [xr1b[k]],
                          xnT[k][:, cc * 128:(cc + 1) * 128],
                          [xnTb[k]], cc % 2, f32=True,
                          deps=(xnTb[k].wr() if cc == 0 else []), final=(cc == NC - 1))
        tqa = []
        for half in range(2):
            for cc in range(NC):
                tqa.append(em.t(lambda e, cc=cc, half=half, k=k: e.matmul(
                    c.psA[:, half * 512:(half + 1) * 512],
                    R(xnT[k][:, cc * 128:(cc + 1) * 128]),
                    R(wq_sb[:, cc * E + half * 512:cc * E + (half + 1) * 512]),
                    start=(cc == 0), stop=(cc == NC - 1)),
                    reads=[xnTb[k], wqb_],
                    deps=(c.psAb.wr() if cc == 0 and half == 0 else [])))
        c.psAb.w = tqa[-1]; c.psAb.wl = []; c.psAb.r = []
        rope_all(em, c, c.psA[:, 0:1024], c.psAb, 16,
                 c.cs_t[k][:], c.sc_t[k][:], c.cs_tb[k], c.sc_tb[k],
                 None, None, c.tmpA[k], c.tmpAb[k], c.tmpB[k], c.tmpBb[k], None)
        for cc in range(NC):
            transpose_128(em, c, c.tmpA[k][:, cc * 128:(cc + 1) * 128],
                          [c.tmpAb[k]],
                          qts[k][:, cc * 128:(cc + 1) * 128],
                          [qtsb[k]], cc % 2, f32=True,
                          deps=(qtsb[k].wr() if cc == 0 else []), final=(cc == NC - 1))
        qts_store(em, c, qts[k], qtsb[k], QTD, i)
        tl = []
        for cc in range(NC):
            tl.append(em.t(lambda e, cc=cc, k=k: e.matmul(
                c.psC[:, 0:512], R(xnT[k][:, cc * 128:(cc + 1) * 128]),
                R(wdkv_sb[:, cc * 512:(cc + 1) * 512]),
                start=(cc == 0), stop=(cc == NC - 1)),
                reads=[xnTb[k], wdkvb],
                deps=(c.psCb.wr() if cc == 0 else [])))
        c.psCb.w = tl[-1]; c.psCb.wl = []; c.psCb.r = []
        st2 = dict(sq=(c.sq[k][:, 0:512], c.sqb[k]), stat=(c.stat[k][:], c.statb[k]))
        ln_to(em, c, c.psC[:, 0:512], c.psCb, latn[k][:], latnb[k], st2, 512)
        for lc in range(4):
            transpose_128(em, c, latn[k][:, lc * 128:(lc + 1) * 128], [latnb[k]],
                          latT[k][:, lc * 128:(lc + 1) * 128],
                          [latTb[k]], lc % 2, f32=True,
                          deps=(latTb[k].wr() if lc == 0 else []), final=(lc == 3))
        tka = []
        for half in range(2):
            for lc in range(4):
                tka.append(em.t(lambda e, lc=lc, half=half, k=k: e.matmul(
                    c.psA[:, half * 512:(half + 1) * 512],
                    R(latT[k][:, lc * 128:(lc + 1) * 128]),
                    R(wuk_sb[:, lc * E + half * 512:lc * E + (half + 1) * 512]),
                    start=(lc == 0), stop=(lc == 3)),
                    reads=[latTb[k], wukb],
                    deps=(c.psAb.wr() if lc == 0 and half == 0 else [])))
        c.psAb.w = tka[-1]; c.psAb.wl = []; c.psAb.r = []
        tkb1 = em.v(lambda e, k=k: e.tensor_tensor(c.psA[:, 0:512],
                                                   c.psA[:, 0:512], bvk[:, 0:512],
                                                   ALU.add),
                    reads=[bvkb, c.psAb], writes_also=[c.psAb])
        tkb = em.v(lambda e, k=k: e.tensor_tensor(c.psA[:, 512:1024],
                                                  c.psA[:, 512:1024],
                                                  bvk[:, 512:1024], ALU.add),
                   deps=[tkb1], reads=[bvkb], writes_also=[c.psAb])
        rope_all(em, c, c.psA[:, 0:1024], c.psAb, 16,
                 c.cs_t[k][:], c.sc_t[k][:], c.cs_tb[k], c.sc_tb[k],
                 None, None, c.tmpA[k], c.tmpAb[k], c.tmpB[k], c.tmpBb[k], None,
                 extra_deps=[tkb])
        for cc in range(NC):
            transpose_128(em, c, c.tmpA[k][:, cc * 128:(cc + 1) * 128],
                          [c.tmpAb[k]],
                          kts[k][:, cc * 128:(cc + 1) * 128],
                          [ktsb[k]], cc % 2, f32=True,
                          deps=(ktsb[k].wr() if cc == 0 else []), final=(cc == NC - 1))
        qts_store(em, c, kts[k], ktsb[k], KTD, i, queue="scalar")
        tva = []
        for half in range(2):
            for lc in range(4):
                tva.append(em.t(lambda e, lc=lc, half=half, k=k: e.matmul(
                    c.psB[:, half * 512:(half + 1) * 512],
                    R(latT[k][:, lc * 128:(lc + 1) * 128]),
                    R(wuv_sb[:, lc * E + half * 512:lc * E + (half + 1) * 512]),
                    start=(lc == 0), stop=(lc == 3)),
                    reads=[latTb[k], wuvb],
                    deps=(c.psBb.wr() if lc == 0 and half == 0 else [])))
        c.psBb.w = tva[-1]; c.psBb.wl = []; c.psBb.r = []
        tvb = em.v(lambda e, k=k: e.tensor_tensor(c.tmpA[k][:], c.psB[:, 0:1024],
                                                  bvv[:], ALU.add),
                   reads=[c.psBb, bvvb], writes=[c.tmpAb[k]])
        tvc = em.s(lambda e, k=k: e.copy(
            vtm[k][:].rearrange("p (h m) -> p h m", h=16)[:, :, 0:64],
            c.tmpA[k][:].rearrange("p (h m) -> p h m", h=16)),
            deps=[tvb] + vtm_war(vtmb[k]), reads=[c.tmpAb[k]],
            writes_also=[vtmb[k]])
        em.dma("gpsimd", lambda e, i=i, k=k: e.dma_start(
            out=VTD[i * 128:(i + 1) * 128, :], in_=vtm[k][:]), reads=[vtmb[k]])

    em.barrier()
    scope_a.close()

    scope_b = contextlib.ExitStack()
    entb = scope_b.enter_context
    masks_sb = entb(nc.sbuf_tensor("p1_masks", [128, 4 * 512], F32)); masksb = Buf()
    em.dma("sync", lambda e: e.dma_start(
        out=masks_sb[:].rearrange("p (a q) -> p a q", a=4),
        in_=W["masks"].rearrange("(a p) q -> p a q", p=128)), writes=[masksb])
    QT = [entb(nc.sbuf_tensor(f"p1_QT{k}", [64, 512], PDT)) for k in range(2)]
    QTb = [Buf(), Buf()]
    KT = [entb(nc.sbuf_tensor(f"p1_KT{k}", [64, S], PDT)) for k in range(2)]
    KTb = [Buf(), Buf()]
    Vh = [entb(nc.sbuf_tensor(f"p1_Vh{k}", [128, NT * 65], PDT)) for k in range(2)]
    Vhb = [Buf(), Buf()]
    P_sb = [entb(nc.sbuf_tensor(f"p1_P{k}", [128, 1024], PDT)) for k in range(3)]
    P_sbb = [[Buf(), Buf()] for _ in range(3)]
    den = entb(nc.sbuf_tensor("p1_den", [1, 512], F32)); denb = Buf()
    rcrep = entb(nc.sbuf_tensor("p1_rcrep", [64, 512], F32)); rcrepb = Buf()
    ot = [entb(nc.sbuf_tensor(f"p1_ot{k}", [64, 512], PDT)) for k in range(2)]
    otb = [Buf(), Buf()]
    cfg = dict(P_sb=P_sb, P_sbb=P_sbb, den=den, denb=denb, rcrep=rcrep,
               rcrepb=rcrepb, ot=ot, otb=otb, masks_sb=masks_sb, masksb=masksb,
               f32=True)

    flash_fork(c)
    for h in range(16):
        kh = h % 2
        em.dma("sync", lambda e, h=h, kh=kh: e.dma_start(
            out=KT[kh][:], in_=KTD[h * 64:(h + 1) * 64, :]), writes=[KTb[kh]])
        em.dma("gpsimd", lambda e, h=h, kh=kh: e.dma_start(
            out=Vh[kh][:].rearrange("p (i m) -> p i m", i=NT),
            in_=VTD.rearrange("(i p) m -> p i m", p=128)[
                :, :, h * 65:(h + 1) * 65]), writes=[Vhb[kh]])
        for qb in range(8):
            kq = (h * 8 + qb) % 2
            em.dma("scalar", lambda e, h=h, qb=qb, kq=kq: e.dma_start(
                out=QT[kq][:], in_=QTD[h * 64:(h + 1) * 64,
                                       qb * 512:(qb + 1) * 512]),
                writes=[QTb[kq]])
            flash_head(c, em, h, qb, kq, KT[kh][:], KTb[kh], Vh[kh][:], Vhb[kh],
                       QT[kq][:], QTb[kq], cfg, OTD, causal=True, Kd=64)
    if cfg.get("pending_tail"):
        cfg["pending_tail"]()
        cfg["pending_tail"] = None
    flash_merge(c)

    em.barrier()
    scope_b.close()

    MSK = c.MSK; MSKb = c.MSKb
    with contextlib.ExitStack() as ps2:
        ent2 = ps2.enter_context
        wo_sb = ent2(nc.sbuf_tensor("p1_wo", [128, NC * E], PDT)); wob = Buf()
        gw_sb = ent2(nc.sbuf_tensor("p1_gw", [128, NC * 4], F32)); gwb = Buf()
        gb_sb = ent2(nc.sbuf_tensor("p1_gb", [1, 4], F32)); gbb = Buf()
        em.dma("gpsimd", lambda e: e.dma_start(
            out=wo_sb[:].rearrange("p (cc m) -> p cc m", cc=NC),
            in_=W["woT0f"].rearrange("(cc p) m -> p cc m", p=128)), writes=[wob])
        em.dma("sync", lambda e: e.dma_start(
            out=gw_sb[:].rearrange("p (cc m) -> p cc m", cc=NC),
            in_=W["gwT"].rearrange("(cc p) m -> p cc m", p=128)), writes=[gwb])
        em.dma("sync", lambda e: e.dma_start(out=gb_sb[:], in_=W["gbrow"]),
               writes=[gbb])
        for cc in range(NC):
            em.v(lambda e, cc=cc: e.tensor_scalar_mul(
                gw_sb[:, cc * 4:(cc + 1) * 4], gw_sb[:, cc * 4:(cc + 1) * 4],
                c.w2ncol[:, cc:cc + 1]), reads=[c.w2ncolb, gwb],
                writes_also=[gwb])
        otc = [ent2(nc.sbuf_tensor(f"p1c_otc{k}", [128, NC * 128], PDT))
               for k in range(2)]
        otcb = [Buf(), Buf()]
        usb = [ent2(nc.sbuf_tensor(f"p1c_u{k}", [128, E], F32)) for k in range(2)]
        usbb = [Buf(), Buf()]
        n2sb = [ent2(nc.sbuf_tensor(f"p1c_n2{k}", [128, E], F32)) for k in range(2)]
        n2sbb = [Buf(), Buf()]
        n2bf = [ent2(nc.sbuf_tensor(f"p1c_n2bf{k}", [128, E], BF16))
                for k in range(2)]
        n2bfb = [Buf(), Buf()]
        n2T = [ent2(nc.sbuf_tensor(f"p1c_n2T{k}", [128, NC * 128], F32))
               for k in range(2)]
        n2Tb = [Buf(), Buf()]
        for i in range(NT):
            k = i % 2
            st = dict(sq=(c.sq[k][:, 0:E], c.sqb[k]), stat=(c.stat[k][:], c.statb[k]))
            em.dma("sync", lambda e, i=i, k=k: e.dma_start(
                out=c.xt[k][:], in_=Xin[i * 128:(i + 1) * 128, :]),
                writes=[c.xtb[k]])
            em.dma("scalar", lambda e, i=i, k=k: e.dma_start(
                out=otc[k][:].rearrange("p (cc t) -> p cc t", cc=NC),
                in_=OTD.rearrange("(cc p) t -> p cc t", p=128)[:, :,
                    i * 128:(i + 1) * 128]),
                writes=[otcb[k]])
            tm = []
            for cc in range(NC):
                for half in range(2):
                    tm.append(em.t(lambda e, cc=cc, half=half, k=k: e.matmul(
                        c.psA[:, half * 512:(half + 1) * 512],
                        R(otc[k][:, cc * 128:(cc + 1) * 128]),
                        R(wo_sb[:, cc * E + half * 512:cc * E + (half + 1) * 512]),
                        start=(cc == 0), stop=(cc == NC - 1)),
                        reads=[otcb[k], wob],
                        deps=(c.psAb.wr() if (cc == 0 and half == 0) else [])))
            c.psAb.w = tm[-1]; c.psAb.wl = []; c.psAb.r = []
            tlnx = ln_to(em, c, c.xt[k][:], c.xtb[k], usb[k][:], usbb[k], st, E)
            tu1 = em.v(lambda e, k=k: e.tensor_tensor(usb[k][:, 0:512],
                                                      usb[k][:, 0:512],
                                                      c.psA[:, 0:512], ALU.add),
                       deps=[tlnx], reads=[c.psAb], writes_also=[usbb[k]])
            tu = em.v(lambda e, k=k: e.tensor_tensor(usb[k][:, 512:1024],
                                                     usb[k][:, 512:1024],
                                                     c.psA[:, 512:1024], ALU.add),
                      deps=[tu1], reads=[c.psAb], writes_also=[usbb[k]])
            em.dma("gpsimd", lambda e, i=i, k=k: e.dma_start(
                out=U[i * 128:(i + 1) * 128, :], in_=usb[k][:]), reads=[usbb[k]])
            tn2 = ln_to(em, c, usb[k][:], usbb[k], n2sb[k][:], n2sbb[k], st, E,
                        eps6=True, rms=True)
            for cc in range(NC):
                transpose_128(em, c, n2sb[k][:, cc * 128:(cc + 1) * 128],
                              [n2sbb[k]], n2T[k][:, cc * 128:(cc + 1) * 128],
                              [n2Tb[k]], cc % 2, f32=True,
                              deps=(n2Tb[k].wr() if cc == 0 else []), final=(cc == NC - 1))
            tn2c = em.v(lambda e, k=k: e.tensor_copy(n2bf[k][:], n2T[k][:]),
                        reads=[n2Tb[k]], writes=[n2bfb[k]])
            em.dma("gpsimd", lambda e, i=i, k=k: e.dma_start(
                out=N2.rearrange("(cc p) t -> p cc t", p=128)[
                    :, :, i * 128:(i + 1) * 128],
                in_=n2bf[k][:].rearrange("p (cc t) -> p cc t", cc=NC)),
                reads=[n2bfb[k]])
            tg = []
            for cc in range(NC):
                tg.append(em.t(lambda e, cc=cc, k=k: e.matmul(
                    c.psC[:, 0:4], n2T[k][:, cc * 128:(cc + 1) * 128],
                    gw_sb[:, cc * 4:(cc + 1) * 4],
                    start=(cc == 0), stop=False),
                    reads=[n2Tb[k], gwb],
                    deps=(c.psCb.wr() if cc == 0 else [])))
            tg.append(em.t(lambda e: e.matmul(
                c.psC[:, 0:4], c.ones_f32[0:1, :], gb_sb[0:1, :],
                start=False, stop=True), reads=[gbb, c.constb]))
            c.psCb.w = tg[-1]; c.psCb.wl = []; c.psCb.r = []
            stt = c.stat[k]
            mx = stt[:, 5:6]; nt_ = stt[:, 6:7]
            tmx = em.v(lambda e, k=k: e.reduce_max(mx, c.psC[:, 0:4], axis=AX.X),
                       reads=[c.psCb], risky=True, writes_also=[c.statb[k]])
            tprev = tmx
            for ee in range(4):
                tq_ = em.v(lambda e, ee=ee, i=i: e.tensor_scalar(
                    MSK[:, i * 4 + ee:i * 4 + ee + 1], c.psC[:, ee:ee + 1],
                    mx[:, 0:1], None, ALU.is_ge),
                    deps=[tprev], risky=True, reads=[c.psCb, c.statb[k]],
                    writes_also=[MSKb])
                if ee == 0:
                    tprev = em.v(lambda e, i=i: e.tensor_scalar(
                        nt_, MSK[:, i * 4:i * 4 + 1], -1.0, 1.0, ALU.mult,
                        ALU.add),
                        deps=[tq_], risky=True, reads=[MSKb],
                        writes_also=[c.statb[k]])
                else:
                    tq2 = em.v(lambda e, ee=ee, i=i: e.tensor_tensor(
                        MSK[:, i * 4 + ee:i * 4 + ee + 1],
                        MSK[:, i * 4 + ee:i * 4 + ee + 1], nt_, ALU.mult),
                        deps=[tq_], risky=True, reads=[c.statb[k]],
                        writes_also=[MSKb])
                    tprev = tq2
                    if ee < 3:
                        tprev = em.v(lambda e, ee=ee, i=i: e.tensor_tensor(
                            nt_, nt_, MSK[:, i * 4 + ee:i * 4 + ee + 1],
                            ALU.subtract),
                            deps=[tq2], risky=True, reads=[MSKb],
                            writes_also=[c.statb[k]])

    em.barrier()

    yacc_trk = [Buf() for _ in range(NT)]
    MSK = c.MSK; MSKb = c.MSKb
    with contextlib.ExitStack() as ps3:
        ent3 = ps3.enter_context
        wte = ent3(nc.sbuf_tensor("p1d_wte", [128, 32 * E], BF16))
        wteb = Buf()
        xmT = [ent3(nc.sbuf_tensor(f"p1d_xmT{k}", [128, NC * 512], BF16))
               for k in range(2)]
        xmTb = [Buf(), Buf()]
        hst = [ent3(nc.sbuf_tensor(f"p1d_hst{k}", [128, 512], BF16))
               for k in range(2)]
        hstb = [Buf(), Buf()]
        hTin = [ent3(nc.sbuf_tensor(f"p1d_hTin{k}", [128, 32 * 128], BF16))
                for k in range(2)]
        hTinb = [Buf(), Buf()]
        b1c = ent3(nc.sbuf_tensor("p1d_b1c", [128, 32], F32)); b1cb = Buf()
        b2r = ent3(nc.sbuf_tensor("p1d_b2r", [1, E], BF16)); b2rb = Buf()
        yin = [ent3(nc.sbuf_tensor(f"p1d_yin{k}", [128, E], F32)) for k in range(2)]
        yinb = [Buf(), Buf()]
        yout = [ent3(nc.sbuf_tensor(f"p1d_yout{k}", [128, E], F32)) for k in range(2)]
        youtb = [Buf(), Buf()]
        uin = [ent3(nc.sbuf_tensor(f"p1d_uin{k}", [128, E], F32)) for k in range(2)]
        uinb = [Buf(), Buf()]
        lnw = [ent3(nc.sbuf_tensor(f"p1d_lnw{k}", [128, E], F32)) for k in range(2)]
        lnwb = [Buf(), Buf()]

        HTDb = Buf()
        for ex in (range(4) if "d" in P1_PARTS else []):
            war_htd = HTDb.wr()
            HTDb.w = None; HTDb.wl = []; HTDb.r = []
            em.dma("gpsimd", lambda e, ex=ex: e.dma_start(
                out=wte[:].rearrange("p (cc m) -> p cc m", cc=NC),
                in_=W["w1T"][ex].rearrange("(cc p) m -> p cc m", p=128)),
                writes=[wteb])
            em.dma("sync", lambda e, ex=ex: e.dma_start(
                out=b1c[:], in_=W["b1col"][ex]), writes=[b1cb])
            for cc in range(NC):
                em.v(lambda e, cc=cc: e.tensor_scalar_mul(
                    wte[:, cc * 4096:(cc + 1) * 4096],
                    wte[:, cc * 4096:(cc + 1) * 4096],
                    c.w2ncol[:, cc:cc + 1]), reads=[c.w2ncolb, wteb],
                    writes_also=[wteb])
            for gg in range(8):
                kg = gg % 2
                war = xmTb[kg].wr()
                lds = []
                for cc in range(NC):
                    lds.append(em.dma("sync", lambda e, gg=gg, kg=kg, cc=cc:
                                      e.dma_start(
                        out=xmT[kg][:, cc * 512:(cc + 1) * 512],
                        in_=N2[cc * 128:(cc + 1) * 128,
                               gg * 512:(gg + 1) * 512]),
                        deps=war))
                xmTb[kg].w = None; xmTb[kg].wl = lds; xmTb[kg].r = []
                for js in range(32):
                    kj = js % 2
                    psH, psHb = (c.psC, c.psCb) if kj == 0 else (c.psD, c.psDb)
                    th = []
                    for cc in range(NC):
                        th.append(em.t(lambda e, js=js, cc=cc, kg=kg, psH=psH:
                                       e.matmul(
                            psH[:, 0:512],
                            wte[:, cc * 4096 + js * 128:cc * 4096 + (js + 1) * 128],
                            xmT[kg][:, cc * 512:(cc + 1) * 512],
                            start=(cc == 0), stop=(cc == NC - 1)),
                            reads=[wteb, xmTb[kg]],
                            deps=(psHb.wr() if cc == 0 else [])))
                    psHb.w = th[-1]; psHb.wl = []; psHb.r = []
                    if kj == 0:
                        tev = em.s(lambda e, kj=kj, js=js, psH=psH: e.activation(
                            hst[kj][:], psH[:, 0:512],
                            AF.Relu, bias=b1c[:, js:js + 1], scale=1.0),
                            reads=[psHb, b1cb], writes=[hstb[kj]])
                    else:
                        tev = em.v(lambda e, kj=kj, js=js, psH=psH: e.tensor_scalar(
                            hst[kj][:], psH[:, 0:512],
                            b1c[:, js:js + 1], 0.0, ALU.add, ALU.max),
                            reads=[psHb, b1cb], writes=[hstb[kj]])
                    em.dma("gpsimd" if kj == 0 else "scalar",
                           lambda e, js=js, gg=gg, kj=kj: e.dma_start(
                               out=HTD.rearrange("(i js q) t -> q i js t",
                                                 js=32, q=128)[
                                   :, gg * 4:(gg + 1) * 4, js, :],
                               in_=hst[kj][:].rearrange("q (i t) -> q i t",
                                                        i=4)),
                           deps=(war_htd if (gg == 0 and js < 2) else []),
                           reads=[hstb[kj]], writes_also=[HTDb])
            em.dma("gpsimd", lambda e, ex=ex: e.dma_start(
                out=wte[:].rearrange("p (cc m) -> p cc m", cc=32),
                in_=W["w2T"][ex].rearrange("(cc p) m -> p cc m", p=128)),
                writes=[wteb])
            em.dma("gpsimd", lambda e, ex=ex: e.dma_start(
                out=b2r[:], in_=W["b2row"][ex]), writes=[b2rb])
            for i in range(NT):
                k = i % 2
                war_h = hTinb[k].wr()
                hld = []
                for qr in range(4):
                    hld.append(em.dma("sync", lambda e, i=i, k=k, qr=qr:
                                      e.dma_start(
                        out=hTin[k][:, qr * 1024:(qr + 1) * 1024].rearrange(
                            "p (js t) -> p js t", js=8),
                        in_=HTD.rearrange("(i js q) t -> q i js t", js=32,
                                          q=128)[:, i, qr * 8:(qr + 1) * 8, :]),
                        deps=war_h, reads=[HTDb]))
                hTinb[k].w = None; hTinb[k].wl = hld; hTinb[k].r = []
                psY, psYb = (c.psA, c.psAb) if k == 0 else (c.psB, c.psBb)
                ty = []
                for js in range(32):
                    for half in range(2):
                        ty.append(em.t(lambda e, js=js, half=half, k=k,
                                       psY=psY: e.matmul(
                            psY[:, half * 512:(half + 1) * 512],
                            hTin[k][:, js * 128:(js + 1) * 128],
                            wte[:, js * E + half * 512:js * E + (half + 1) * 512],
                            start=(js == 0), stop=False),
                            reads=[hTinb[k], wteb],
                            deps=(psYb.wr() if js == 0 and half == 0 else [])))
                for half in range(2):
                    ty.append(em.t(lambda e, half=half, psY=psY: e.matmul(
                        psY[:, half * 512:(half + 1) * 512],
                        c.ones_bf[0:1, :], b2r[0:1, half * 512:(half + 1) * 512],
                        start=False, stop=True),
                        reads=[b2rb, c.constb]))
                psYb.w = ty[-1]; psYb.wl = []; psYb.r = []
                if ex == 0:
                    tv = em.v(lambda e, i=i, k=k, psY=psY: e.tensor_scalar_mul(
                        yout[k][:], psY[:], MSK[:, i * 4:i * 4 + 1]),
                        reads=[psYb, MSKb], writes=[youtb[k]])
                    em.dma("gpsimd", lambda e, i=i, k=k: e.dma_start(
                        out=YACC[i * 128:(i + 1) * 128, :], in_=yout[k][:]),
                        reads=[youtb[k]], writes=[yacc_trk[i]])
                elif ex < 3:
                    em.dma("sync", lambda e, i=i, k=k: e.dma_start(
                        out=yin[k][:], in_=YACC[i * 128:(i + 1) * 128, :]),
                        reads=[yacc_trk[i]], writes=[yinb[k]])
                    tv = em.v(lambda e, i=i, k=k, ex=ex, psY=psY:
                              e.tensor_scalar_mul(
                        yout[k][:], psY[:], MSK[:, i * 4 + ex:i * 4 + ex + 1]),
                        reads=[psYb, MSKb], writes=[youtb[k]])
                    tv2 = em.v(lambda e, k=k: e.tensor_tensor(
                        yout[k][:], yout[k][:], yin[k][:], ALU.add),
                        deps=[tv], reads=[yinb[k]], writes=[youtb[k]])
                    em.dma("gpsimd", lambda e, i=i, k=k: e.dma_start(
                        out=YACC[i * 128:(i + 1) * 128, :], in_=yout[k][:]),
                        reads=[youtb[k]], writes=[yacc_trk[i]])
                else:
                    em.dma("sync", lambda e, i=i, k=k: e.dma_start(
                        out=yin[k][:], in_=YACC[i * 128:(i + 1) * 128, :]),
                        reads=[yacc_trk[i]], writes=[yinb[k]])
                    em.dma("sync", lambda e, i=i, k=k: e.dma_start(
                        out=uin[k][:], in_=U[i * 128:(i + 1) * 128, :]),
                        writes=[uinb[k]])
                    em.dma("scalar", lambda e, i=i, k=k: e.dma_start(
                        out=c.xt[k][:], in_=Xin[i * 128:(i + 1) * 128, :]),
                        writes=[c.xtb[k]])
                    tv = em.v(lambda e, i=i, k=k, psY=psY: e.tensor_scalar_mul(
                        yout[k][:], psY[:], MSK[:, i * 4 + 3:i * 4 + 4]),
                        reads=[psYb, MSKb], writes=[youtb[k]])
                    tv2 = em.v(lambda e, k=k: e.tensor_tensor(
                        yout[k][:], yout[k][:], yin[k][:], ALU.add),
                        deps=[tv], reads=[yinb[k]], writes=[youtb[k]])
                    tv3 = em.v(lambda e, k=k: e.tensor_tensor(
                        yout[k][:], yout[k][:], uin[k][:], ALU.add),
                        deps=[tv2], reads=[uinb[k]], writes=[youtb[k]])
                    st = dict(sq=(c.sq[k][:, 0:E], c.sqb[k]),
                              stat=(c.stat[k][:], c.statb[k]))
                    tln = ln_to(em, c, yout[k][:], youtb[k], lnw[k][:],
                                lnwb[k], st, E)
                    em.v(lambda e, k=k: e.tensor_tensor(
                        lnw[k][:], lnw[k][:], c.xt[k][:], ALU.add),
                        deps=[tln], reads=[c.xtb[k]], writes=[lnwb[k]])
                    tgt = out_final if out_final is not None else X2
                    em.dma("gpsimd", lambda e, i=i, k=k, tgt=tgt: e.dma_start(
                        out=tgt[i * 128:(i + 1) * 128, :], in_=lnw[k][:]),
                        reads=[lnwb[k]])


def phase2(c, pstack, W, Xin, X3, XN, OTD, out_final=None):
    nc = c.nc; em = c.em
    ent = pstack.enter_context

    wo_sb = ent(nc.sbuf_tensor("p2_wo", [128, NC * E], BF16)); wob = Buf()
    bo_row = ent(nc.sbuf_tensor("p2_bo", [1, E], BF16)); bob = Buf()
    em.dma("gpsimd", lambda e: e.dma_start(
        out=wo_sb[:].rearrange("p (cc m) -> p cc m", cc=NC),
        in_=W["wo2"].rearrange("(cc p) m -> p cc m", p=128)), writes=[wob])
    em.dma("gpsimd", lambda e: e.dma_start(out=bo_row[:], in_=W["bo2row"]),
           writes=[bob])

    bqc = ent(nc.sbuf_tensor("p2_bqc", [128, 8], F32)); bqcb = Buf()
    bkc = ent(nc.sbuf_tensor("p2_bkc", [128, 8], F32)); bkcb = Buf()
    brc = ent(nc.sbuf_tensor("p2_brc", [64, 8], F32)); brcb = Buf()
    brcs = ent(nc.sbuf_tensor("p2_brcs", [64, 8], F32)); brcsb = Buf()
    bqcs = ent(nc.sbuf_tensor("p2_bqcs", [128, 8], F32)); bqcsb = Buf()
    bv_row = ent(nc.sbuf_tensor("p2_bv", [1, E], BF16)); bvb = Buf()
    em.dma("sync", lambda e: e.dma_start(out=bqc[:], in_=W["bq2col"]), writes=[bqcb])
    em.dma("sync", lambda e: e.dma_start(out=bkc[:], in_=W["bk2col"]), writes=[bkcb])
    em.dma("sync", lambda e: e.dma_start(out=brc[:], in_=W["br2col"]), writes=[brcb])
    em.dma("gpsimd", lambda e: e.dma_start(out=bv_row[:], in_=W["bv2row"]),
           writes=[bvb])
    em.v(lambda e: e.tensor_scalar_mul(bqcs[:], bqc[:], SCALE96), reads=[bqcb],
         writes=[bqcsb])
    em.v(lambda e: e.tensor_scalar_mul(brcs[:], brc[:], SCALE96), reads=[brcb],
         writes=[brcsb])

    xnbf = [ent(nc.sbuf_tensor(f"p2_xnbf{k}", [128, E], BF16)) for k in range(2)]
    xnbfb = [Buf(), Buf()]
    for i in range(NT):
        k = i % 2
        st = dict(sq=(c.sq[k][:, 0:E], c.sqb[k]), stat=(c.stat[k][:], c.statb[k]))
        em.dma("sync", lambda e, i=i, k=k: e.dma_start(
            out=c.xt[k][:], in_=Xin[i * 128:(i + 1) * 128, :]), writes=[c.xtb[k]])
        ln_to(em, c, c.xt[k][:], c.xtb[k], xnbf[k][:], xnbfb[k], st, E)
        em.dma("gpsimd", lambda e, i=i, k=k: e.dma_start(
            out=XN[i * 128:(i + 1) * 128, :], in_=xnbf[k][:]), reads=[xnbfb[k]])

    em.barrier()

    QTh = [ent(nc.sbuf_tensor(f"p2_QT{j}", [96, S], BF16)) for j in range(2)]
    QThb = [Buf(), Buf()]
    KTh = [ent(nc.sbuf_tensor(f"p2_KT{j}", [96, S], BF16)) for j in range(2)]
    KThb = [Buf(), Buf()]
    Vh = [ent(nc.sbuf_tensor(f"p2_V{j}", [128, NT * 65], BF16)) for j in range(2)]
    Vhb = [Buf(), Buf()]
    xnT = [ent(nc.sbuf_tensor(f"p2_xnT{k}", [128, NC * 512], BF16))
           for k in range(2)]
    xnTb = [Buf(), Buf()]
    wqp = ent(nc.sbuf_tensor("p2_wqp", [128, NC * 128], BF16)); wqpb = Buf()
    wkp = ent(nc.sbuf_tensor("p2_wkp", [128, NC * 128], BF16)); wkpb = Buf()
    wvp = ent(nc.sbuf_tensor("p2_wvp", [128, NC * 128], BF16)); wvpb = Buf()
    wrp = ent(nc.sbuf_tensor("p2_wrp", [128, NC * 64], BF16)); wrpb = Buf()
    P_sb = [ent(nc.sbuf_tensor(f"p2_P{k}", [128, 1024], BF16)) for k in range(3)]
    P_sbb = [[Buf(), Buf()] for _ in range(3)]
    den = ent(nc.sbuf_tensor("p2_den", [1, 512], F32)); denb = Buf()
    rcrep = ent(nc.sbuf_tensor("p2_rcrep", [64, 512], F32)); rcrepb = Buf()
    ot = [ent(nc.sbuf_tensor(f"p2_ot{k}", [64, 512], BF16)) for k in range(2)]
    otb = [Buf(), Buf()]
    cfg = dict(P_sb=P_sb, P_sbb=P_sbb, den=den, denb=denb, rcrep=rcrep,
               rcrepb=rcrepb, ot=ot, otb=otb, masks_sb=None, masksb=None)

    for p in range(8):
        em.dma("gpsimd", lambda e, p=p: e.dma_start(
            out=wqp[:].rearrange("pp (cc m) -> pp cc m", cc=NC),
            in_=W["wq2"].rearrange("(cc pp) m -> pp cc m", pp=128)[
                :, :, p * 128:(p + 1) * 128]), writes=[wqpb])
        em.dma("gpsimd", lambda e, p=p: e.dma_start(
            out=wkp[:].rearrange("pp (cc m) -> pp cc m", cc=NC),
            in_=W["wk2"].rearrange("(cc pp) m -> pp cc m", pp=128)[
                :, :, p * 128:(p + 1) * 128]), writes=[wkpb])
        em.dma("gpsimd", lambda e, p=p: e.dma_start(
            out=wvp[:].rearrange("pp (cc m) -> pp cc m", cc=NC),
            in_=W["wv2"].rearrange("(cc pp) m -> pp cc m", pp=128)[
                :, :, p * 128:(p + 1) * 128]), writes=[wvpb])
        em.dma("gpsimd", lambda e, p=p: e.dma_start(
            out=wrp[:].rearrange("pp (cc m) -> pp cc m", cc=NC),
            in_=W["wr2"].rearrange("(cc pp) m -> pp cc m", pp=128)[
                :, :, p * 64:(p + 1) * 64]), writes=[wrpb])
        for s in range(8):
            ks = s % 2
            ld = []
            war = xnTb[ks].wr()
            for cc in range(NC):
                ld.append(em.dma("sync", lambda e, s=s, cc=cc, ks=ks: e.dma_start(
                    out=xnT[ks][:, cc * 512:(cc + 1) * 512],
                    in_=XN[s * 512:(s + 1) * 512,
                           cc * 128:(cc + 1) * 128], transpose=True),
                    deps=war))
            xnTb[ks].w = None; xnTb[ks].wl = list(ld); xnTb[ks].r = []
            tq = []
            for cc in range(NC):
                tq.append(em.t(lambda e, cc=cc, ks=ks: e.matmul(
                    c.psA[:, 0:512], wqp[:, cc * 128:(cc + 1) * 128],
                    xnT[ks][:, cc * 512:(cc + 1) * 512],
                    start=(cc == 0), stop=(cc == NC - 1)),
                    reads=[wqpb, xnTb[ks]],
                    deps=(c.psAb.wr() if cc == 0 else [])))
            c.psAb.w = tq[-1]; c.psAb.wl = []; c.psAb.r = []
            for hh in range(2):
                em.s(lambda e, hh=hh, s=s, p=p: e.activation(
                    QTh[hh][0:64, s * 512:(s + 1) * 512],
                    c.psA[hh * 64:(hh + 1) * 64, 0:512], AF.Identity,
                    bias=bqcs[hh * 64:(hh + 1) * 64, p:p + 1], scale=SCALE96),
                    reads=[c.psAb, bqcsb],
                    writes=([QThb[hh]] if s == 0 else []),
                    writes_also=([QThb[hh]] if s > 0 else []),
                    deps=(QThb[hh].wr() if s == 0 else []))
            tk = []
            for cc in range(NC):
                tk.append(em.t(lambda e, cc=cc, ks=ks: e.matmul(
                    c.psB[:, 0:512], wkp[:, cc * 128:(cc + 1) * 128],
                    xnT[ks][:, cc * 512:(cc + 1) * 512],
                    start=(cc == 0), stop=(cc == NC - 1)),
                    reads=[wkpb, xnTb[ks]],
                    deps=(c.psBb.wr() if cc == 0 else [])))
            c.psBb.w = tk[-1]; c.psBb.wl = []; c.psBb.r = []
            for hh in range(2):
                em.s(lambda e, hh=hh, s=s, p=p: e.activation(
                    KTh[hh][0:64, s * 512:(s + 1) * 512],
                    c.psB[hh * 64:(hh + 1) * 64, 0:512], AF.Identity,
                    bias=bkc[hh * 64:(hh + 1) * 64, p:p + 1], scale=1.0),
                    reads=[c.psBb, bkcb],
                    writes=([KThb[hh]] if s == 0 else []),
                    writes_also=([KThb[hh]] if s > 0 else []),
                    deps=(KThb[hh].wr() if s == 0 else []))
            tr = []
            for cc in range(NC):
                tr.append(em.t(lambda e, cc=cc, ks=ks: e.matmul(
                    c.psC[0:64, 0:512], wrp[:, cc * 64:(cc + 1) * 64],
                    xnT[ks][:, cc * 512:(cc + 1) * 512],
                    start=(cc == 0), stop=(cc == NC - 1)),
                    reads=[wrpb, xnTb[ks]],
                    deps=(c.psCb.wr() if cc == 0 else [])))
            c.psCb.w = tr[-1]; c.psCb.wl = []; c.psCb.r = []
            for hh in range(2):
                em.s(lambda e, hh=hh, s=s, p=p: e.activation(
                    QTh[hh][64:96, s * 512:(s + 1) * 512],
                    c.psC[hh * 32:(hh + 1) * 32, 0:512], AF.Identity,
                    bias=brcs[hh * 32:(hh + 1) * 32, p:p + 1], scale=SCALE96),
                    reads=[c.psCb, brcsb], writes_also=[QThb[hh]])
                em.s(lambda e, hh=hh, s=s, p=p: e.activation(
                    KTh[hh][64:96, s * 512:(s + 1) * 512],
                    c.psC[hh * 32:(hh + 1) * 32, 0:512], AF.Identity,
                    bias=brc[hh * 32:(hh + 1) * 32, p:p + 1], scale=1.0),
                    reads=[c.psCb, brcb], writes_also=[KThb[hh]])
            for tt in range(4):
                i = s * 4 + tt
                tv = []
                for cc in range(NC):
                    tv.append(em.t(lambda e, cc=cc, tt=tt, ks=ks: e.matmul(
                        c.psD[:, 0:128],
                        xnT[ks][:, cc * 512 + tt * 128:cc * 512 + (tt + 1) * 128],
                        wvp[:, cc * 128:(cc + 1) * 128],
                        start=(cc == 0), stop=False),
                        reads=[wvpb, xnTb[ks]],
                        deps=(c.psDb.wr() if cc == 0 else [])))
                tv.append(em.t(lambda e, p=p: e.matmul(
                    c.psD[:, 0:128], c.ones_bf[0:1, :],
                    bv_row[0:1, p * 128:(p + 1) * 128], start=False, stop=True),
                    reads=[bvb, c.constb]))
                c.psDb.w = tv[-1]; c.psDb.wl = []; c.psDb.r = []
                for hh in range(2):
                    em.s(lambda e, hh=hh, i=i: e.copy(
                        Vh[hh][:, i * 65:i * 65 + 64],
                        c.psD[:, hh * 64:(hh + 1) * 64]),
                        reads=[c.psDb],
                        writes=([Vhb[hh]] if i == 0 else []),
                        writes_also=([Vhb[hh]] if i > 0 else []),
                        deps=(Vhb[hh].wr() if i == 0 else []))
                    em.v(lambda e, hh=hh, i=i: e.memset(
                        Vh[hh][:, i * 65 + 64:i * 65 + 65], 1.0),
                        writes_also=[Vhb[hh]])
        flash_fork(c)
        for hh in range(2):
            h = 2 * p + hh
            for qb in range(8):
                kq = (h * 8 + qb) % 2
                flash_head(c, em, h, qb, kq, KTh[hh][:], KThb[hh], Vh[hh][:],
                           Vhb[hh], QTh[hh][:, qb * 512:(qb + 1) * 512],
                           QThb[hh], cfg, OTD, causal=False, Kd=96)
        if cfg.get("pending_tail"):
            cfg["pending_tail"]()
            cfg["pending_tail"] = None
        flash_merge(c)

    em.barrier()

    with contextlib.ExitStack() as ps2:
        tgt = out_final if out_final is not None else X3
        proj_residual(c, em, ps2, OTD, Xin, tgt, wo_sb[:], wob, bo_row, bob,
                      "p2c")


def phase3(c, pstack, W, Xin, out):
    nc = c.nc; em = c.em
    ent = pstack.enter_context

    t1sb = ent(nc.sbuf_tensor("p3_t1", [128, NC * 4096], BF16)); t1b = Buf()
    t2sb = ent(nc.sbuf_tensor("p3_t2", [128, 32 * E], BF16)); t2b = Buf()
    stat2 = ent(nc.sbuf_tensor("p3_stat2", [128, 16], F32)); stat2b = Buf()
    g1c = ent(nc.sbuf_tensor("p3_g1c", [128, 8], F32)); g1cb = Buf()
    b1lnc = ent(nc.sbuf_tensor("p3_b1lnc", [128, 8], BF16)); b1lncb = Buf()
    g2c = ent(nc.sbuf_tensor("p3_g2c", [128, 32], F32)); g2cb = Buf()
    b2lnc = ent(nc.sbuf_tensor("p3_b2lnc", [128, 32], BF16)); b2lncb = Buf()
    bv1bf = ent(nc.sbuf_tensor("p3_bv1bf", [1, 4096], BF16)); bv1bfb = Buf()
    bv2bf = ent(nc.sbuf_tensor("p3_bv2bf", [1, E], BF16)); bv2bfb = Buf()

    em.dma("sync", lambda e: e.dma_start(out=g1c[:], in_=W["l1g1col"]), writes=[g1cb])
    em.dma("gpsimd", lambda e: e.dma_start(out=b1lnc[:], in_=W["l1b1lncol"]),
           writes=[b1lncb])
    em.dma("sync", lambda e: e.dma_start(out=g2c[:], in_=W["l1g2col"]), writes=[g2cb])
    em.dma("gpsimd", lambda e: e.dma_start(out=b2lnc[:], in_=W["l1b2lncol"]),
           writes=[b2lncb])

    prep = contextlib.ExitStack()
    entp = prep.enter_context
    _wch = entp(nc.sbuf_tensor("p3_wch", [128, 4096], F32))
    wch = [_wch, _wch]
    _wchb = Buf()
    wchb = [_wchb, _wchb]
    srow = entp(nc.sbuf_tensor("p3_srow", [1, 16], F32)); srowb = Buf()
    bsl = entp(nc.sbuf_tensor("p3_bsl", [1, 512], BF16)); bslb = Buf()

    def quantize(wT_ap, nchunk, chw, tgt_sb, tgt_b, lo):
        acc = stat2[:, lo + 4:lo + 5]
        part = stat2[:, lo + 5:lo + 6]
        ta = em.v(lambda e: e.memset(acc, 0.0), writes_also=[stat2b], risky=True)
        tprev = ta
        for ch in range(nchunk):
            kc = ch % 2
            for sub in range(0, chw, E):
                em.dma("sync", lambda e, ch=ch, sub=sub, kc=kc: e.dma_start(
                    out=wch[kc][:, sub:sub + E],
                    in_=wT_ap[ch * 128:(ch + 1) * 128, sub:sub + E]),
                    writes=([wchb[kc]] if sub == 0 else []),
                    writes_also=([wchb[kc]] if sub else []))
            tb_ = em.s(lambda e, kc=kc, chw=chw: e.activation(
                c.sq[kc][:, 0:E], wch[kc][:, 0:chw] if chw <= E else wch[kc][:, 0:E],
                AF.Abs, bias=0.0, scale=1.0, accum_out=part),
                risky=True, reads=[wchb[kc]], writes=[c.sqb[kc]],
                writes_also=[stat2b], deps=[tprev]) if chw <= E else None
            if chw > E:
                for sub in range(0, chw, E):
                    tb_ = em.s(lambda e, kc=kc, sub=sub: e.activation(
                        c.sq[kc][:, 0:E], wch[kc][:, sub:sub + E],
                        AF.Abs, bias=0.0, scale=1.0, accum_out=part),
                        risky=True, reads=[wchb[kc]], writes=[c.sqb[kc]],
                        writes_also=[stat2b], deps=[tprev])
                    tprev = em.v(lambda e: e.tensor_tensor(acc, acc, part,
                                                           ALU.add),
                                 deps=[tb_], risky=True, writes_also=[stat2b])
            else:
                tprev = em.v(lambda e: e.tensor_tensor(acc, acc, part, ALU.add),
                             deps=[tb_], risky=True, writes_also=[stat2b])
        tsum = em.t(lambda e: e.matmul(c.psD[0:1, 0:1], acc, c.ones_f32[:, 0:1],
                                       start=True, stop=True),
                    deps=[tprev] + c.psDb.wr(), reads=[stat2b, c.constb])
        c.psDb.w = tsum; c.psDb.wl = []; c.psDb.r = []
        tm = em.s(lambda e: e.copy(srow[0:1, lo:lo + 1], c.psD[0:1, 0:1]),
                  deps=[tsum], reads=[c.psDb], writes_also=[srowb])
        tm2 = em.v(lambda e: e.tensor_scalar(
            srow[0:1, lo:lo + 1], srow[0:1, lo:lo + 1],
            1.0 / (4096.0 * 1024.0), 1e-5, ALU.mult, ALU.max),
            deps=[tm], risky=True, writes_also=[srowb])
        trep = em.t(lambda e: e.matmul(c.psD[:, 0:1], c.ones_f32[0:1, :],
                                       srow[0:1, lo:lo + 1],
                                       start=True, stop=True),
                    deps=[tm2] + c.psDb.wr(), reads=[srowb, c.constb])
        c.psDb.w = trep; c.psDb.wl = []; c.psDb.r = []
        tmc = em.s(lambda e: e.copy(stat2[:, lo:lo + 1], c.psD[:, 0:1]),
                   deps=[trep], reads=[c.psDb], writes_also=[stat2b])
        t_thr = em.v(lambda e: e.tensor_scalar_mul(
            stat2[:, lo + 1:lo + 2], stat2[:, lo:lo + 1], 0.5),
            deps=[tmc], risky=True, writes_also=[stat2b])
        t_nthr = em.v(lambda e: e.tensor_scalar_mul(
            stat2[:, lo + 2:lo + 3], stat2[:, lo:lo + 1], -0.5),
            deps=[tmc], risky=True, writes_also=[stat2b])
        t_im = em.v(lambda e: e.reciprocal(
            stat2[:, lo + 3:lo + 4], stat2[:, lo:lo + 1]),
            deps=[tmc], risky=True, writes_also=[stat2b])
        for ch in range(nchunk):
            kc = ch % 2
            for sub in range(0, chw, E):
                em.dma("scalar", lambda e, ch=ch, sub=sub, kc=kc: e.dma_start(
                    out=wch[kc][:, sub:sub + E],
                    in_=wT_ap[ch * 128:(ch + 1) * 128, sub:sub + E]),
                    writes=([wchb[kc]] if sub == 0 else []),
                    writes_also=([wchb[kc]] if sub else []))
            for sub in range(0, chw, E):
                tg_ = em.v(lambda e, kc=kc, sub=sub: e.tensor_scalar(
                    c.tmpA[kc][:, 0:E], wch[kc][:, sub:sub + E],
                    stat2[:, lo + 1:lo + 2], None, ALU.is_gt),
                    deps=[t_thr], reads=[wchb[kc], stat2b],
                    writes=[c.tmpAb[kc]])
                tl_ = em.v(lambda e, kc=kc, sub=sub: e.tensor_scalar(
                    c.tmpB[kc][:, 0:E], wch[kc][:, sub:sub + E],
                    stat2[:, lo + 2:lo + 3], None, ALU.is_lt),
                    deps=[t_nthr], reads=[wchb[kc], stat2b],
                    writes=[c.tmpBb[kc]])
                em.v(lambda e, ch=ch, kc=kc, sub=sub, chw=chw: e.tensor_tensor(
                    tgt_sb[:, ch * chw + sub:ch * chw + sub + E],
                    c.tmpA[kc][:, 0:E], c.tmpB[kc][:, 0:E], ALU.subtract),
                    deps=[tg_, tl_], reads=[c.tmpAb[kc], c.tmpBb[kc]],
                    writes_also=[tgt_b])

    quantize(W["l1w1T"], NC, 4096, t1sb, t1b, 0)
    quantize(W["l1w2T"], 32, E, t2sb, t2b, 8)

    for jb in range(8):
        tb_ = []
        for cc in range(NC):
            tb_.append(em.t(lambda e, jb=jb, cc=cc: e.matmul(
                c.psC[0:1, 0:512], b1lnc[:, cc:cc + 1],
                t1sb[:, cc * 4096 + jb * 512:cc * 4096 + (jb + 1) * 512],
                start=(cc == 0), stop=(cc == NC - 1)),
                reads=[b1lncb, t1b],
                deps=(c.psCb.wr() if cc == 0 else [])))
        c.psCb.w = tb_[-1]; c.psCb.wl = []; c.psCb.r = []
        em.dma("gpsimd", lambda e, jb=jb: e.dma_start(
            out=bsl[:], in_=W["l1b1row"][0:1, jb * 512:(jb + 1) * 512]),
            writes=[bslb])
        tsc = em.v(lambda e, jb=jb: e.tensor_scalar_mul(
            c.sq[0][0:1, 0:512], bsl[0:1, :], stat2[0:1, 3:4]),
            reads=[bslb, stat2b], writes=[c.sqb[0]], risky=True)
        em.v(lambda e, jb=jb: e.tensor_tensor(
            bv1bf[0:1, jb * 512:(jb + 1) * 512], c.psC[0:1, 0:512],
            c.sq[0][0:1, 0:512], ALU.add),
            deps=[tsc], reads=[c.psCb, c.sqb[0]], writes_also=[bv1bfb], risky=True)
    for half in range(2):
        tb_ = []
        for js in range(32):
            tb_.append(em.t(lambda e, js=js, half=half: e.matmul(
                c.psC[0:1, 0:512], b2lnc[:, js:js + 1],
                t2sb[:, js * E + half * 512:js * E + (half + 1) * 512],
                start=(js == 0), stop=(js == 31)),
                reads=[b2lncb, t2b],
                deps=(c.psCb.wr() if js == 0 else [])))
        c.psCb.w = tb_[-1]; c.psCb.wl = []; c.psCb.r = []
        em.dma("gpsimd", lambda e, half=half: e.dma_start(
            out=bsl[:], in_=W["l1b2row"][0:1, half * 512:(half + 1) * 512]),
            writes=[bslb])
        tsc = em.v(lambda e, half=half: e.tensor_scalar_mul(
            c.sq[0][0:1, 0:512], bsl[0:1, :], stat2[0:1, 11:12]),
            reads=[bslb, stat2b], writes=[c.sqb[0]], risky=True)
        em.v(lambda e, half=half: e.tensor_tensor(
            bv2bf[0:1, half * 512:(half + 1) * 512], c.psC[0:1, 0:512],
            c.sq[0][0:1, 0:512], ALU.add),
            deps=[tsc], reads=[c.psCb, c.sqb[0]], writes_also=[bv2bfb], risky=True)
    for cc in range(NC):
        em.v(lambda e, cc=cc: e.tensor_scalar_mul(
            t1sb[:, cc * 4096:(cc + 1) * 4096], t1sb[:, cc * 4096:(cc + 1) * 4096],
            g1c[:, cc:cc + 1]), reads=[g1cb, t1b], writes_also=[t1b])
    for js in range(32):
        em.v(lambda e, js=js: e.tensor_scalar_mul(
            t2sb[:, js * E:(js + 1) * E], t2sb[:, js * E:(js + 1) * E],
            g2c[:, js:js + 1]), reads=[g2cb, t2b], writes_also=[t2b])

    em.barrier()
    prep.close()

    n1bf = [ent(nc.sbuf_tensor(f"p3_n1{k}", [128, E], BF16)) for k in range(2)]
    n1bfb = [Buf(), Buf()]
    n1T = [ent(nc.sbuf_tensor(f"p3_n1T{k}", [128, NC * 128], BF16))
           for k in range(2)]
    n1Tb = [Buf(), Buf()]
    hbf_l = [ent(nc.sbuf_tensor(f"p3_h{k}", [128, 4096], BF16))
             for k in range(2)]
    hbfb_l = [Buf(), Buf()]
    n2hT_s = ent(nc.sbuf_tensor("p3_n2hT", [128, 32 * 128], BF16))
    n2hTb_s = Buf()
    ysb = c.tmpB; ysbb = c.tmpBb
    lno = c.tmpA; lnob = c.tmpAb

    m1col = stat2[:, 0:1]
    m2col = stat2[:, 8:9]

    def ln4k(x_ap, x_buf, out_ap, out_buf, k):
        stat, stb = c.stat[k], c.statb[k]
        s1 = stat[:, 0:1]; s2 = stat[:, 1:2]; rstd = stat[:, 2:3]
        negmr = stat[:, 3:4]; tmp = stat[:, 4:5]
        p1 = stat[:, 8:9]; p2 = stat[:, 9:10]
        tz = None
        for q in range(4):
            tsq = em.s(lambda e, q=q: e.activation(
                c.sq[k][:, 0:E], x_ap[:, q * E:(q + 1) * E], AF.Square,
                bias=0.0, scale=1.0, accum_out=(s2 if q == 0 else p2)),
                risky=True, reads=[x_buf], writes=[c.sqb[k]], writes_also=[stb],
                deps=([tz] if tz else []))
            trs = em.v(lambda e, q=q: e.reduce_sum(
                (s1 if q == 0 else p1), x_ap[:, q * E:(q + 1) * E], axis=AX.X),
                deps=[tsq], risky=True, reads=[x_buf], writes_also=[stb])
            if q > 0:
                t1_ = em.v(lambda e: e.tensor_tensor(s1, s1, p1, ALU.add),
                           deps=[trs], risky=True, writes_also=[stb])
                tz = em.v(lambda e: e.tensor_tensor(s2, s2, p2, ALU.add),
                          deps=[t1_, tsq], risky=True, writes_also=[stb])
            else:
                tz = trs
        t_a = em.v(lambda e: e.tensor_scalar(s1, s1, 1.0 / 4096.0, None, ALU.mult),
                   deps=[tz], risky=True, writes_also=[stb])
        t_b = em.v(lambda e: e.tensor_scalar(s2, s2, 1.0 / 4096.0, None, ALU.mult),
                   deps=[t_a], risky=True, writes_also=[stb])
        t_c = em.v(lambda e: e.tensor_tensor(tmp, s1, s1, ALU.mult),
                   deps=[t_b], risky=True, writes_also=[stb])
        t_d = em.v(lambda e: e.tensor_tensor(s2, s2, tmp, ALU.subtract),
                   deps=[t_c], risky=True, writes_also=[stb])
        t_ln = em.s(lambda e: e.activation(tmp, s2, AF.Ln, bias=c.eps5[:, 0:1],
                                           scale=1.0),
                    deps=[t_d], risky=True, reads=[c.constb], writes_also=[stb])
        t_ex = em.s(lambda e: e.activation(rstd, tmp, AF.Exp, bias=0.0,
                                           scale=-0.5),
                    deps=[t_ln], risky=True, writes_also=[stb])
        t_nm = em.v(lambda e: e.scalar_tensor_tensor(negmr, s1, -1.0, rstd,
                                                     ALU.mult, ALU.mult),
                    deps=[t_ex], risky=True, writes_also=[stb])
        return em.s(lambda e: e.activation(out_ap, x_ap, AF.Identity,
                                           bias=negmr, scale=rstd),
                    deps=[t_nm, t_ex], reads=[x_buf], writes=[out_buf])

    def emitA(i):
        k = i % 2
        hbf = hbf_l[k]; hbfb = hbfb_l[k]
        st = dict(sq=(c.sq[k][:, 0:E], c.sqb[k]), stat=(c.stat[k][:], c.statb[k]))
        em.dma("sync", lambda e, i=i, k=k: e.dma_start(
            out=c.xt[k][:], in_=Xin[i * 128:(i + 1) * 128, :]), writes=[c.xtb[k]])
        ln_compose(em, c, c.xt[k][:], c.xtb[k], n1bf[k][:], n1bfb[k], st, E)
        for cc in range(NC):
            transpose_128(em, c, n1bf[k][:, cc * 128:(cc + 1) * 128], [n1bfb[k]],
                          n1T[k][:, cc * 128:(cc + 1) * 128],
                          [n1Tb[k]], cc % 2,
                          deps=(n1Tb[k].wr() if cc == 0 else []), final=(cc == NC - 1))
        for jb in range(8):
            psH, psHb = (c.psC, c.psCb) if jb % 2 == 0 else (c.psD, c.psDb)
            th = []
            for cc in range(NC):
                th.append(em.t(lambda e, jb=jb, cc=cc, k=k, psH=psH: e.matmul(
                    psH[:, 0:512], n1T[k][:, cc * 128:(cc + 1) * 128],
                    t1sb[:, cc * 4096 + jb * 512:cc * 4096 + (jb + 1) * 512],
                    start=(cc == 0), stop=False),
                    reads=[n1Tb[k], t1b],
                    deps=(psHb.wr() if cc == 0 else [])))
            th.append(em.t(lambda e, jb=jb, psH=psH: e.matmul(
                psH[:, 0:512], c.ones_bf[0:1, :],
                bv1bf[0:1, jb * 512:(jb + 1) * 512], start=False, stop=True),
                reads=[bv1bfb, c.constb]))
            psHb.w = th[-1]; psHb.wl = []; psHb.r = []
            em.s(lambda e, jb=jb, psH=psH, hbf=hbf: e.activation(
                hbf[:, jb * 512:(jb + 1) * 512], psH[:, 0:512], AF.Gelu,
                bias=0.0, scale=m1col),
                reads=[psHb, stat2b],
                writes=([hbfb] if jb == 0 else []),
                writes_also=([hbfb] if jb > 0 else []),
                deps=(hbfb.wr() if jb == 0 else []))
        ln4k(hbf[:], hbfb, hbf[:], hbfb, k)

    def emitB(i):
        k = i % 2
        hbf = hbf_l[k]; hbfb = hbfb_l[k]
        n2hT = n2hT_s; n2hTb = n2hTb_s
        psY, psYb = (c.psA, c.psAb) if k == 0 else (c.psB, c.psBb)
        st = dict(sq=(c.sq[k][:, 0:E], c.sqb[k]), stat=(c.stat[k][:], c.statb[k]))
        for js in range(32):
            transpose_128(em, c, hbf[:, js * 128:(js + 1) * 128], [hbfb],
                          n2hT[:, js * 128:(js + 1) * 128],
                          [n2hTb], js % 2,
                          deps=(n2hTb.wr() if js == 0 else []), final=(js == 31))
        ty = []
        for js in range(32):
            for half in range(2):
                ty.append(em.t(lambda e, js=js, half=half, n2hT=n2hT,
                               psY=psY: e.matmul(
                    psY[:, half * 512:(half + 1) * 512],
                    n2hT[:, js * 128:(js + 1) * 128],
                    t2sb[:, js * E + half * 512:js * E + (half + 1) * 512],
                    start=(js == 0), stop=False),
                    reads=[n2hTb, t2b],
                    deps=(psYb.wr() if js == 0 and half == 0 else [])))
        for half in range(2):
            ty.append(em.t(lambda e, half=half, psY=psY: e.matmul(
                psY[:, half * 512:(half + 1) * 512], c.ones_bf[0:1, :],
                bv2bf[0:1, half * 512:(half + 1) * 512], start=False, stop=True),
                reads=[bv2bfb, c.constb]))
        psYb.w = ty[-1]; psYb.wl = []; psYb.r = []
        tev = em.s(lambda e, k=k, psY=psY: e.activation(
            ysb[k][:], psY[:], AF.Identity, bias=0.0, scale=m2col),
            reads=[psYb, stat2b], writes=[ysbb[k]])
        tlno = ln_to(em, c, ysb[k][:], ysbb[k], lno[k][:], lnob[k], st, E)
        em.v(lambda e, k=k: e.tensor_tensor(lno[k][:], lno[k][:],
                                            c.xt[k][:], ALU.add),
             deps=[tlno], reads=[c.xtb[k]], writes=[lnob[k]])
        em.dma("gpsimd", lambda e, i=i, k=k: e.dma_start(
            out=out[i * 128:(i + 1) * 128, :], in_=lno[k][:]), reads=[lnob[k]])

    for i in range(NT):
        emitA(i)
        if i > 0:
            emitB(i - 1)
    emitB(NT - 1)



def kernel(**inputs):
    x = np.asarray(inputs["L0_attn_x"], dtype=np.float32)
    prep = host_prep(inputs)
    nc = build(phases=PHASES)
    base = {k: np.ascontiguousarray(prep[k], dtype=np.float32) for k in HOST_KEYS}
    in_maps = []
    for b in range(8):
        m = dict(base)
        m["x"] = np.ascontiguousarray(x[b])
        in_maps.append(m)
    res = run_bass_kernel_spmd(nc, in_maps, core_ids=list(range(8)))
    return np.stack([res.results[b]["out"] for b in range(8)], axis=0)


# version 09 (final)
# speedup vs baseline: 1.1539x; 1.1539x over previous
import contextlib
import numpy as np
import concourse.bass as bass
import concourse.mybir as mybir
from concourse.bass_utils import run_bass_kernel_spmd

F32 = mybir.dt.float32
F32R = mybir.dt.float32r
PDT = mybir.dt.float32
BF16 = mybir.dt.bfloat16
AF = mybir.ActivationFunctionType
ALU = mybir.AluOpType
AX = mybir.AxisListType

S = 4096
E = 1024
NT = S // 128
NC = E // 128
NEG = -1.0e30
SCALE8 = 0.125
SCALE96 = float(1.0 / np.sqrt(96.0))

ENGINES = ("tensor", "vector", "scalar", "gpsimd", "sync")
QUEUES = ("sync", "scalar", "gpsimd")

PHASES = (0, 1, 2, 3)
DBG_X1 = False
DBG_U = False
P0_PARTS = "abc"
P0A_STEP = 99
P1_PARTS = "abcd"
P1A_STEP = 9


class Tok:
    __slots__ = ("engine", "idx")
    def __init__(self, engine, idx):
        self.engine = engine; self.idx = idx


class Buf:
    __slots__ = ("w", "wl", "r")
    def __init__(self):
        self.w = None; self.wl = []; self.r = []
    def rd(self):
        return ([self.w] if self.w is not None else []) + self.wl
    def wr(self):
        return self.rd() + self.r


class Em:
    NPOOL = 8

    def __init__(self):
        self.ops = {e: [] for e in ENGINES}
        self.barriers = []

    def _emit(self, engine, fn, deps, risky=False, is_dma=False,
              reads=(), writes=(), writes_also=()):
        dd = []
        for b in reads:
            dd += b.rd()
        for b in writes:
            dd += b.wr()
        dd += [d for d in deps if d is not None]
        rec = dict(fn=fn, deps=dd, marked=False, risky=risky,
                   is_dma=is_dma, inc_val=None, dma_idx=None)
        self.ops[engine].append(rec)
        tok = Tok(engine, len(self.ops[engine]) - 1)
        for b in reads:
            b.r.append(tok)
        for b in writes:
            b.w = tok; b.wl = []; b.r = []
        for b in writes_also:
            b.wl.append(tok)
        return tok

    def t(self, fn, deps=(), risky=False, reads=(), writes=(), writes_also=()):
        return self._emit("tensor", fn, list(deps), risky, False, reads, writes,
                          writes_also)
    def v(self, fn, deps=(), risky=False, reads=(), writes=(), writes_also=()):
        return self._emit("vector", fn, list(deps), risky, False, reads, writes,
                          writes_also)
    def s(self, fn, deps=(), risky=False, reads=(), writes=(), writes_also=()):
        return self._emit("scalar", fn, list(deps), risky, False, reads, writes,
                          writes_also)
    def g(self, fn, deps=(), risky=False, reads=(), writes=(), writes_also=()):
        return self._emit("gpsimd", fn, list(deps), risky, False, reads, writes,
                          writes_also)
    def dma(self, queue, fn, deps=(), reads=(), writes=(), writes_also=()):
        return self._emit(queue, fn, list(deps), False, True, reads, writes,
                          writes_also)

    def barrier(self):
        self.barriers.append({e: len(self.ops[e]) for e in ENGINES})

    def finalize(self, block, sems, dsems):
        for q in QUEUES:
            n = 0
            for rec in self.ops[q]:
                if rec["is_dma"]:
                    rec["dma_idx"] = n; n += 1
        for e in ENGINES:
            for rec in self.ops[e]:
                for d in rec["deps"]:
                    dep = self.ops[d.engine][d.idx]
                    if not dep["is_dma"]:
                        dep["marked"] = True
        for bar in self.barriers:
            for e in ENGINES:
                if bar[e] > 0:
                    self.ops[e][bar[e] - 1]["marked"] = True
        for e in ENGINES:
            cnt = 0
            for rec in self.ops[e]:
                if (not rec["is_dma"]) and rec["marked"]:
                    cnt += 1; rec["inc_val"] = cnt
            assert cnt < 60000, (e, cnt)
            ndma = sum(1 for r in self.ops[e] if r["is_dma"])
            assert (ndma // self.NPOOL + 1) * 16 < 60000, (e, ndma)
        bar_targets = []
        for bar in self.barriers:
            tg = {}
            for e in ENGINES:
                mc = dc = 0
                for rec in self.ops[e][:bar[e]]:
                    if rec["is_dma"]:
                        dc += 1
                    elif rec["marked"]:
                        mc += 1
                tg[e] = (mc, dc)
            bar_targets.append(tg)

        def make_body(eng_name):
            def body(eng):
                wm = {}
                bars = [(self.barriers[i][eng_name], bar_targets[i])
                        for i in range(len(self.barriers))]
                bi = 0
                for i, rec in enumerate(self.ops[eng_name]):
                    while bi < len(bars) and bars[bi][0] <= i:
                        for oe in ENGINES:
                            mc, dc = bars[bi][1][oe]
                            if (oe != eng_name and mc > 0
                                    and wm.get(("c", oe), 0) < mc):
                                eng.wait_ge(sems[oe], mc); wm[("c", oe)] = mc
                            if oe in QUEUES and dc > 0:
                                for sl in range(self.NPOOL):
                                    nsl = (dc - sl + self.NPOOL - 1) // self.NPOOL
                                    if nsl > 0 and wm.get(("d", oe, sl), 0) < nsl:
                                        eng.wait_ge(dsems[oe][sl], 16 * nsl)
                                        wm[("d", oe, sl)] = nsl
                        bi += 1
                    for d in rec["deps"]:
                        dep = self.ops[d.engine][d.idx]
                        if dep["is_dma"]:
                            sl = dep["dma_idx"] % self.NPOOL
                            tgt = dep["dma_idx"] // self.NPOOL + 1
                            if wm.get(("d", d.engine, sl), 0) < tgt:
                                eng.wait_ge(dsems[d.engine][sl], 16 * tgt)
                                wm[("d", d.engine, sl)] = tgt
                        else:
                            if d.engine == eng_name == "tensor":
                                continue
                            tgt = dep["inc_val"]
                            if wm.get(("c", d.engine), 0) < tgt:
                                eng.wait_ge(sems[d.engine], tgt)
                                wm[("c", d.engine)] = tgt
                    if rec["is_dma"] and rec["dma_idx"] >= self.NPOOL:
                        sl = rec["dma_idx"] % self.NPOOL
                        tgt = rec["dma_idx"] // self.NPOOL
                        if wm.get(("d", eng_name, sl), 0) < tgt:
                            eng.wait_ge(dsems[eng_name][sl], 16 * tgt)
                            wm[("d", eng_name, sl)] = tgt
                    inst = rec["fn"](eng)
                    if rec["is_dma"]:
                        inst.then_inc(dsems[eng_name][rec["dma_idx"] % self.NPOOL], 16)
                    elif rec["marked"]:
                        inst.then_inc(sems[eng_name], 1)
                if eng_name == "gpsimd":
                    for q in QUEUES:
                        tot = sum(1 for r in self.ops[q] if r["is_dma"])
                        for sl in range(self.NPOOL):
                            nsl = (tot - sl + self.NPOOL - 1) // self.NPOOL
                            if nsl > 0 and wm.get(("d", q, sl), 0) < nsl:
                                eng.wait_ge(dsems[q][sl], 16 * nsl)
            return body

        block.tensor(make_body("tensor"))
        block.vector(make_body("vector"))
        block.scalar(make_body("scalar"))
        block.gpsimd(make_body("gpsimd"))
        block.sync(make_body("sync"))


class Ctx:
    pass



def ln_to(em, c, x_ap, x_buf, out_ap, out_buf, st, n, eps6=False, rms=False):
    sq, sqb = st["sq"]
    stat, stb = st["stat"]
    s1 = stat[:, 0:1]; s2 = stat[:, 1:2]; rstd = stat[:, 2:3]
    negmr = stat[:, 3:4]; tmp = stat[:, 4:5]
    t_sq = em.s(lambda e: e.activation(sq, x_ap, AF.Square, bias=0.0,
                                       scale=1.0, accum_out=s2),
                risky=True, reads=[x_buf], writes=[sqb, stb])
    if not rms:
        t_r = em.v(lambda e: e.reduce_sum(s1, x_ap, axis=AX.X),
                   deps=[t_sq], risky=True, reads=[x_buf], writes=[stb])
        t_a = em.v(lambda e: e.tensor_scalar(s1, s1, 1.0 / n, None, ALU.mult),
                   deps=[t_r], risky=True, writes=[stb])
        t_b = em.v(lambda e: e.tensor_scalar(s2, s2, 1.0 / n, None, ALU.mult),
                   deps=[t_sq, t_a], risky=True, writes=[stb])
        t_c = em.v(lambda e: e.tensor_tensor(tmp, s1, s1, ALU.mult),
                   deps=[t_b], risky=True, writes=[stb])
        last = em.v(lambda e: e.tensor_tensor(s2, s2, tmp, ALU.subtract),
                    deps=[t_c], risky=True, writes=[stb])
    else:
        last = em.v(lambda e: e.tensor_scalar(s2, s2, 1.0 / n, None, ALU.mult),
                    deps=[t_sq], risky=True, writes=[stb])
    eps_ap = c.eps6[:, 0:1] if eps6 else c.eps5[:, 0:1]
    t_ln = em.s(lambda e: e.activation(tmp, s2, AF.Ln, bias=eps_ap, scale=1.0),
                deps=[last], risky=True, reads=[c.constb], writes=[stb])
    t_ex = em.s(lambda e: e.activation(rstd, tmp, AF.Exp, bias=0.0, scale=-0.5),
                deps=[t_ln], risky=True, writes=[stb])
    if not rms:
        t_nm = em.v(lambda e: e.scalar_tensor_tensor(negmr, s1, -1.0, rstd,
                                                     ALU.mult, ALU.mult),
                    deps=[t_ex], risky=True, writes=[stb])
        t_out = em.s(lambda e: e.activation(out_ap, x_ap, AF.Identity,
                                            bias=negmr, scale=rstd),
                     deps=[t_nm, t_ex], reads=[x_buf], writes=[out_buf])
    else:
        t_out = em.s(lambda e: e.activation(out_ap, x_ap, AF.Identity,
                                            bias=0.0, scale=rstd),
                     deps=[t_ex], reads=[x_buf], writes=[out_buf])
    return t_out



def ln_compose(em, c, x_ap, x_buf, out_ap, out_buf, st, n, eps6_2=False):
    sq, sqb = st["sq"]
    stat, stb = st["stat"]
    s1 = stat[:, 0:1]; s2 = stat[:, 1:2]; rstd = stat[:, 2:3]
    negmr = stat[:, 3:4]; tmp = stat[:, 4:5]; v2 = stat[:, 5:6]
    r2 = stat[:, 6:7]; rr = stat[:, 7:8]
    t_sq = em.s(lambda e: e.activation(sq, x_ap, AF.Square, bias=0.0,
                                       scale=1.0, accum_out=s2),
                risky=True, reads=[x_buf], writes=[sqb, stb])
    t_r = em.v(lambda e: e.reduce_sum(s1, x_ap, axis=AX.X),
               deps=[t_sq], risky=True, reads=[x_buf], writes=[stb])
    t_a = em.v(lambda e: e.tensor_scalar(s1, s1, 1.0 / n, None, ALU.mult),
               deps=[t_r], risky=True, writes=[stb])
    t_b = em.v(lambda e: e.tensor_scalar(s2, s2, 1.0 / n, None, ALU.mult),
               deps=[t_sq, t_a], risky=True, writes=[stb])
    t_c = em.v(lambda e: e.tensor_tensor(tmp, s1, s1, ALU.mult),
               deps=[t_b], risky=True, writes=[stb])
    t_d = em.v(lambda e: e.tensor_tensor(s2, s2, tmp, ALU.subtract),
               deps=[t_c], risky=True, writes=[stb])
    t_ln = em.s(lambda e: e.activation(tmp, s2, AF.Ln, bias=c.eps5[:, 0:1],
                                       scale=1.0),
                deps=[t_d], risky=True, reads=[c.constb], writes=[stb])
    t_ex = em.s(lambda e: e.activation(rstd, tmp, AF.Exp, bias=0.0, scale=-0.5),
                deps=[t_ln], risky=True, writes=[stb])
    t_v1 = em.v(lambda e: e.tensor_tensor(v2, s2, rstd, ALU.mult),
                deps=[t_ex], risky=True, writes=[stb])
    t_v2 = em.v(lambda e: e.tensor_tensor(v2, v2, rstd, ALU.mult),
                deps=[t_v1], risky=True, writes=[stb])
    eps_ap = c.eps6[:, 0:1] if eps6_2 else c.eps5[:, 0:1]
    t_l2 = em.s(lambda e: e.activation(tmp, v2, AF.Ln, bias=eps_ap, scale=1.0),
                deps=[t_v2], risky=True, reads=[c.constb], writes=[stb])
    t_e2 = em.s(lambda e: e.activation(r2, tmp, AF.Exp, bias=0.0, scale=-0.5),
                deps=[t_l2], risky=True, writes=[stb])
    t_rr = em.v(lambda e: e.tensor_tensor(rr, rstd, r2, ALU.mult),
                deps=[t_e2], risky=True, writes=[stb])
    t_nm = em.v(lambda e: e.scalar_tensor_tensor(negmr, s1, -1.0, rr,
                                                 ALU.mult, ALU.mult),
                deps=[t_rr], risky=True, writes=[stb])
    t_out = em.s(lambda e: e.activation(out_ap, x_ap, AF.Identity,
                                        bias=negmr, scale=rr),
                 deps=[t_nm, t_rr], reads=[x_buf], writes=[out_buf])
    return t_out


def transpose_128(em, c, src_ap, src_bufs, dst_ap, dst_bufs, pt_sel, deps=(),
                  f32=False, split_dst=None, final=True):
    pt, ptb = (c.ptA, c.ptAb) if pt_sel % 2 == 0 else (c.ptB, c.ptBb)
    pt_ap = pt[:, 0:128]
    m = src_ap.shape[1]
    if f32:
        t1 = em.t(lambda e: e.transpose(pt_ap[0:m, :], src_ap, c.ident_f32[:]),
                  deps=deps, reads=list(src_bufs) + [c.constb], writes=[ptb])
    else:
        t1 = em.t(lambda e: e.matmul(pt_ap[0:m, :], src_ap, c.ident_bf[:],
                                     start=True, stop=True),
                  deps=deps, reads=list(src_bufs) + [c.constb], writes=[ptb])
    dst_dt = getattr(getattr(dst_ap if split_dst is None else split_dst[0][0],
                             "tensor", None), "dtype", None)
    if dst_dt == F32R:
        use_s = True
    else:
        use_s = pt_sel % 2 != 0
    ev = em.s if use_s else em.v
    if split_dst is not None:
        out_toks = []
        for u, (dtens, i, dbuf) in enumerate(split_dst):
            out_toks.append(ev(lambda e, u=u, dtens=dtens, i=i: e.copy(
                dtens[:, i * 128:(i + 1) * 128], pt_ap[u * 64:(u + 1) * 64, :])
                if use_s else e.tensor_copy(
                dtens[:, i * 128:(i + 1) * 128], pt_ap[u * 64:(u + 1) * 64, :]),
                deps=[t1], reads=[ptb], writes=[dbuf]))
        return out_toks[-1]
    t2 = ev(lambda e: e.copy(dst_ap, pt_ap[0:m, :]) if use_s
            else e.tensor_copy(dst_ap, pt_ap[0:m, :]),
            deps=[t1], reads=[ptb], writes_also=list(dst_bufs))
    return t2


def rope_all(em, c, ps_ap, ps_buf, nh, cs_ap, sc_ap, cs_buf, sc_buf,
             out_ap, out_buf, tmpA, tmpAb, tmpB, tmpBb, scale, extra_deps=()):
    w = nh * 64
    tA = em.v(lambda e: e.tensor_tensor(tmpA[:, 0:w], ps_ap, cs_ap, ALU.mult),
              deps=list(extra_deps), reads=[ps_buf, cs_buf], writes=[tmpAb])
    tB = em.v(lambda e: e.tensor_tensor(tmpB[:, 0:w], ps_ap, sc_ap, ALU.mult),
              deps=list(extra_deps), reads=[ps_buf, sc_buf], writes=[tmpBb])
    vA = tmpA[:, 0:w].rearrange("p (h x) -> p h x", h=nh)
    vB = tmpB[:, 0:w].rearrange("p (h x) -> p h x", h=nh)
    t1 = em.v(lambda e: e.tensor_tensor(vA[:, :, 0:32], vA[:, :, 0:32],
                                        vA[:, :, 32:64], ALU.subtract),
              deps=[tA], writes=[tmpAb])
    t2 = em.v(lambda e: e.tensor_tensor(vA[:, :, 32:64], vB[:, :, 0:32],
                                        vB[:, :, 32:64], ALU.add),
              deps=[tB, t1], reads=[tmpBb], writes=[tmpAb])
    if out_ap is None:
        return t2
    tc = em.s(lambda e: e.activation(out_ap, tmpA[:, 0:w], AF.Copy,
                                     bias=0.0, scale=scale),
              deps=[t1, t2], reads=[tmpAb], writes=[out_buf])
    return tc


def replicate_row(em, c, row_ap, row_buf, out_ap, out_buf, n, ps, psb,
                  deps=()):
    toks = []
    for o in range(0, n, 512):
        w = min(512, n - o)
        toks.append(em.t(lambda e, o=o, w=w: e.matmul(
            ps[:, o:o + w], c.ones_f32[0:1, :], row_ap[0:1, o:o + w],
            start=True, stop=True),
            deps=list(deps) + (psb.wr() if o == 0 else []),
            reads=[row_buf, c.constb]))
    psb.w = toks[-1]; psb.wl = []; psb.r = []
    t2 = em.s(lambda e: e.copy(out_ap, ps[:, 0:n]), deps=toks, reads=[psb],
              writes=[out_buf])
    return t2



def _colform(v, p=128):
    n = np.asarray(v).shape[0]
    return np.ascontiguousarray(np.asarray(v, np.float32).reshape(n // p, p).T)


def host_prep(inputs):
    d = {}
    g = lambda k: np.asarray(inputs[k], dtype=np.float32)

    def ef_perm(nheads, hd=64):
        p = []
        for h in range(nheads):
            p += [h * hd + i for i in range(0, hd, 2)]
            p += [h * hd + i for i in range(1, hd, 2)]
        return np.array(p)

    d["wqa"] = g("L0_attn_Wqa")
    d["wqb"] = np.ascontiguousarray(g("L0_attn_Wqb")[:, ef_perm(16)])
    d["wkva"] = g("L0_attn_Wkva")
    wkvb = g("L0_attn_Wkvb")
    cols = []
    for h in range(4):
        cols += [128 * h + 4 * i for i in range(32)]
        cols += [128 * h + 4 * i + 2 for i in range(32)]
    for h in range(4):
        cols += [128 * h + 2 * dd + 1 for dd in range(64)]
    d["wkvb"] = np.ascontiguousarray(wkvb[:, np.array(cols)])
    d["wout"] = g("L0_attn_Wout")
    d["bout_row"] = np.ascontiguousarray(g("L0_attn_bout").reshape(1, E))

    d["wqT0f"] = np.ascontiguousarray(g("L0_ffn_Wq").T)
    d["wdkvT"] = np.ascontiguousarray(g("L0_ffn_Wdkv").T)
    d["wukT"] = np.ascontiguousarray(g("L0_ffn_Wuk").T)
    d["wuvT"] = np.ascontiguousarray(g("L0_ffn_Wuv").T)
    d["woT0f"] = np.ascontiguousarray(g("L0_ffn_Wo").T)
    d["w1ncol"] = _colform(g("L0_ffn_norm1_w"))
    d["w2ncol"] = _colform(g("L0_ffn_norm2_w"))
    d["lngcol"] = _colform(g("L0_ffn_ln_g"))
    d["lnbcol"] = _colform(g("L0_ffn_ln_b"))
    d["gwT"] = np.ascontiguousarray(g("L0_ffn_gate_w").T)
    d["gbrow"] = np.ascontiguousarray(g("L0_ffn_gate_b").reshape(1, 4))
    d["w1T"] = np.ascontiguousarray(g("L0_ffn_W1").transpose(0, 2, 1))
    d["w2T"] = np.ascontiguousarray(g("L0_ffn_W2").transpose(0, 2, 1))
    d["b1col"] = np.stack([_colform(g("L0_ffn_b1")[e]) for e in range(4)])
    d["b2row"] = np.ascontiguousarray(g("L0_ffn_b2").reshape(4, 1, E))

    d["wq2"] = g("L1_attn_Wq"); d["wk2"] = g("L1_attn_Wk")
    d["wv2"] = g("L1_attn_Wv"); d["wr2"] = g("L1_attn_Wr")
    d["wo2"] = g("L1_attn_Wo")
    d["bq2col"] = _colform(g("L1_attn_bq")); d["bk2col"] = _colform(g("L1_attn_bk"))
    d["br2col"] = np.ascontiguousarray(g("L1_attn_br").reshape(8, 64).T)
    d["bv2row"] = np.ascontiguousarray(g("L1_attn_bv").reshape(1, E))
    d["bo2row"] = np.ascontiguousarray(g("L1_attn_bo").reshape(1, E))

    d["l1w1T"] = np.ascontiguousarray(g("L1_ffn_w1").T)
    d["l1w2T"] = np.ascontiguousarray(g("L1_ffn_w2").T)
    d["l1g1col"] = _colform(g("L1_ffn_ln1_g"))
    d["l1b1lncol"] = _colform(g("L1_ffn_ln1_b"))
    d["l1b1row"] = np.ascontiguousarray(g("L1_ffn_b1").reshape(1, 4096))
    d["l1g2col"] = _colform(g("L1_ffn_ln2_g"))
    d["l1b2lncol"] = _colform(g("L1_ffn_ln2_b"))
    d["l1b2row"] = np.ascontiguousarray(g("L1_ffn_b2").reshape(1, E))

    t = np.arange(S, dtype=np.float64)
    inv = 10000.0 ** (-np.arange(32, dtype=np.float64) / 32.0)
    ang = np.outer(t, inv)
    cos = np.cos(ang).astype(np.float32); sin = np.sin(ang).astype(np.float32)
    cs = np.concatenate([cos, sin], axis=1)
    sc = np.concatenate([sin, cos], axis=1)
    d["cs16"] = np.ascontiguousarray(np.tile(cs, (1, 16)))
    d["sc16"] = np.ascontiguousarray(np.tile(sc, (1, 16)))
    msk = np.zeros((4, 128, 512), np.float32)
    for a in range(4):
        kk = np.arange(128)[:, None] + 128 * a
        qq = np.arange(512)[None, :]
        msk[a] = np.where(kk > qq, NEG, 0.0)
    d["masks"] = np.ascontiguousarray(msk.reshape(4 * 128, 512))
    d["ident"] = np.eye(128, dtype=np.float32)
    return d


HOST_KEYS = ["wqa", "wqb", "wkva", "wkvb", "wout", "bout_row",
             "wqT0f", "wdkvT", "wukT", "wuvT", "woT0f", "w1ncol", "w2ncol",
             "lngcol", "lnbcol", "gwT", "gbrow", "w1T", "w2T", "b1col", "b2row",
             "wq2", "wk2", "wv2", "wr2", "wo2", "bq2col", "bk2col", "br2col",
             "bv2row", "bo2row",
             "l1w1T", "l1w2T", "l1g1col", "l1b1lncol", "l1b1row",
             "l1g2col", "l1b2lncol", "l1b2row",
             "cs16", "sc16", "masks", "ident"]



def build(phases=None):
    if phases is None:
        phases = PHASES
    nc = bass.Bass("TRN2", target_bir_lowering=False, debug=False)
    c = Ctx(); c.nc = nc
    em = Em(); c.em = em

    def din(name, shape, dt=F32):
        return nc.dram_tensor(name, list(shape), dt, kind="ExternalInput").ap()

    x_in = din("x", [S, E])
    W = {}
    shapes = dict(
        wqa=[E, 64], wqb=[64, E], wkva=[E, 32], wkvb=[32, 512], wout=[E, E],
        bout_row=[1, E],
        wqT0f=[E, E], wdkvT=[E, 512], wukT=[512, E], wuvT=[512, E],
        woT0f=[E, E], w1ncol=[128, 8], w2ncol=[128, 8], lngcol=[128, 4],
        lnbcol=[128, 4], gwT=[E, 4], gbrow=[1, 4], w1T=[4, E, 4096],
        w2T=[4, 4096, E], b1col=[4, 128, 32], b2row=[4, 1, E],
        wq2=[E, E], wk2=[E, E], wv2=[E, E], wr2=[E, 512], wo2=[E, E],
        bq2col=[128, 8], bk2col=[128, 8], br2col=[64, 8], bv2row=[1, E],
        bo2row=[1, E],
        l1w1T=[E, 4096], l1w2T=[4096, E], l1g1col=[128, 8], l1b1lncol=[128, 8],
        l1b1row=[1, 4096], l1g2col=[128, 32], l1b2lncol=[128, 32],
        l1b2row=[1, E],
        cs16=[S, E], sc16=[S, E], masks=[4 * 128, 512], ident=[128, 128],
    )
    for kk, sh in shapes.items():
        W[kk] = din(kk, sh)

    out = nc.dram_tensor("out", [S, E], F32, kind="ExternalOutput").ap()

    def dscratch(name, shape, dt):
        return nc.dram_tensor(name, list(shape), dt, kind="Internal").ap()

    X1 = dscratch("X1", [S, E], F32)
    X2 = dscratch("X2", [S, E], F32)
    X3 = dscratch("X3", [S, E], F32)
    QTD = dscratch("QTD", [E, S], PDT)
    KTD = dscratch("KTD", [E, S], PDT)
    OTD = dscratch("OTD", [E, S], PDT)
    U = dscratch("U", [S, E], F32)
    N2 = dscratch("N2", [E, S], BF16)
    XN = dscratch("XN", [S, E], BF16)
    YACC = dscratch("YACC", [S, E], F32)
    VTD = dscratch("VTD", [S, 16 * 65], PDT)
    OTD2 = dscratch("OTD2", [E, S], BF16)
    HTD = dscratch("HTD", [NT * 32 * 128, 128], BF16)

    with contextlib.ExitStack() as stack:
        ent = stack.enter_context
        c.ident_bf = ent(nc.sbuf_tensor("ident_bf", [128, 128], BF16))
        c.ident_f32 = ent(nc.sbuf_tensor("ident_f32", [128, 128], F32))
        c.ones_bf = ent(nc.sbuf_tensor("ones_bf", [128, 128], BF16))
        c.ones_f32 = ent(nc.sbuf_tensor("ones_f32", [128, 128], F32))
        c.ones_f32r = ent(nc.sbuf_tensor("ones_f32r", [1, 128],
                                         PDT if PDT != F32 else F32R))
        c.eps5 = ent(nc.sbuf_tensor("eps5", [128, 1], F32))
        c.eps6 = ent(nc.sbuf_tensor("eps6", [128, 1], F32))

        c.psA = ent(nc.psum_tensor("psA", [128, 1024], F32)); c.psAb = Buf()
        c.psB = ent(nc.psum_tensor("psB", [128, 1024], F32)); c.psBb = Buf()
        c.psC = ent(nc.psum_tensor("psC", [128, 512], F32)); c.psCb = Buf()
        c.psD = ent(nc.psum_tensor("psD", [128, 512], F32)); c.psDb = Buf()
        c.ptA = ent(nc.psum_tensor("ptA", [128, 512], F32)); c.ptAb = Buf()
        c.ptB = ent(nc.psum_tensor("ptB", [128, 512], F32)); c.ptBb = Buf()
        c.psDT = [c.ptAb, c.ptBb]
        c.psAh = [Buf(), Buf()]
        c.psBh = [Buf(), Buf()]

        cb = Buf()
        em.dma("sync", lambda e: e.dma_start(out=c.ident_f32[:], in_=W["ident"]),
               writes=[cb])
        em.v(lambda e: e.tensor_copy(c.ident_bf[:], c.ident_f32[:]), reads=[cb],
             writes_also=[cb])
        em.v(lambda e: e.memset(c.ones_bf[:], 1.0), writes_also=[cb])
        em.v(lambda e: e.memset(c.ones_f32[:], 1.0), writes_also=[cb])
        em.s(lambda e: e.copy(c.ones_f32r[:], c.ones_f32[0:1, :]), reads=[cb],
             writes_also=[cb])
        c.ones_pdt = c.ones_f32 if PDT == F32 else c.ones_f32r
        em.v(lambda e: e.memset(c.eps5[:], 1e-5), writes_also=[cb])
        em.v(lambda e: e.memset(c.eps6[:], 1e-6), writes_also=[cb])
        c.constb = cb

        c.sq = [ent(nc.sbuf_tensor(f"sq1k{k}", [128, 1024], F32))
                for k in range(2)]
        c.sqb = [Buf(), Buf()]
        c.stat = [ent(nc.sbuf_tensor(f"stat{k}", [128, 12], F32)) for k in range(2)]
        c.statb = [Buf(), Buf()]
        c.xt = [ent(nc.sbuf_tensor(f"xt{k}", [128, E], F32)) for k in range(2)]
        c.xtb = [Buf(), Buf()]
        c.tmpA = [ent(nc.sbuf_tensor(f"tmpA{k}", [128, E], F32)) for k in range(2)]
        c.tmpAb = [Buf(), Buf()]
        c.tmpB = [ent(nc.sbuf_tensor(f"tmpB{k}", [128, E], F32)) for k in range(2)]
        c.tmpBb = [Buf(), Buf()]
        c.W = W

        if 0 in phases:
            with contextlib.ExitStack() as pstack:
                phase0(c, pstack, W, x_in, X1, QTD, KTD, VTD, OTD,
                       out_final=(out if max(phases) == 0 else None))
            em.barrier()
        if DBG_X1:
            dbg = nc.dram_tensor("dbg", [S, E], F32, kind="ExternalOutput").ap()
            em.dma("gpsimd", lambda e: e.dma_start(out=dbg, in_=X1))
            em.barrier()
        if 1 in phases:
            xin1 = x_in if 0 not in phases else X1
            with contextlib.ExitStack() as pstack:
                phase1(c, pstack, W, xin1, X2, QTD, KTD, OTD, U, N2, YACC, VTD,
                       HTD, out_final=(out if max(phases) == 1 else None))
            em.barrier()
            if DBG_U:
                dbg = nc.dram_tensor("dbg", [S, E], F32, kind="ExternalOutput").ap()
                em.dma("gpsimd", lambda e: e.dma_start(out=dbg, in_=U))
                dbg2 = nc.dram_tensor("dbg2", [128, NT * 4], F32,
                                      kind="ExternalOutput").ap()
                em.dma("gpsimd", lambda e: e.dma_start(out=dbg2, in_=c.MSK[:]),
                       reads=[c.MSKb])
                em.barrier()
        if 2 in phases:
            xin2 = x_in if 1 not in phases else X2
            with contextlib.ExitStack() as pstack:
                phase2(c, pstack, W, xin2, X3, XN, OTD2,
                       out_final=(out if max(phases) == 2 else None))
            em.barrier()
        if 3 in phases:
            xin3 = x_in if 2 not in phases else X3
            with contextlib.ExitStack() as pstack:
                phase3(c, pstack, W, xin3, out)
            em.barrier()

        sems = {}; dsems = {}
        for e in ENGINES:
            sems[e] = ent(nc.semaphore("sem_" + e))
        for q in QUEUES:
            dsems[q] = [ent(nc.semaphore(f"dsem_{q}{j}")) for j in range(Em.NPOOL)]
        blk = ent(nc.Block())
        em.finalize(blk, sems, dsems)
    return nc



def vtm_war(b):
    return b.wr()


def stream_tables(em, c, i, k):
    cs_dst = c.cs_t[k][:]
    sc_dst = c.sc_t[k][:]
    cs_src = c.W["cs16"][i * 128:(i + 1) * 128, :]
    sc_src = c.W["sc16"][i * 128:(i + 1) * 128, :]
    em.dma("sync", lambda e, d=cs_dst, s=cs_src: e.dma_start(out=d, in_=s),
        writes=[c.cs_tb[k]])
    em.dma("sync", lambda e, d=sc_dst, s=sc_src: e.dma_start(out=d, in_=s),
        writes=[c.sc_tb[k]])


def qts_store(em, c, qts, qtsb, tgt, i, queue="scalar"):
    em.dma(queue, lambda e, i=i: e.dma_start(
        out=tgt.rearrange("(cc p) t -> p cc t", p=128)[:, :, i * 128:(i + 1) * 128],
        in_=qts[:].rearrange("p (cc t) -> p cc t", cc=NC)),
        reads=[qtsb])


def flash_head(c, em, h, qb, kq, KT_ap, KT_b, V_ap, V_b, QT_ap, QT_b,
               cfg, OTD, causal, Kd):
    Rc = lambda ap: ap
    P_sb = cfg["P_sb"]; P_sbb = cfg["P_sbb"]
    den = cfg["den"]; denb = cfg["denb"]
    rcrep = cfg["rcrep"]; rcrepb = cfg["rcrepb"]
    ot = cfg["ot"][kq]; otb = cfg["otb"][kq]
    masks_sb = cfg["masks_sb"]; masksb = cfg["masksb"]

    njt = 4 * (qb + 1) if causal else NT
    slots = [
        [(c.psA[:, 0:512], c.psAh[0]), (c.psA[:, 512:1024], c.psAh[1])],
        [(c.psB[:, 0:512], c.psBh[0]), (c.psB[:, 512:1024], c.psBh[1])],
        [(c.ptA[:, 0:512], c.ptAb), (c.ptB[:, 0:512], c.ptBb)],
    ]
    pairs = [[j for j in (jp, jp + 1) if j < njt] for jp in range(0, njt, 2)]
    tes = {}

    def emit_scores(n):
        sl = n % 3
        for u, j in enumerate(pairs[n]):
            ap_u, buf_u = slots[sl][u]
            pb = P_sbb[sl][u]
            mmu = em.t(lambda e, j=j, ap_u=ap_u: e.matmul(
                ap_u, Rc(KT_ap[0:Kd, j * 128:(j + 1) * 128]), Rc(QT_ap[0:Kd, :]),
                start=True, stop=True),
                reads=[KT_b, QT_b], deps=buf_u.wr())
            buf_u.w = mmu; buf_u.wl = []; buf_u.r = []
            toks = [mmu]
            if causal:
                a = j - 4 * qb
                if 0 <= a <= 3:
                    toks.append(em.v(lambda e, a=a, ap_u=ap_u: e.tensor_tensor(
                        ap_u, ap_u, masks_sb[:, a * 512:(a + 1) * 512], ALU.add),
                        deps=[mmu], reads=[masksb], writes_also=[buf_u]))
            te = em.s(lambda e, ap_u=ap_u, sl=sl, u=u: e.activation(
                P_sb[sl][:, u * 512:(u + 1) * 512], ap_u, AF.Exp, bias=0.0,
                scale=1.0),
                deps=toks + pb.wr(), reads=[buf_u])
            pb.w = te; pb.wl = []; pb.r = []
            tes[(n, u)] = te

    pv_toks = []
    DEPTH = 2
    for n in range(min(DEPTH, len(pairs))):
        emit_scores(n)
    if cfg.get("pending_tail"):
        cfg["pending_tail"]()
        cfg["pending_tail"] = None
    for n, pair in enumerate(pairs):
        sl = n % 3
        for u, j in enumerate(pair):
            pb = P_sbb[sl][u]
            pv_toks.append(em.t(lambda e, j=j, sl=sl, u=u: e.matmul(
                c.psC[0:65, 0:512], Rc(V_ap[:, j * 65:(j + 1) * 65]),
                Rc(P_sb[sl][:, u * 512:(u + 1) * 512]),
                start=(j == 0), stop=(j == njt - 1)),
                deps=[tes[(n, u)]] + (c.psCb.wr() if j == 0 else []),
                reads=[V_b, pb]))
        if n + DEPTH < len(pairs):
            emit_scores(n + DEPTH)
    c.psCb.w = pv_toks[-1]; c.psCb.wl = []; c.psCb.r = []
    t1 = em.s(lambda e: e.copy(den[:], c.psC[64:65, 0:512]),
              reads=[c.psCb], writes=[denb])
    t2 = em.v(lambda e: e.reciprocal(den[:], den[:]), deps=[t1], risky=True,
              writes=[denb])

    def tail():
        t3 = em.t(lambda e: e.matmul(c.psD[0:64, 0:512], c.ones_f32[0:1, 0:64],
                                     den[0:1, :], start=True, stop=True),
                  deps=[t2] + c.psDb.wr(), reads=[denb, c.constb])
        c.psDb.w = t3; c.psDb.wl = []; c.psDb.r = []
        t4 = em.s(lambda e: e.copy(rcrep[:], c.psD[0:64, 0:512]), deps=[t3],
                  reads=[c.psDb], writes=[rcrepb])
        if cfg.get("f32") and PDT != F32:
            t5a = em.v(lambda e: e.tensor_tensor(rcrep[:], c.psC[0:64, 0:512],
                                                 rcrep[:], ALU.mult),
                       deps=[t4], reads=[c.psCb], writes_also=[rcrepb])
            t5 = em.s(lambda e: e.copy(ot[:], rcrep[:]),
                      deps=[t5a], reads=[rcrepb], writes=[otb])
        else:
            t5 = em.v(lambda e: e.tensor_tensor(ot[:], c.psC[0:64, 0:512],
                                                rcrep[:], ALU.mult),
                      deps=[t4], reads=[c.psCb, rcrepb], writes=[otb])
        em.dma("scalar", lambda e, h=h, qb=qb: e.dma_start(
            out=OTD[h * 64:(h + 1) * 64, qb * 512:(qb + 1) * 512], in_=ot[:]),
            reads=[otb])

    cfg["pending_tail"] = tail


def flash_fork(c):
    for hb, cb in ((c.psAh, c.psAb), (c.psBh, c.psBb)):
        for b in hb:
            b.w = None; b.wl = list(cb.rd()); b.r = list(cb.r)


def flash_merge(c):
    for hb, cb in ((c.psAh, c.psAb), (c.psBh, c.psBb)):
        toks = []
        rs = []
        for b in hb:
            toks += b.rd()
            rs += b.r
            b.w = None; b.wl = []; b.r = []
        cb.w = None; cb.wl = toks; cb.r = rs


def proj_residual(c, em, pstack, OTD, x_src, x_dst, wsb, wsbb, bias_row,
                  bias_rowb, tag, f32=False):
    ent = pstack.enter_context
    nc = c.nc
    dt_ = PDT if f32 else BF16
    Rc = lambda ap: ap
    ones_row = c.ones_pdt if f32 else c.ones_bf
    otc = [ent(nc.sbuf_tensor(f"{tag}_otc{k}", [128, NC * 128], dt_))
           for k in range(2)]
    otcb = [Buf(), Buf()]
    ysb = [ent(nc.sbuf_tensor(f"{tag}_ysb{k}", [128, E], F32)) for k in range(2)]
    ysbb = [Buf(), Buf()]
    lnb = [ent(nc.sbuf_tensor(f"{tag}_lnb{k}", [128, E], F32)) for k in range(2)]
    lnbb = [Buf(), Buf()]
    for i in range(NT):
        k = i % 2
        st = dict(sq=(c.sq[k][:, 0:E], c.sqb[k]), stat=(c.stat[k][:], c.statb[k]))
        em.dma("sync", lambda e, i=i, k=k: e.dma_start(
            out=c.xt[k][:], in_=x_src[i * 128:(i + 1) * 128, :]),
            writes=[c.xtb[k]])
        em.dma("scalar", lambda e, i=i, k=k: e.dma_start(
            out=otc[k][:].rearrange("p (cc t) -> p cc t", cc=NC),
            in_=OTD.rearrange("(cc p) t -> p cc t", p=128)[:, :,
                i * 128:(i + 1) * 128]),
            writes=[otcb[k]])
        tm = []
        for cc in range(NC):
            for half in range(2):
                tm.append(em.t(lambda e, cc=cc, half=half, k=k: e.matmul(
                    c.psA[:, half * 512:(half + 1) * 512],
                    Rc(otc[k][:, cc * 128:(cc + 1) * 128]),
                    Rc(wsb[:, cc * E + half * 512:cc * E + (half + 1) * 512]),
                    start=(cc == 0), stop=False),
                    reads=[otcb[k], wsbb],
                    deps=(c.psAb.wr() if (cc == 0 and half == 0) else [])))
        for half in range(2):
            tm.append(em.t(lambda e, half=half: e.matmul(
                c.psA[:, half * 512:(half + 1) * 512],
                Rc(ones_row[0:1, :]),
                Rc(bias_row[0:1, half * 512:(half + 1) * 512]),
                start=False, stop=True),
                reads=[bias_rowb, c.constb]))
        c.psAb.w = tm[-1]; c.psAb.wl = []; c.psAb.r = []
        tev = em.s(lambda e, k=k: e.copy(ysb[k][:], c.psA[:]),
                   reads=[c.psAb], writes=[ysbb[k]])
        tln = ln_to(em, c, ysb[k][:], ysbb[k], lnb[k][:], lnbb[k], st, E)
        tadd = em.v(lambda e, k=k: e.tensor_tensor(ysb[k][:], lnb[k][:],
                                                   c.xt[k][:], ALU.add),
                    deps=[tln], reads=[lnbb[k], c.xtb[k]], writes=[ysbb[k]])
        em.dma("gpsimd", lambda e, i=i, k=k: e.dma_start(
            out=x_dst[i * 128:(i + 1) * 128, :], in_=ysb[k][:]),
            reads=[ysbb[k]])



def phase0(c, pstack, W, x_in, X1, QTD, KTD, VTD, OTD, out_final=None):
    nc = c.nc; em = c.em
    R = lambda ap: ap

    scope_ab = contextlib.ExitStack()
    ent = scope_ab.enter_context

    masks_sb = ent(nc.sbuf_tensor("p0_masks", [128, 4 * 512], F32)); masksb = Buf()
    em.dma("sync", lambda e: e.dma_start(
        out=masks_sb[:].rearrange("p (a q) -> p a q", a=4),
        in_=W["masks"].rearrange("(a p) q -> p a q", p=128)), writes=[masksb])

    c.cs_t = [ent(nc.sbuf_tensor(f"p0cs_t{k}", [128, E], F32)) for k in range(2)]
    c.cs_tb = [Buf(), Buf()]
    c.sc_t = [ent(nc.sbuf_tensor(f"p0sc_t{k}", [128, E], F32)) for k in range(2)]
    c.sc_tb = [Buf(), Buf()]

    wqa_sb = ent(nc.sbuf_tensor("p0_wqa", [128, NC * 64], PDT)); wqab = Buf()
    wkva_sb = ent(nc.sbuf_tensor("p0_wkva", [128, NC * 32], PDT)); wkvab = Buf()
    wqb_sb = ent(nc.sbuf_tensor("p0_wqb", [64, E], PDT)); wqbb = Buf()
    wkvb_sb = ent(nc.sbuf_tensor("p0_wkvb", [32, 512], PDT)); wkvbb = Buf()

    em.dma("gpsimd", lambda e: e.dma_start(
        out=wqa_sb[:].rearrange("p (cc m) -> p cc m", cc=NC),
        in_=W["wqa"].rearrange("(cc p) m -> p cc m", p=128)), writes=[wqab])
    em.dma("gpsimd", lambda e: e.dma_start(
        out=wkva_sb[:].rearrange("p (cc m) -> p cc m", cc=NC),
        in_=W["wkva"].rearrange("(cc p) m -> p cc m", p=128)), writes=[wkvab])
    em.dma("gpsimd", lambda e: e.dma_start(out=wqb_sb[:], in_=W["wqb"]),
           writes=[wqbb])
    em.dma("gpsimd", lambda e: e.dma_start(out=wkvb_sb[:], in_=W["wkvb"]),
           writes=[wkvbb])
    em.v(lambda e: e.tensor_scalar_mul(wqb_sb[:], wqb_sb[:], SCALE8),
         reads=[wqbb], writes_also=[wqbb])

    kts0 = [ent(nc.sbuf_tensor(f"p0_kts0{k}", [128, 256], PDT)) for k in range(2)]
    kts0b = [Buf(), Buf()]
    vtm0 = [ent(nc.sbuf_tensor(f"p0_vtm0{k}", [128, 4 * 65], PDT)) for k in range(2)]
    vtm0b = [Buf(), Buf()]
    for k in range(2):
        em.s(lambda e, k=k: e.copy(
            vtm0[k][:].rearrange("p (h m) -> p h m", h=4)[:, :, 64:65],
            c.ones_f32[:, 0:4].rearrange("p (h m) -> p h m", m=1)),
            reads=[c.constb], writes=[vtm0b[k]])

    xn = [ent(nc.sbuf_tensor(f"p0_xn{k}", [128, E], F32)) for k in range(2)]
    xnb = [Buf(), Buf()]
    xnT = [ent(nc.sbuf_tensor(f"p0_xnT{k}", [128, NC * 128], PDT))
           for k in range(2)]
    xnTb = [Buf(), Buf()]
    kvl = [ent(nc.sbuf_tensor(f"p0_kvl{k}", [32, 128], PDT)) for k in range(2)]
    kvlb = [Buf(), Buf()]
    ql_t = [ent(nc.sbuf_tensor(f"p0_ql{k}", [64, 128], PDT)) for k in range(2)]
    ql_tb = [Buf(), Buf()]
    kr = [ent(nc.sbuf_tensor(f"p0_kr{k}", [128, 256], F32)) for k in range(2)]
    krb = [Buf(), Buf()]
    vtmp = [ent(nc.sbuf_tensor(f"p0_vtmp{k}", [128, 256], F32)) for k in range(2)]
    vtmpb = [Buf(), Buf()]
    qts = [ent(nc.sbuf_tensor(f"p0_qts{k}", [128, NC * 128], PDT)) for k in range(2)]
    qtsb = [Buf(), Buf()]

    for i in (range(NT) if "a" in P0_PARTS else []):
        k = i % 2
        st = dict(sq=(c.sq[k][:, 0:E], c.sqb[k]), stat=(c.stat[k][:], c.statb[k]))
        em.dma("sync", lambda e, i=i, k=k: e.dma_start(
            out=c.xt[k][:], in_=x_in[i * 128:(i + 1) * 128, :]), writes=[c.xtb[k]])
        stream_tables(em, c, i, k)
        ln_to(em, c, c.xt[k][:], c.xtb[k], xn[k][:], xnb[k], st, E)
        if P0A_STEP < 2:
            continue
        for cc in range(NC):
            transpose_128(em, c, xn[k][:, cc * 128:(cc + 1) * 128], [xnb[k]],
                          xnT[k][:, cc * 128:(cc + 1) * 128],
                          [xnTb[k]], cc % 2, f32=True,
                          deps=(xnTb[k].wr() if cc == 0 else []), final=(cc == NC - 1))
        if P0A_STEP < 21:
            continue
        tq = []
        for cc in range(NC):
            tq.append(em.t(lambda e, cc=cc, k=k: e.matmul(
                c.psC[0:64, 0:128], R(wqa_sb[:, cc * 64:(cc + 1) * 64]),
                R(xnT[k][:, cc * 128:(cc + 1) * 128]),
                start=(cc == 0), stop=(cc == NC - 1)),
                reads=[wqab, xnTb[k]],
                deps=(c.psCb.wr() if cc == 0 else [])))
        c.psCb.w = tq[-1]; c.psCb.wl = []; c.psCb.r = []
        em.s(lambda e, k=k: e.copy(ql_t[k][:], c.psC[0:64, 0:128]),
             reads=[c.psCb], writes=[ql_tb[k]])
        if P0A_STEP < 22:
            continue
        tkv = []
        for cc in range(NC):
            tkv.append(em.t(lambda e, cc=cc, k=k: e.matmul(
                c.psD[0:32, 0:128], R(wkva_sb[:, cc * 32:(cc + 1) * 32]),
                R(xnT[k][:, cc * 128:(cc + 1) * 128]),
                start=(cc == 0), stop=(cc == NC - 1)),
                reads=[wkvab, xnTb[k]],
                deps=(c.psDb.wr() if cc == 0 else [])))
        c.psDb.w = tkv[-1]; c.psDb.wl = []; c.psDb.r = []
        em.s(lambda e, k=k: e.copy(kvl[k][:], c.psD[0:32, 0:128]),
             reads=[c.psDb], writes=[kvlb[k]])
        if P0A_STEP < 3:
            continue
        tkvmm = em.t(lambda e, k=k: e.matmul(c.psB[:, 0:512], R(kvl[k][:]),
                                             R(wkvb_sb[:]), start=True, stop=True),
                     reads=[kvlb[k], wkvbb], deps=c.psBb.wr())
        c.psBb.w = tkvmm; c.psBb.wl = []; c.psBb.r = []
        tA = em.v(lambda e, k=k, _cs=c.cs_t[k][:, 0:256]: e.tensor_tensor(
            c.tmpA[k][:, 0:256], c.psB[:, 0:256], _cs, ALU.mult),
            reads=[c.psBb, c.cs_tb[k]], writes=[c.tmpAb[k]])
        tB = em.v(lambda e, k=k, _sc=c.sc_t[k][:, 0:256]: e.tensor_tensor(
            c.tmpB[k][:, 0:256], c.psB[:, 0:256], _sc, ALU.mult),
            reads=[c.psBb, c.sc_tb[k]], writes=[c.tmpBb[k]])
        vA = lambda k: c.tmpA[k][:, 0:256].rearrange("p (h x) -> p h x", h=4)
        vB = lambda k: c.tmpB[k][:, 0:256].rearrange("p (h x) -> p h x", h=4)
        vK = lambda k: kr[k][:].rearrange("p (h x) -> p h x", h=4)
        t1 = em.v(lambda e, k=k: e.tensor_tensor(
            vK(k)[:, :, 0:32], vA(k)[:, :, 0:32], vA(k)[:, :, 32:64],
            ALU.subtract), deps=[tA], reads=[c.tmpAb[k]], writes=[krb[k]])
        t2 = em.v(lambda e, k=k: e.tensor_tensor(
            vK(k)[:, :, 32:64], vB(k)[:, :, 0:32], vB(k)[:, :, 32:64],
            ALU.add), deps=[tB], reads=[c.tmpBb[k]], writes_also=[krb[k]])
        if P0A_STEP < 4:
            continue
        em.v(lambda e, k=k: e.tensor_copy(vtmp[k][:], c.psB[:, 256:512]),
             reads=[c.psBb], writes=[vtmpb[k]])
        em.s(lambda e, k=k: e.copy(
            vtm0[k][:].rearrange("p (h m) -> p h m", h=4)[:, :, 0:64],
            vtmp[k][:].rearrange("p (h m) -> p h m", h=4)),
            deps=vtm0b[k].wr(), reads=[vtmpb[k]], writes_also=[vtm0b[k]])
        em.dma("gpsimd", lambda e, i=i, k=k: e.dma_start(
            out=VTD[i * 128:(i + 1) * 128, 0:260], in_=vtm0[k][:]),
            reads=[vtm0b[k]])
        if P0A_STEP < 5:
            continue
        for blkk in range(2):
            transpose_128(em, c, kr[k][:, blkk * 128:(blkk + 1) * 128],
                          [krb[k]],
                          kts0[k][:, blkk * 128:(blkk + 1) * 128],
                          [kts0b[k]], blkk, f32=True,
                          deps=(kts0b[k].wr() if blkk == 0 else []), final=(blkk == 1))
        em.dma("scalar", lambda e, i=i, k=k: e.dma_start(
            out=KTD.rearrange("(cc p) t -> p cc t", p=128)[
                :, 0:2, i * 128:(i + 1) * 128],
            in_=kts0[k][:].rearrange("p (cc t) -> p cc t", cc=2)),
            reads=[kts0b[k]])
        if P0A_STEP < 6:
            continue
        tqa = []
        for half in range(2):
            tqa.append(em.t(lambda e, half=half, k=k: e.matmul(
                c.psA[:, half * 512:(half + 1) * 512], R(ql_t[k][:]),
                R(wqb_sb[:, half * 512:(half + 1) * 512]), start=True, stop=True),
                reads=[ql_tb[k], wqbb],
                deps=(c.psAb.wr() if half == 0 else [])))
        c.psAb.w = tqa[-1]; c.psAb.wl = []; c.psAb.r = []
        rope_all(em, c, c.psA[:, 0:1024], c.psAb, 16,
                 c.cs_t[k][:], c.sc_t[k][:], c.cs_tb[k], c.sc_tb[k],
                 None, None, c.tmpA[k], c.tmpAb[k], c.tmpB[k],
                 c.tmpBb[k], None)
        for cc in range(NC):
            transpose_128(em, c, c.tmpA[k][:, cc * 128:(cc + 1) * 128],
                          [c.tmpAb[k]],
                          qts[k][:, cc * 128:(cc + 1) * 128],
                          [qtsb[k]], cc % 2, f32=True,
                          deps=(qtsb[k].wr() if cc == 0 else []), final=(cc == NC - 1))
        qts_store(em, c, qts[k], qtsb[k], QTD, i)

    em.barrier()

    QT = [ent(nc.sbuf_tensor(f"p0_QT{k}", [64, 512], PDT)) for k in range(2)]
    QTb = [Buf(), Buf()]
    KT = [ent(nc.sbuf_tensor(f"p0_KTh{k}", [64, S], PDT)) for k in range(2)]
    KTb = [Buf(), Buf()]
    Vh = [ent(nc.sbuf_tensor(f"p0_Vh{k}", [128, NT * 65], PDT)) for k in range(2)]
    Vhb = [Buf(), Buf()]
    P_sb = [ent(nc.sbuf_tensor(f"p0_P{k}", [128, 1024], PDT)) for k in range(3)]
    P_sbb = [[Buf(), Buf()] for _ in range(3)]
    den = ent(nc.sbuf_tensor("p0_den", [1, 512], F32)); denb = Buf()
    rcrep = ent(nc.sbuf_tensor("p0_rcrep", [64, 512], F32)); rcrepb = Buf()
    ot = [ent(nc.sbuf_tensor(f"p0_ot{k}", [64, 512], PDT)) for k in range(2)]
    otb = [Buf(), Buf()]
    cfg = dict(P_sb=P_sb, P_sbb=P_sbb, den=den, denb=denb, rcrep=rcrep,
               rcrepb=rcrepb, ot=ot, otb=otb, masks_sb=masks_sb, masksb=masksb,
               f32=True)

    flash_fork(c)
    for h in (range(16) if "b" in P0_PARTS else []):
        kv = h // 4
        kk = kv % 2
        if h % 4 == 0:
            em.dma("sync", lambda e, kv=kv, kk=kk: e.dma_start(
                out=KT[kk][:], in_=KTD[kv * 64:(kv + 1) * 64, :]),
                writes=[KTb[kk]])
            em.dma("gpsimd", lambda e, kv=kv, kk=kk: e.dma_start(
                out=Vh[kk][:].rearrange("p (i m) -> p i m", i=NT),
                in_=VTD.rearrange("(i p) m -> p i m", p=128)[
                    :, :, kv * 65:(kv + 1) * 65]), writes=[Vhb[kk]])
        for qb in range(8):
            kq = (h * 8 + qb) % 2
            em.dma("sync", lambda e, h=h, qb=qb, kq=kq: e.dma_start(
                out=QT[kq][:], in_=QTD[h * 64:(h + 1) * 64,
                                       qb * 512:(qb + 1) * 512]),
                writes=[QTb[kq]])
            flash_head(c, em, h, qb, kq, KT[kk][:], KTb[kk], Vh[kk][:], Vhb[kk],
                       QT[kq][:], QTb[kq], cfg, OTD, causal=True, Kd=64)
    if cfg.get("pending_tail"):
        cfg["pending_tail"]()
        cfg["pending_tail"] = None
    flash_merge(c)

    em.barrier()
    scope_ab.close()

    with contextlib.ExitStack() as ps2:
        ent2 = ps2.enter_context
        wout_sb = ent2(nc.sbuf_tensor("p0_wout", [128, NC * E], PDT))
        woutb = Buf()
        boutr_sb = ent2(nc.sbuf_tensor("p0_boutr", [1, E], PDT)); boutb = Buf()
        em.dma("gpsimd", lambda e: e.dma_start(
            out=wout_sb[:].rearrange("p (cc m) -> p cc m", cc=NC),
            in_=W["wout"].rearrange("(cc p) m -> p cc m", p=128)), writes=[woutb])
        em.dma("gpsimd", lambda e: e.dma_start(out=boutr_sb[:], in_=W["bout_row"]),
               writes=[boutb])
        tgt = out_final if out_final is not None else X1
        if "c" in P0_PARTS:
            proj_residual(c, em, ps2, OTD, x_in, tgt, wout_sb[:], woutb,
                          boutr_sb, boutb, "p0c", f32=True)


def phase1(c, pstack, W, Xin, X2, QTD, KTD, OTD, U, N2, YACC, VTD, HTD,
           out_final=None):
    nc = c.nc; em = c.em
    R = lambda ap: ap
    ent_outer = pstack.enter_context
    w2ncol = ent_outer(nc.sbuf_tensor("p1_w2ncol", [128, 8], F32))
    c.w2ncol = w2ncol
    w2ncolb = Buf(); c.w2ncolb = w2ncolb
    c.MSK = ent_outer(nc.sbuf_tensor("p1_MSK", [128, NT * 4], F32))
    c.MSKb = Buf()
    em.dma("sync", lambda e: e.dma_start(out=w2ncol[:], in_=W["w2ncol"]),
           writes=[w2ncolb])

    scope_a = contextlib.ExitStack()
    ent = scope_a.enter_context
    c.cs_t = [ent(nc.sbuf_tensor(f"p1cs_t{k}", [128, E], F32)) for k in range(2)]
    c.cs_tb = [Buf(), Buf()]
    c.sc_t = [ent(nc.sbuf_tensor(f"p1sc_t{k}", [128, E], F32)) for k in range(2)]
    c.sc_tb = [Buf(), Buf()]

    wq_sb = ent(nc.sbuf_tensor("p1_wq", [128, NC * E], PDT)); wqb_ = Buf()
    wdkv_sb = ent(nc.sbuf_tensor("p1_wdkv", [128, NC * 512], PDT)); wdkvb = Buf()
    wuk_sb = ent(nc.sbuf_tensor("p1_wuk", [128, 4 * E], PDT)); wukb = Buf()
    wuv_sb = ent(nc.sbuf_tensor("p1_wuv", [128, 4 * E], PDT)); wuvb = Buf()
    w1ncol = ent(nc.sbuf_tensor("p1_w1ncol", [128, 8], F32)); w1ncolb = Buf()
    lngcol = ent(nc.sbuf_tensor("p1_lngcol", [128, 4], F32)); lngcolb = Buf()
    lnbcol = ent(nc.sbuf_tensor("p1_lnbcol", [128, 4], PDT)); lnbcolb = Buf()

    em.dma("gpsimd", lambda e: e.dma_start(
        out=wq_sb[:].rearrange("p (cc m) -> p cc m", cc=NC),
        in_=W["wqT0f"].rearrange("(cc p) m -> p cc m", p=128)), writes=[wqb_])
    em.dma("gpsimd", lambda e: e.dma_start(
        out=wdkv_sb[:].rearrange("p (cc m) -> p cc m", cc=NC),
        in_=W["wdkvT"].rearrange("(cc p) m -> p cc m", p=128)), writes=[wdkvb])
    em.dma("gpsimd", lambda e: e.dma_start(
        out=wuk_sb[:].rearrange("p (cc m) -> p cc m", cc=4),
        in_=W["wukT"].rearrange("(cc p) m -> p cc m", p=128)), writes=[wukb])
    em.dma("gpsimd", lambda e: e.dma_start(
        out=wuv_sb[:].rearrange("p (cc m) -> p cc m", cc=4),
        in_=W["wuvT"].rearrange("(cc p) m -> p cc m", p=128)), writes=[wuvb])
    em.dma("sync", lambda e: e.dma_start(out=w1ncol[:], in_=W["w1ncol"]),
           writes=[w1ncolb])
    em.dma("sync", lambda e: e.dma_start(out=lngcol[:], in_=W["lngcol"]),
           writes=[lngcolb])
    em.dma("gpsimd", lambda e: e.dma_start(out=lnbcol[:], in_=W["lnbcol"]),
           writes=[lnbcolb])

    bvk_row = ent(nc.sbuf_tensor("p1_bvkrow", [1, E], F32)); bvkrb = Buf()
    bvv_row = ent(nc.sbuf_tensor("p1_bvvrow", [1, E], F32)); bvvrb = Buf()
    bvk = ent(nc.sbuf_tensor("p1_bvk", [128, E], F32)); bvkb = Buf()
    bvv = ent(nc.sbuf_tensor("p1_bvv", [128, E], F32)); bvvb = Buf()
    for (wsb, wsbb, row, rowb, rep, repb) in (
            (wuk_sb, wukb, bvk_row, bvkrb, bvk, bvkb),
            (wuv_sb, wuvb, bvv_row, bvvrb, bvv, bvvb)):
        for half in range(2):
            tmm = []
            for lc in range(4):
                tmm.append(em.t(lambda e, lc=lc, half=half, wsb=wsb: e.matmul(
                    c.psC[0:1, 0:512], R(lnbcol[:, lc:lc + 1]),
                    R(wsb[:, lc * E + half * 512:lc * E + (half + 1) * 512]),
                    start=(lc == 0), stop=(lc == 3)),
                    reads=[lnbcolb, wsbb],
                    deps=(c.psCb.wr() if lc == 0 else [])))
            c.psCb.w = tmm[-1]; c.psCb.wl = []; c.psCb.r = []
            em.s(lambda e, half=half, row=row: e.copy(
                row[0:1, half * 512:(half + 1) * 512], c.psC[0:1, 0:512]),
                reads=[c.psCb], writes=([rowb]),
                writes_also=([rowb] if half == 0 else []))
        replicate_row(em, c, row[:], rowb, rep[:], repb, E, c.psB, c.psBb)
    for cc in range(NC):
        em.v(lambda e, cc=cc: e.tensor_scalar(
            wq_sb[:, cc * E:(cc + 1) * E], wq_sb[:, cc * E:(cc + 1) * E],
            w1ncol[:, cc:cc + 1], SCALE8, ALU.mult, ALU.mult),
            deps=wqb_.wr(), reads=[w1ncolb], writes_also=[wqb_])
        em.v(lambda e, cc=cc: e.tensor_scalar_mul(
            wdkv_sb[:, cc * 512:(cc + 1) * 512], wdkv_sb[:, cc * 512:(cc + 1) * 512],
            w1ncol[:, cc:cc + 1]), deps=wdkvb.wr(), reads=[w1ncolb],
            writes_also=[wdkvb])
    for lc in range(4):
        em.v(lambda e, lc=lc: e.tensor_scalar_mul(
            wuk_sb[:, lc * E:(lc + 1) * E], wuk_sb[:, lc * E:(lc + 1) * E],
            lngcol[:, lc:lc + 1]), deps=wukb.wr(), reads=[lngcolb],
            writes_also=[wukb])
        em.v(lambda e, lc=lc: e.tensor_scalar_mul(
            wuv_sb[:, lc * E:(lc + 1) * E], wuv_sb[:, lc * E:(lc + 1) * E],
            lngcol[:, lc:lc + 1]), deps=wuvb.wr(), reads=[lngcolb],
            writes_also=[wuvb])

    xr1 = [ent(nc.sbuf_tensor(f"p1_xr1{k}", [128, E], F32)) for k in range(2)]
    xr1b = [Buf(), Buf()]
    xnT = [ent(nc.sbuf_tensor(f"p1_xnT{k}", [128, NC * 128], PDT)) for k in range(2)]
    xnTb = [Buf(), Buf()]
    latn = [ent(nc.sbuf_tensor(f"p1_latn{k}", [128, 512], F32)) for k in range(2)]
    latnb = [Buf(), Buf()]
    latT = [ent(nc.sbuf_tensor(f"p1_latT{k}", [128, 4 * 128], PDT)) for k in range(2)]
    latTb = [Buf(), Buf()]
    qts = [ent(nc.sbuf_tensor(f"p1_qts{k}", [128, NC * 128], PDT)) for k in range(2)]
    qtsb = [Buf(), Buf()]
    kts = [ent(nc.sbuf_tensor(f"p1_kts{k}", [128, NC * 128], PDT)) for k in range(2)]
    ktsb = [Buf(), Buf()]
    vtm = [ent(nc.sbuf_tensor(f"p1_vtm{k}", [128, 16 * 65], PDT)) for k in range(2)]
    vtmb = [Buf(), Buf()]
    for k in range(2):
        em.s(lambda e, k=k: e.copy(
            vtm[k][:].rearrange("p (h m) -> p h m", h=16)[:, :, 64:65],
            c.ones_f32[:, 0:16].rearrange("p (h m) -> p h m", m=1)),
            reads=[c.constb], writes=[vtmb[k]])

    for i in range(NT):
        k = i % 2
        st = dict(sq=(c.sq[k][:, 0:E], c.sqb[k]), stat=(c.stat[k][:], c.statb[k]))
        em.dma("sync", lambda e, i=i, k=k: e.dma_start(
            out=c.xt[k][:], in_=Xin[i * 128:(i + 1) * 128, :]), writes=[c.xtb[k]])
        stream_tables(em, c, i, k)
        ln_compose(em, c, c.xt[k][:], c.xtb[k], xr1[k][:], xr1b[k], st, E,
                   eps6_2=True)
        for cc in range(NC):
            transpose_128(em, c, xr1[k][:, cc * 128:(cc + 1) * 128], [xr1b[k]],
                          xnT[k][:, cc * 128:(cc + 1) * 128],
                          [xnTb[k]], cc % 2, f32=True,
                          deps=(xnTb[k].wr() if cc == 0 else []), final=(cc == NC - 1))
        tqa = []
        for half in range(2):
            for cc in range(NC):
                tqa.append(em.t(lambda e, cc=cc, half=half, k=k: e.matmul(
                    c.psA[:, half * 512:(half + 1) * 512],
                    R(xnT[k][:, cc * 128:(cc + 1) * 128]),
                    R(wq_sb[:, cc * E + half * 512:cc * E + (half + 1) * 512]),
                    start=(cc == 0), stop=(cc == NC - 1)),
                    reads=[xnTb[k], wqb_],
                    deps=(c.psAb.wr() if cc == 0 and half == 0 else [])))
        c.psAb.w = tqa[-1]; c.psAb.wl = []; c.psAb.r = []
        rope_all(em, c, c.psA[:, 0:1024], c.psAb, 16,
                 c.cs_t[k][:], c.sc_t[k][:], c.cs_tb[k], c.sc_tb[k],
                 None, None, c.tmpA[k], c.tmpAb[k], c.tmpB[k], c.tmpBb[k], None)
        for cc in range(NC):
            transpose_128(em, c, c.tmpA[k][:, cc * 128:(cc + 1) * 128],
                          [c.tmpAb[k]],
                          qts[k][:, cc * 128:(cc + 1) * 128],
                          [qtsb[k]], cc % 2, f32=True,
                          deps=(qtsb[k].wr() if cc == 0 else []), final=(cc == NC - 1))
        qts_store(em, c, qts[k], qtsb[k], QTD, i)
        tl = []
        for cc in range(NC):
            tl.append(em.t(lambda e, cc=cc, k=k: e.matmul(
                c.psC[:, 0:512], R(xnT[k][:, cc * 128:(cc + 1) * 128]),
                R(wdkv_sb[:, cc * 512:(cc + 1) * 512]),
                start=(cc == 0), stop=(cc == NC - 1)),
                reads=[xnTb[k], wdkvb],
                deps=(c.psCb.wr() if cc == 0 else [])))
        c.psCb.w = tl[-1]; c.psCb.wl = []; c.psCb.r = []
        st2 = dict(sq=(c.sq[k][:, 0:512], c.sqb[k]), stat=(c.stat[k][:], c.statb[k]))
        ln_to(em, c, c.psC[:, 0:512], c.psCb, latn[k][:], latnb[k], st2, 512)
        for lc in range(4):
            transpose_128(em, c, latn[k][:, lc * 128:(lc + 1) * 128], [latnb[k]],
                          latT[k][:, lc * 128:(lc + 1) * 128],
                          [latTb[k]], lc % 2, f32=True,
                          deps=(latTb[k].wr() if lc == 0 else []), final=(lc == 3))
        tka = []
        for half in range(2):
            for lc in range(4):
                tka.append(em.t(lambda e, lc=lc, half=half, k=k: e.matmul(
                    c.psA[:, half * 512:(half + 1) * 512],
                    R(latT[k][:, lc * 128:(lc + 1) * 128]),
                    R(wuk_sb[:, lc * E + half * 512:lc * E + (half + 1) * 512]),
                    start=(lc == 0), stop=(lc == 3)),
                    reads=[latTb[k], wukb],
                    deps=(c.psAb.wr() if lc == 0 and half == 0 else [])))
        c.psAb.w = tka[-1]; c.psAb.wl = []; c.psAb.r = []
        tkb1 = em.v(lambda e, k=k: e.tensor_tensor(c.psA[:, 0:512],
                                                   c.psA[:, 0:512], bvk[:, 0:512],
                                                   ALU.add),
                    reads=[bvkb, c.psAb], writes_also=[c.psAb])
        tkb = em.v(lambda e, k=k: e.tensor_tensor(c.psA[:, 512:1024],
                                                  c.psA[:, 512:1024],
                                                  bvk[:, 512:1024], ALU.add),
                   deps=[tkb1], reads=[bvkb], writes_also=[c.psAb])
        rope_all(em, c, c.psA[:, 0:1024], c.psAb, 16,
                 c.cs_t[k][:], c.sc_t[k][:], c.cs_tb[k], c.sc_tb[k],
                 None, None, c.tmpA[k], c.tmpAb[k], c.tmpB[k], c.tmpBb[k], None,
                 extra_deps=[tkb])
        for cc in range(NC):
            transpose_128(em, c, c.tmpA[k][:, cc * 128:(cc + 1) * 128],
                          [c.tmpAb[k]],
                          kts[k][:, cc * 128:(cc + 1) * 128],
                          [ktsb[k]], cc % 2, f32=True,
                          deps=(ktsb[k].wr() if cc == 0 else []), final=(cc == NC - 1))
        qts_store(em, c, kts[k], ktsb[k], KTD, i, queue="scalar")
        tva = []
        for half in range(2):
            for lc in range(4):
                tva.append(em.t(lambda e, lc=lc, half=half, k=k: e.matmul(
                    c.psB[:, half * 512:(half + 1) * 512],
                    R(latT[k][:, lc * 128:(lc + 1) * 128]),
                    R(wuv_sb[:, lc * E + half * 512:lc * E + (half + 1) * 512]),
                    start=(lc == 0), stop=(lc == 3)),
                    reads=[latTb[k], wuvb],
                    deps=(c.psBb.wr() if lc == 0 and half == 0 else [])))
        c.psBb.w = tva[-1]; c.psBb.wl = []; c.psBb.r = []
        tvb = em.v(lambda e, k=k: e.tensor_tensor(c.tmpA[k][:], c.psB[:, 0:1024],
                                                  bvv[:], ALU.add),
                   reads=[c.psBb, bvvb], writes=[c.tmpAb[k]])
        tvc = em.s(lambda e, k=k: e.copy(
            vtm[k][:].rearrange("p (h m) -> p h m", h=16)[:, :, 0:64],
            c.tmpA[k][:].rearrange("p (h m) -> p h m", h=16)),
            deps=[tvb] + vtm_war(vtmb[k]), reads=[c.tmpAb[k]],
            writes_also=[vtmb[k]])
        em.dma("gpsimd", lambda e, i=i, k=k: e.dma_start(
            out=VTD[i * 128:(i + 1) * 128, :], in_=vtm[k][:]), reads=[vtmb[k]])

    em.barrier()
    scope_a.close()

    scope_b = contextlib.ExitStack()
    entb = scope_b.enter_context
    masks_sb = entb(nc.sbuf_tensor("p1_masks", [128, 4 * 512], F32)); masksb = Buf()
    em.dma("sync", lambda e: e.dma_start(
        out=masks_sb[:].rearrange("p (a q) -> p a q", a=4),
        in_=W["masks"].rearrange("(a p) q -> p a q", p=128)), writes=[masksb])
    QT = [entb(nc.sbuf_tensor(f"p1_QT{k}", [64, 512], PDT)) for k in range(2)]
    QTb = [Buf(), Buf()]
    KT = [entb(nc.sbuf_tensor(f"p1_KT{k}", [64, S], PDT)) for k in range(2)]
    KTb = [Buf(), Buf()]
    Vh = [entb(nc.sbuf_tensor(f"p1_Vh{k}", [128, NT * 65], PDT)) for k in range(2)]
    Vhb = [Buf(), Buf()]
    P_sb = [entb(nc.sbuf_tensor(f"p1_P{k}", [128, 1024], PDT)) for k in range(3)]
    P_sbb = [[Buf(), Buf()] for _ in range(3)]
    den = entb(nc.sbuf_tensor("p1_den", [1, 512], F32)); denb = Buf()
    rcrep = entb(nc.sbuf_tensor("p1_rcrep", [64, 512], F32)); rcrepb = Buf()
    ot = [entb(nc.sbuf_tensor(f"p1_ot{k}", [64, 512], PDT)) for k in range(2)]
    otb = [Buf(), Buf()]
    cfg = dict(P_sb=P_sb, P_sbb=P_sbb, den=den, denb=denb, rcrep=rcrep,
               rcrepb=rcrepb, ot=ot, otb=otb, masks_sb=masks_sb, masksb=masksb,
               f32=True)

    flash_fork(c)
    for h in range(16):
        kh = h % 2
        em.dma("sync", lambda e, h=h, kh=kh: e.dma_start(
            out=KT[kh][:], in_=KTD[h * 64:(h + 1) * 64, :]), writes=[KTb[kh]])
        em.dma("gpsimd", lambda e, h=h, kh=kh: e.dma_start(
            out=Vh[kh][:].rearrange("p (i m) -> p i m", i=NT),
            in_=VTD.rearrange("(i p) m -> p i m", p=128)[
                :, :, h * 65:(h + 1) * 65]), writes=[Vhb[kh]])
        for qb in range(8):
            kq = (h * 8 + qb) % 2
            em.dma("scalar", lambda e, h=h, qb=qb, kq=kq: e.dma_start(
                out=QT[kq][:], in_=QTD[h * 64:(h + 1) * 64,
                                       qb * 512:(qb + 1) * 512]),
                writes=[QTb[kq]])
            flash_head(c, em, h, qb, kq, KT[kh][:], KTb[kh], Vh[kh][:], Vhb[kh],
                       QT[kq][:], QTb[kq], cfg, OTD, causal=True, Kd=64)
    if cfg.get("pending_tail"):
        cfg["pending_tail"]()
        cfg["pending_tail"] = None
    flash_merge(c)

    em.barrier()
    scope_b.close()

    MSK = c.MSK; MSKb = c.MSKb
    with contextlib.ExitStack() as ps2:
        ent2 = ps2.enter_context
        wo_sb = ent2(nc.sbuf_tensor("p1_wo", [128, NC * E], PDT)); wob = Buf()
        gw_sb = ent2(nc.sbuf_tensor("p1_gw", [128, NC * 4], F32)); gwb = Buf()
        gb_sb = ent2(nc.sbuf_tensor("p1_gb", [1, 4], F32)); gbb = Buf()
        em.dma("gpsimd", lambda e: e.dma_start(
            out=wo_sb[:].rearrange("p (cc m) -> p cc m", cc=NC),
            in_=W["woT0f"].rearrange("(cc p) m -> p cc m", p=128)), writes=[wob])
        em.dma("sync", lambda e: e.dma_start(
            out=gw_sb[:].rearrange("p (cc m) -> p cc m", cc=NC),
            in_=W["gwT"].rearrange("(cc p) m -> p cc m", p=128)), writes=[gwb])
        em.dma("sync", lambda e: e.dma_start(out=gb_sb[:], in_=W["gbrow"]),
               writes=[gbb])
        for cc in range(NC):
            em.v(lambda e, cc=cc: e.tensor_scalar_mul(
                gw_sb[:, cc * 4:(cc + 1) * 4], gw_sb[:, cc * 4:(cc + 1) * 4],
                c.w2ncol[:, cc:cc + 1]), reads=[c.w2ncolb, gwb],
                writes_also=[gwb])
        otc = [ent2(nc.sbuf_tensor(f"p1c_otc{k}", [128, NC * 128], PDT))
               for k in range(2)]
        otcb = [Buf(), Buf()]
        usb = [ent2(nc.sbuf_tensor(f"p1c_u{k}", [128, E], F32)) for k in range(2)]
        usbb = [Buf(), Buf()]
        n2sb = [ent2(nc.sbuf_tensor(f"p1c_n2{k}", [128, E], F32)) for k in range(2)]
        n2sbb = [Buf(), Buf()]
        n2bf = [ent2(nc.sbuf_tensor(f"p1c_n2bf{k}", [128, E], BF16))
                for k in range(2)]
        n2bfb = [Buf(), Buf()]
        n2T = [ent2(nc.sbuf_tensor(f"p1c_n2T{k}", [128, NC * 128], F32))
               for k in range(2)]
        n2Tb = [Buf(), Buf()]
        for i in range(NT):
            k = i % 2
            st = dict(sq=(c.sq[k][:, 0:E], c.sqb[k]), stat=(c.stat[k][:], c.statb[k]))
            em.dma("sync", lambda e, i=i, k=k: e.dma_start(
                out=c.xt[k][:], in_=Xin[i * 128:(i + 1) * 128, :]),
                writes=[c.xtb[k]])
            em.dma("scalar", lambda e, i=i, k=k: e.dma_start(
                out=otc[k][:].rearrange("p (cc t) -> p cc t", cc=NC),
                in_=OTD.rearrange("(cc p) t -> p cc t", p=128)[:, :,
                    i * 128:(i + 1) * 128]),
                writes=[otcb[k]])
            tm = []
            for cc in range(NC):
                for half in range(2):
                    tm.append(em.t(lambda e, cc=cc, half=half, k=k: e.matmul(
                        c.psA[:, half * 512:(half + 1) * 512],
                        R(otc[k][:, cc * 128:(cc + 1) * 128]),
                        R(wo_sb[:, cc * E + half * 512:cc * E + (half + 1) * 512]),
                        start=(cc == 0), stop=(cc == NC - 1)),
                        reads=[otcb[k], wob],
                        deps=(c.psAb.wr() if (cc == 0 and half == 0) else [])))
            c.psAb.w = tm[-1]; c.psAb.wl = []; c.psAb.r = []
            tlnx = ln_to(em, c, c.xt[k][:], c.xtb[k], usb[k][:], usbb[k], st, E)
            tu1 = em.v(lambda e, k=k: e.tensor_tensor(usb[k][:, 0:512],
                                                      usb[k][:, 0:512],
                                                      c.psA[:, 0:512], ALU.add),
                       deps=[tlnx], reads=[c.psAb], writes_also=[usbb[k]])
            tu = em.v(lambda e, k=k: e.tensor_tensor(usb[k][:, 512:1024],
                                                     usb[k][:, 512:1024],
                                                     c.psA[:, 512:1024], ALU.add),
                      deps=[tu1], reads=[c.psAb], writes_also=[usbb[k]])
            em.dma("gpsimd", lambda e, i=i, k=k: e.dma_start(
                out=U[i * 128:(i + 1) * 128, :], in_=usb[k][:]), reads=[usbb[k]])
            tn2 = ln_to(em, c, usb[k][:], usbb[k], n2sb[k][:], n2sbb[k], st, E,
                        eps6=True, rms=True)
            for cc in range(NC):
                transpose_128(em, c, n2sb[k][:, cc * 128:(cc + 1) * 128],
                              [n2sbb[k]], n2T[k][:, cc * 128:(cc + 1) * 128],
                              [n2Tb[k]], cc % 2, f32=True,
                              deps=(n2Tb[k].wr() if cc == 0 else []), final=(cc == NC - 1))
            tn2c = em.v(lambda e, k=k: e.tensor_copy(n2bf[k][:], n2T[k][:]),
                        reads=[n2Tb[k]], writes=[n2bfb[k]])
            em.dma("gpsimd", lambda e, i=i, k=k: e.dma_start(
                out=N2.rearrange("(cc p) t -> p cc t", p=128)[
                    :, :, i * 128:(i + 1) * 128],
                in_=n2bf[k][:].rearrange("p (cc t) -> p cc t", cc=NC)),
                reads=[n2bfb[k]])
            tg = []
            for cc in range(NC):
                tg.append(em.t(lambda e, cc=cc, k=k: e.matmul(
                    c.psC[:, 0:4], n2T[k][:, cc * 128:(cc + 1) * 128],
                    gw_sb[:, cc * 4:(cc + 1) * 4],
                    start=(cc == 0), stop=False),
                    reads=[n2Tb[k], gwb],
                    deps=(c.psCb.wr() if cc == 0 else [])))
            tg.append(em.t(lambda e: e.matmul(
                c.psC[:, 0:4], c.ones_f32[0:1, :], gb_sb[0:1, :],
                start=False, stop=True), reads=[gbb, c.constb]))
            c.psCb.w = tg[-1]; c.psCb.wl = []; c.psCb.r = []
            stt = c.stat[k]
            mx = stt[:, 5:6]; nt_ = stt[:, 6:7]
            tmx = em.v(lambda e, k=k: e.reduce_max(mx, c.psC[:, 0:4], axis=AX.X),
                       reads=[c.psCb], risky=True, writes_also=[c.statb[k]])
            tprev = tmx
            for ee in range(4):
                tq_ = em.v(lambda e, ee=ee, i=i: e.tensor_scalar(
                    MSK[:, i * 4 + ee:i * 4 + ee + 1], c.psC[:, ee:ee + 1],
                    mx[:, 0:1], None, ALU.is_ge),
                    deps=[tprev], risky=True, reads=[c.psCb, c.statb[k]],
                    writes_also=[MSKb])
                if ee == 0:
                    tprev = em.v(lambda e, i=i: e.tensor_scalar(
                        nt_, MSK[:, i * 4:i * 4 + 1], -1.0, 1.0, ALU.mult,
                        ALU.add),
                        deps=[tq_], risky=True, reads=[MSKb],
                        writes_also=[c.statb[k]])
                else:
                    tq2 = em.v(lambda e, ee=ee, i=i: e.tensor_tensor(
                        MSK[:, i * 4 + ee:i * 4 + ee + 1],
                        MSK[:, i * 4 + ee:i * 4 + ee + 1], nt_, ALU.mult),
                        deps=[tq_], risky=True, reads=[c.statb[k]],
                        writes_also=[MSKb])
                    tprev = tq2
                    if ee < 3:
                        tprev = em.v(lambda e, ee=ee, i=i: e.tensor_tensor(
                            nt_, nt_, MSK[:, i * 4 + ee:i * 4 + ee + 1],
                            ALU.subtract),
                            deps=[tq2], risky=True, reads=[MSKb],
                            writes_also=[c.statb[k]])

    em.barrier()

    yacc_trk = [Buf() for _ in range(NT)]
    MSK = c.MSK; MSKb = c.MSKb
    with contextlib.ExitStack() as ps3:
        ent3 = ps3.enter_context
        wte = ent3(nc.sbuf_tensor("p1d_wte", [128, 32 * E], BF16))
        wteb = Buf()
        xmT = [ent3(nc.sbuf_tensor(f"p1d_xmT{k}", [128, NC * 512], BF16))
               for k in range(2)]
        xmTb = [Buf(), Buf()]
        hst = [ent3(nc.sbuf_tensor(f"p1d_hst{k}", [128, 512], BF16))
               for k in range(2)]
        hstb = [Buf(), Buf()]
        hTin = [ent3(nc.sbuf_tensor(f"p1d_hTin{k}", [128, 32 * 128], BF16))
                for k in range(2)]
        hTinb = [Buf(), Buf()]
        b1c = ent3(nc.sbuf_tensor("p1d_b1c", [128, 32], F32)); b1cb = Buf()
        b2r = ent3(nc.sbuf_tensor("p1d_b2r", [1, E], BF16)); b2rb = Buf()
        yin = [ent3(nc.sbuf_tensor(f"p1d_yin{k}", [128, E], F32)) for k in range(2)]
        yinb = [Buf(), Buf()]
        yout = [ent3(nc.sbuf_tensor(f"p1d_yout{k}", [128, E], F32)) for k in range(2)]
        youtb = [Buf(), Buf()]
        uin = [ent3(nc.sbuf_tensor(f"p1d_uin{k}", [128, E], F32)) for k in range(2)]
        uinb = [Buf(), Buf()]
        lnw = [ent3(nc.sbuf_tensor(f"p1d_lnw{k}", [128, E], F32)) for k in range(2)]
        lnwb = [Buf(), Buf()]

        HTDb = Buf()
        for ex in (range(4) if "d" in P1_PARTS else []):
            war_htd = HTDb.wr()
            HTDb.w = None; HTDb.wl = []; HTDb.r = []
            em.dma("gpsimd", lambda e, ex=ex: e.dma_start(
                out=wte[:].rearrange("p (cc m) -> p cc m", cc=NC),
                in_=W["w1T"][ex].rearrange("(cc p) m -> p cc m", p=128)),
                writes=[wteb])
            em.dma("sync", lambda e, ex=ex: e.dma_start(
                out=b1c[:], in_=W["b1col"][ex]), writes=[b1cb])
            for cc in range(NC):
                em.v(lambda e, cc=cc: e.tensor_scalar_mul(
                    wte[:, cc * 4096:(cc + 1) * 4096],
                    wte[:, cc * 4096:(cc + 1) * 4096],
                    c.w2ncol[:, cc:cc + 1]), reads=[c.w2ncolb, wteb],
                    writes_also=[wteb])
            for gg in range(8):
                kg = gg % 2
                war = xmTb[kg].wr()
                lds = []
                for cc in range(NC):
                    lds.append(em.dma("sync", lambda e, gg=gg, kg=kg, cc=cc:
                                      e.dma_start(
                        out=xmT[kg][:, cc * 512:(cc + 1) * 512],
                        in_=N2[cc * 128:(cc + 1) * 128,
                               gg * 512:(gg + 1) * 512]),
                        deps=war))
                xmTb[kg].w = None; xmTb[kg].wl = lds; xmTb[kg].r = []
                for js in range(32):
                    kj = js % 2
                    psH, psHb = (c.psC, c.psCb) if kj == 0 else (c.psD, c.psDb)
                    th = []
                    for cc in range(NC):
                        th.append(em.t(lambda e, js=js, cc=cc, kg=kg, psH=psH:
                                       e.matmul(
                            psH[:, 0:512],
                            wte[:, cc * 4096 + js * 128:cc * 4096 + (js + 1) * 128],
                            xmT[kg][:, cc * 512:(cc + 1) * 512],
                            start=(cc == 0), stop=(cc == NC - 1)),
                            reads=[wteb, xmTb[kg]],
                            deps=(psHb.wr() if cc == 0 else [])))
                    psHb.w = th[-1]; psHb.wl = []; psHb.r = []
                    if kj == 0:
                        tev = em.s(lambda e, kj=kj, js=js, psH=psH: e.activation(
                            hst[kj][:], psH[:, 0:512],
                            AF.Relu, bias=b1c[:, js:js + 1], scale=1.0),
                            reads=[psHb, b1cb], writes=[hstb[kj]])
                    else:
                        tev = em.v(lambda e, kj=kj, js=js, psH=psH: e.tensor_scalar(
                            hst[kj][:], psH[:, 0:512],
                            b1c[:, js:js + 1], 0.0, ALU.add, ALU.max),
                            reads=[psHb, b1cb], writes=[hstb[kj]])
                    em.dma("gpsimd" if kj == 0 else "scalar",
                           lambda e, js=js, gg=gg, kj=kj: e.dma_start(
                               out=HTD.rearrange("(i js q) t -> q i js t",
                                                 js=32, q=128)[
                                   :, gg * 4:(gg + 1) * 4, js, :],
                               in_=hst[kj][:].rearrange("q (i t) -> q i t",
                                                        i=4)),
                           deps=(war_htd if (gg == 0 and js < 2) else []),
                           reads=[hstb[kj]], writes_also=[HTDb])
            em.dma("gpsimd", lambda e, ex=ex: e.dma_start(
                out=wte[:].rearrange("p (cc m) -> p cc m", cc=32),
                in_=W["w2T"][ex].rearrange("(cc p) m -> p cc m", p=128)),
                writes=[wteb])
            em.dma("gpsimd", lambda e, ex=ex: e.dma_start(
                out=b2r[:], in_=W["b2row"][ex]), writes=[b2rb])
            for i in range(NT):
                k = i % 2
                war_h = hTinb[k].wr()
                hld = []
                for qr in range(4):
                    hld.append(em.dma("sync", lambda e, i=i, k=k, qr=qr:
                                      e.dma_start(
                        out=hTin[k][:, qr * 1024:(qr + 1) * 1024].rearrange(
                            "p (js t) -> p js t", js=8),
                        in_=HTD.rearrange("(i js q) t -> q i js t", js=32,
                                          q=128)[:, i, qr * 8:(qr + 1) * 8, :]),
                        deps=war_h, reads=[HTDb]))
                hTinb[k].w = None; hTinb[k].wl = hld; hTinb[k].r = []
                psY, psYb = (c.psA, c.psAb) if k == 0 else (c.psB, c.psBb)
                ty = []
                for js in range(32):
                    for half in range(2):
                        ty.append(em.t(lambda e, js=js, half=half, k=k,
                                       psY=psY: e.matmul(
                            psY[:, half * 512:(half + 1) * 512],
                            hTin[k][:, js * 128:(js + 1) * 128],
                            wte[:, js * E + half * 512:js * E + (half + 1) * 512],
                            start=(js == 0), stop=False),
                            reads=[hTinb[k], wteb],
                            deps=(psYb.wr() if js == 0 and half == 0 else [])))
                for half in range(2):
                    ty.append(em.t(lambda e, half=half, psY=psY: e.matmul(
                        psY[:, half * 512:(half + 1) * 512],
                        c.ones_bf[0:1, :], b2r[0:1, half * 512:(half + 1) * 512],
                        start=False, stop=True),
                        reads=[b2rb, c.constb]))
                psYb.w = ty[-1]; psYb.wl = []; psYb.r = []
                if ex == 0:
                    tv = em.v(lambda e, i=i, k=k, psY=psY: e.tensor_scalar_mul(
                        yout[k][:], psY[:], MSK[:, i * 4:i * 4 + 1]),
                        reads=[psYb, MSKb], writes=[youtb[k]])
                    em.dma("gpsimd", lambda e, i=i, k=k: e.dma_start(
                        out=YACC[i * 128:(i + 1) * 128, :], in_=yout[k][:]),
                        reads=[youtb[k]], writes=[yacc_trk[i]])
                elif ex < 3:
                    em.dma("sync", lambda e, i=i, k=k: e.dma_start(
                        out=yin[k][:], in_=YACC[i * 128:(i + 1) * 128, :]),
                        reads=[yacc_trk[i]], writes=[yinb[k]])
                    tv = em.v(lambda e, i=i, k=k, ex=ex, psY=psY:
                              e.tensor_scalar_mul(
                        yout[k][:], psY[:], MSK[:, i * 4 + ex:i * 4 + ex + 1]),
                        reads=[psYb, MSKb], writes=[youtb[k]])
                    tv2 = em.v(lambda e, k=k: e.tensor_tensor(
                        yout[k][:], yout[k][:], yin[k][:], ALU.add),
                        deps=[tv], reads=[yinb[k]], writes=[youtb[k]])
                    em.dma("gpsimd", lambda e, i=i, k=k: e.dma_start(
                        out=YACC[i * 128:(i + 1) * 128, :], in_=yout[k][:]),
                        reads=[youtb[k]], writes=[yacc_trk[i]])
                else:
                    em.dma("sync", lambda e, i=i, k=k: e.dma_start(
                        out=yin[k][:], in_=YACC[i * 128:(i + 1) * 128, :]),
                        reads=[yacc_trk[i]], writes=[yinb[k]])
                    em.dma("sync", lambda e, i=i, k=k: e.dma_start(
                        out=uin[k][:], in_=U[i * 128:(i + 1) * 128, :]),
                        writes=[uinb[k]])
                    em.dma("scalar", lambda e, i=i, k=k: e.dma_start(
                        out=c.xt[k][:], in_=Xin[i * 128:(i + 1) * 128, :]),
                        writes=[c.xtb[k]])
                    tv = em.v(lambda e, i=i, k=k, psY=psY: e.tensor_scalar_mul(
                        yout[k][:], psY[:], MSK[:, i * 4 + 3:i * 4 + 4]),
                        reads=[psYb, MSKb], writes=[youtb[k]])
                    tv2 = em.v(lambda e, k=k: e.tensor_tensor(
                        yout[k][:], yout[k][:], yin[k][:], ALU.add),
                        deps=[tv], reads=[yinb[k]], writes=[youtb[k]])
                    tv3 = em.v(lambda e, k=k: e.tensor_tensor(
                        yout[k][:], yout[k][:], uin[k][:], ALU.add),
                        deps=[tv2], reads=[uinb[k]], writes=[youtb[k]])
                    st = dict(sq=(c.sq[k][:, 0:E], c.sqb[k]),
                              stat=(c.stat[k][:], c.statb[k]))
                    tln = ln_to(em, c, yout[k][:], youtb[k], lnw[k][:],
                                lnwb[k], st, E)
                    em.v(lambda e, k=k: e.tensor_tensor(
                        lnw[k][:], lnw[k][:], c.xt[k][:], ALU.add),
                        deps=[tln], reads=[c.xtb[k]], writes=[lnwb[k]])
                    tgt = out_final if out_final is not None else X2
                    em.dma("gpsimd", lambda e, i=i, k=k, tgt=tgt: e.dma_start(
                        out=tgt[i * 128:(i + 1) * 128, :], in_=lnw[k][:]),
                        reads=[lnwb[k]])


def phase2(c, pstack, W, Xin, X3, XN, OTD, out_final=None):
    nc = c.nc; em = c.em
    ent = pstack.enter_context

    wo_sb = ent(nc.sbuf_tensor("p2_wo", [128, NC * E], BF16)); wob = Buf()
    bo_row = ent(nc.sbuf_tensor("p2_bo", [1, E], BF16)); bob = Buf()
    em.dma("gpsimd", lambda e: e.dma_start(
        out=wo_sb[:].rearrange("p (cc m) -> p cc m", cc=NC),
        in_=W["wo2"].rearrange("(cc p) m -> p cc m", p=128)), writes=[wob])
    em.dma("gpsimd", lambda e: e.dma_start(out=bo_row[:], in_=W["bo2row"]),
           writes=[bob])

    bqc = ent(nc.sbuf_tensor("p2_bqc", [128, 8], F32)); bqcb = Buf()
    bkc = ent(nc.sbuf_tensor("p2_bkc", [128, 8], F32)); bkcb = Buf()
    brc = ent(nc.sbuf_tensor("p2_brc", [64, 8], F32)); brcb = Buf()
    brcs = ent(nc.sbuf_tensor("p2_brcs", [64, 8], F32)); brcsb = Buf()
    bqcs = ent(nc.sbuf_tensor("p2_bqcs", [128, 8], F32)); bqcsb = Buf()
    bv_row = ent(nc.sbuf_tensor("p2_bv", [1, E], BF16)); bvb = Buf()
    em.dma("sync", lambda e: e.dma_start(out=bqc[:], in_=W["bq2col"]), writes=[bqcb])
    em.dma("sync", lambda e: e.dma_start(out=bkc[:], in_=W["bk2col"]), writes=[bkcb])
    em.dma("sync", lambda e: e.dma_start(out=brc[:], in_=W["br2col"]), writes=[brcb])
    em.dma("gpsimd", lambda e: e.dma_start(out=bv_row[:], in_=W["bv2row"]),
           writes=[bvb])
    em.v(lambda e: e.tensor_scalar_mul(bqcs[:], bqc[:], SCALE96), reads=[bqcb],
         writes=[bqcsb])
    em.v(lambda e: e.tensor_scalar_mul(brcs[:], brc[:], SCALE96), reads=[brcb],
         writes=[brcsb])

    xnbf = [ent(nc.sbuf_tensor(f"p2_xnbf{k}", [128, E], BF16)) for k in range(2)]
    xnbfb = [Buf(), Buf()]
    for i in range(NT):
        k = i % 2
        st = dict(sq=(c.sq[k][:, 0:E], c.sqb[k]), stat=(c.stat[k][:], c.statb[k]))
        em.dma("sync", lambda e, i=i, k=k: e.dma_start(
            out=c.xt[k][:], in_=Xin[i * 128:(i + 1) * 128, :]), writes=[c.xtb[k]])
        ln_to(em, c, c.xt[k][:], c.xtb[k], xnbf[k][:], xnbfb[k], st, E)
        em.dma("gpsimd", lambda e, i=i, k=k: e.dma_start(
            out=XN[i * 128:(i + 1) * 128, :], in_=xnbf[k][:]), reads=[xnbfb[k]])

    em.barrier()

    QTh = [ent(nc.sbuf_tensor(f"p2_QT{j}", [96, S], BF16)) for j in range(2)]
    QThb = [Buf(), Buf()]
    KTh = [ent(nc.sbuf_tensor(f"p2_KT{j}", [96, S], BF16)) for j in range(2)]
    KThb = [Buf(), Buf()]
    Vh = [ent(nc.sbuf_tensor(f"p2_V{j}", [128, NT * 65], BF16)) for j in range(2)]
    Vhb = [Buf(), Buf()]
    xnT = [ent(nc.sbuf_tensor(f"p2_xnT{k}", [128, NC * 512], BF16))
           for k in range(2)]
    xnTb = [Buf(), Buf()]
    wqp = ent(nc.sbuf_tensor("p2_wqp", [128, NC * 128], BF16)); wqpb = Buf()
    wkp = ent(nc.sbuf_tensor("p2_wkp", [128, NC * 128], BF16)); wkpb = Buf()
    wvp = ent(nc.sbuf_tensor("p2_wvp", [128, NC * 128], BF16)); wvpb = Buf()
    wrp = ent(nc.sbuf_tensor("p2_wrp", [128, NC * 64], BF16)); wrpb = Buf()
    P_sb = [ent(nc.sbuf_tensor(f"p2_P{k}", [128, 1024], BF16)) for k in range(3)]
    P_sbb = [[Buf(), Buf()] for _ in range(3)]
    den = ent(nc.sbuf_tensor("p2_den", [1, 512], F32)); denb = Buf()
    rcrep = ent(nc.sbuf_tensor("p2_rcrep", [64, 512], F32)); rcrepb = Buf()
    ot = [ent(nc.sbuf_tensor(f"p2_ot{k}", [64, 512], BF16)) for k in range(2)]
    otb = [Buf(), Buf()]
    cfg = dict(P_sb=P_sb, P_sbb=P_sbb, den=den, denb=denb, rcrep=rcrep,
               rcrepb=rcrepb, ot=ot, otb=otb, masks_sb=None, masksb=None)

    for p in range(8):
        em.dma("gpsimd", lambda e, p=p: e.dma_start(
            out=wqp[:].rearrange("pp (cc m) -> pp cc m", cc=NC),
            in_=W["wq2"].rearrange("(cc pp) m -> pp cc m", pp=128)[
                :, :, p * 128:(p + 1) * 128]), writes=[wqpb])
        em.dma("gpsimd", lambda e, p=p: e.dma_start(
            out=wkp[:].rearrange("pp (cc m) -> pp cc m", cc=NC),
            in_=W["wk2"].rearrange("(cc pp) m -> pp cc m", pp=128)[
                :, :, p * 128:(p + 1) * 128]), writes=[wkpb])
        em.dma("gpsimd", lambda e, p=p: e.dma_start(
            out=wvp[:].rearrange("pp (cc m) -> pp cc m", cc=NC),
            in_=W["wv2"].rearrange("(cc pp) m -> pp cc m", pp=128)[
                :, :, p * 128:(p + 1) * 128]), writes=[wvpb])
        em.dma("gpsimd", lambda e, p=p: e.dma_start(
            out=wrp[:].rearrange("pp (cc m) -> pp cc m", cc=NC),
            in_=W["wr2"].rearrange("(cc pp) m -> pp cc m", pp=128)[
                :, :, p * 64:(p + 1) * 64]), writes=[wrpb])
        for s in range(8):
            ks = s % 2
            ld = []
            war = xnTb[ks].wr()
            for cc in range(NC):
                ld.append(em.dma("sync", lambda e, s=s, cc=cc, ks=ks: e.dma_start(
                    out=xnT[ks][:, cc * 512:(cc + 1) * 512],
                    in_=XN[s * 512:(s + 1) * 512,
                           cc * 128:(cc + 1) * 128], transpose=True),
                    deps=war))
            xnTb[ks].w = None; xnTb[ks].wl = list(ld); xnTb[ks].r = []
            tq = []
            for cc in range(NC):
                tq.append(em.t(lambda e, cc=cc, ks=ks: e.matmul(
                    c.psA[:, 0:512], wqp[:, cc * 128:(cc + 1) * 128],
                    xnT[ks][:, cc * 512:(cc + 1) * 512],
                    start=(cc == 0), stop=(cc == NC - 1)),
                    reads=[wqpb, xnTb[ks]],
                    deps=(c.psAb.wr() if cc == 0 else [])))
            c.psAb.w = tq[-1]; c.psAb.wl = []; c.psAb.r = []
            for hh in range(2):
                em.s(lambda e, hh=hh, s=s, p=p: e.activation(
                    QTh[hh][0:64, s * 512:(s + 1) * 512],
                    c.psA[hh * 64:(hh + 1) * 64, 0:512], AF.Identity,
                    bias=bqcs[hh * 64:(hh + 1) * 64, p:p + 1], scale=SCALE96),
                    reads=[c.psAb, bqcsb],
                    writes=([QThb[hh]] if s == 0 else []),
                    writes_also=([QThb[hh]] if s > 0 else []),
                    deps=(QThb[hh].wr() if s == 0 else []))
            tk = []
            for cc in range(NC):
                tk.append(em.t(lambda e, cc=cc, ks=ks: e.matmul(
                    c.psB[:, 0:512], wkp[:, cc * 128:(cc + 1) * 128],
                    xnT[ks][:, cc * 512:(cc + 1) * 512],
                    start=(cc == 0), stop=(cc == NC - 1)),
                    reads=[wkpb, xnTb[ks]],
                    deps=(c.psBb.wr() if cc == 0 else [])))
            c.psBb.w = tk[-1]; c.psBb.wl = []; c.psBb.r = []
            for hh in range(2):
                em.s(lambda e, hh=hh, s=s, p=p: e.activation(
                    KTh[hh][0:64, s * 512:(s + 1) * 512],
                    c.psB[hh * 64:(hh + 1) * 64, 0:512], AF.Identity,
                    bias=bkc[hh * 64:(hh + 1) * 64, p:p + 1], scale=1.0),
                    reads=[c.psBb, bkcb],
                    writes=([KThb[hh]] if s == 0 else []),
                    writes_also=([KThb[hh]] if s > 0 else []),
                    deps=(KThb[hh].wr() if s == 0 else []))
            tr = []
            for cc in range(NC):
                tr.append(em.t(lambda e, cc=cc, ks=ks: e.matmul(
                    c.psC[0:64, 0:512], wrp[:, cc * 64:(cc + 1) * 64],
                    xnT[ks][:, cc * 512:(cc + 1) * 512],
                    start=(cc == 0), stop=(cc == NC - 1)),
                    reads=[wrpb, xnTb[ks]],
                    deps=(c.psCb.wr() if cc == 0 else [])))
            c.psCb.w = tr[-1]; c.psCb.wl = []; c.psCb.r = []
            for hh in range(2):
                em.s(lambda e, hh=hh, s=s, p=p: e.activation(
                    QTh[hh][64:96, s * 512:(s + 1) * 512],
                    c.psC[hh * 32:(hh + 1) * 32, 0:512], AF.Identity,
                    bias=brcs[hh * 32:(hh + 1) * 32, p:p + 1], scale=SCALE96),
                    reads=[c.psCb, brcsb], writes_also=[QThb[hh]])
                em.s(lambda e, hh=hh, s=s, p=p: e.activation(
                    KTh[hh][64:96, s * 512:(s + 1) * 512],
                    c.psC[hh * 32:(hh + 1) * 32, 0:512], AF.Identity,
                    bias=brc[hh * 32:(hh + 1) * 32, p:p + 1], scale=1.0),
                    reads=[c.psCb, brcb], writes_also=[KThb[hh]])
            for tt in range(4):
                i = s * 4 + tt
                tv = []
                for cc in range(NC):
                    tv.append(em.t(lambda e, cc=cc, tt=tt, ks=ks: e.matmul(
                        c.psD[:, 0:128],
                        xnT[ks][:, cc * 512 + tt * 128:cc * 512 + (tt + 1) * 128],
                        wvp[:, cc * 128:(cc + 1) * 128],
                        start=(cc == 0), stop=False),
                        reads=[wvpb, xnTb[ks]],
                        deps=(c.psDb.wr() if cc == 0 else [])))
                tv.append(em.t(lambda e, p=p: e.matmul(
                    c.psD[:, 0:128], c.ones_bf[0:1, :],
                    bv_row[0:1, p * 128:(p + 1) * 128], start=False, stop=True),
                    reads=[bvb, c.constb]))
                c.psDb.w = tv[-1]; c.psDb.wl = []; c.psDb.r = []
                for hh in range(2):
                    em.s(lambda e, hh=hh, i=i: e.copy(
                        Vh[hh][:, i * 65:i * 65 + 64],
                        c.psD[:, hh * 64:(hh + 1) * 64]),
                        reads=[c.psDb],
                        writes=([Vhb[hh]] if i == 0 else []),
                        writes_also=([Vhb[hh]] if i > 0 else []),
                        deps=(Vhb[hh].wr() if i == 0 else []))
                    em.v(lambda e, hh=hh, i=i: e.memset(
                        Vh[hh][:, i * 65 + 64:i * 65 + 65], 1.0),
                        writes_also=[Vhb[hh]])
        flash_fork(c)
        for hh in range(2):
            h = 2 * p + hh
            for qb in range(8):
                kq = (h * 8 + qb) % 2
                flash_head(c, em, h, qb, kq, KTh[hh][:], KThb[hh], Vh[hh][:],
                           Vhb[hh], QTh[hh][:, qb * 512:(qb + 1) * 512],
                           QThb[hh], cfg, OTD, causal=False, Kd=96)
        if cfg.get("pending_tail"):
            cfg["pending_tail"]()
            cfg["pending_tail"] = None
        flash_merge(c)

    em.barrier()

    with contextlib.ExitStack() as ps2:
        tgt = out_final if out_final is not None else X3
        proj_residual(c, em, ps2, OTD, Xin, tgt, wo_sb[:], wob, bo_row, bob,
                      "p2c")


def phase3(c, pstack, W, Xin, out):
    nc = c.nc; em = c.em
    ent = pstack.enter_context

    t1sb = ent(nc.sbuf_tensor("p3_t1", [128, NC * 4096], BF16)); t1b = Buf()
    t2sb = ent(nc.sbuf_tensor("p3_t2", [128, 32 * E], BF16)); t2b = Buf()
    stat2 = ent(nc.sbuf_tensor("p3_stat2", [128, 16], F32)); stat2b = Buf()
    g1c = ent(nc.sbuf_tensor("p3_g1c", [128, 8], F32)); g1cb = Buf()
    b1lnc = ent(nc.sbuf_tensor("p3_b1lnc", [128, 8], BF16)); b1lncb = Buf()
    g2c = ent(nc.sbuf_tensor("p3_g2c", [128, 32], F32)); g2cb = Buf()
    b2lnc = ent(nc.sbuf_tensor("p3_b2lnc", [128, 32], BF16)); b2lncb = Buf()
    bv1bf = ent(nc.sbuf_tensor("p3_bv1bf", [1, 4096], BF16)); bv1bfb = Buf()
    bv2bf = ent(nc.sbuf_tensor("p3_bv2bf", [1, E], BF16)); bv2bfb = Buf()

    em.dma("sync", lambda e: e.dma_start(out=g1c[:], in_=W["l1g1col"]), writes=[g1cb])
    em.dma("gpsimd", lambda e: e.dma_start(out=b1lnc[:], in_=W["l1b1lncol"]),
           writes=[b1lncb])
    em.dma("sync", lambda e: e.dma_start(out=g2c[:], in_=W["l1g2col"]), writes=[g2cb])
    em.dma("gpsimd", lambda e: e.dma_start(out=b2lnc[:], in_=W["l1b2lncol"]),
           writes=[b2lncb])

    prep = contextlib.ExitStack()
    entp = prep.enter_context
    _wch = entp(nc.sbuf_tensor("p3_wch", [128, 4096], F32))
    wch = [_wch, _wch]
    _wchb = Buf()
    wchb = [_wchb, _wchb]
    srow = entp(nc.sbuf_tensor("p3_srow", [1, 16], F32)); srowb = Buf()
    bsl = entp(nc.sbuf_tensor("p3_bsl", [1, 512], BF16)); bslb = Buf()

    def quantize(wT_ap, nchunk, chw, tgt_sb, tgt_b, lo):
        acc = stat2[:, lo + 4:lo + 5]
        part = stat2[:, lo + 5:lo + 6]
        ta = em.v(lambda e: e.memset(acc, 0.0), writes_also=[stat2b], risky=True)
        tprev = ta
        for ch in range(nchunk):
            kc = ch % 2
            for sub in range(0, chw, E):
                em.dma("sync", lambda e, ch=ch, sub=sub, kc=kc: e.dma_start(
                    out=wch[kc][:, sub:sub + E],
                    in_=wT_ap[ch * 128:(ch + 1) * 128, sub:sub + E]),
                    writes=([wchb[kc]] if sub == 0 else []),
                    writes_also=([wchb[kc]] if sub else []))
            tb_ = em.s(lambda e, kc=kc, chw=chw: e.activation(
                c.sq[kc][:, 0:E], wch[kc][:, 0:chw] if chw <= E else wch[kc][:, 0:E],
                AF.Abs, bias=0.0, scale=1.0, accum_out=part),
                risky=True, reads=[wchb[kc]], writes=[c.sqb[kc]],
                writes_also=[stat2b], deps=[tprev]) if chw <= E else None
            if chw > E:
                for sub in range(0, chw, E):
                    tb_ = em.s(lambda e, kc=kc, sub=sub: e.activation(
                        c.sq[kc][:, 0:E], wch[kc][:, sub:sub + E],
                        AF.Abs, bias=0.0, scale=1.0, accum_out=part),
                        risky=True, reads=[wchb[kc]], writes=[c.sqb[kc]],
                        writes_also=[stat2b], deps=[tprev])
                    tprev = em.v(lambda e: e.tensor_tensor(acc, acc, part,
                                                           ALU.add),
                                 deps=[tb_], risky=True, writes_also=[stat2b])
            else:
                tprev = em.v(lambda e: e.tensor_tensor(acc, acc, part, ALU.add),
                             deps=[tb_], risky=True, writes_also=[stat2b])
        tsum = em.t(lambda e: e.matmul(c.psD[0:1, 0:1], acc, c.ones_f32[:, 0:1],
                                       start=True, stop=True),
                    deps=[tprev] + c.psDb.wr(), reads=[stat2b, c.constb])
        c.psDb.w = tsum; c.psDb.wl = []; c.psDb.r = []
        tm = em.s(lambda e: e.copy(srow[0:1, lo:lo + 1], c.psD[0:1, 0:1]),
                  deps=[tsum], reads=[c.psDb], writes_also=[srowb])
        tm2 = em.v(lambda e: e.tensor_scalar(
            srow[0:1, lo:lo + 1], srow[0:1, lo:lo + 1],
            1.0 / (4096.0 * 1024.0), 1e-5, ALU.mult, ALU.max),
            deps=[tm], risky=True, writes_also=[srowb])
        trep = em.t(lambda e: e.matmul(c.psD[:, 0:1], c.ones_f32[0:1, :],
                                       srow[0:1, lo:lo + 1],
                                       start=True, stop=True),
                    deps=[tm2] + c.psDb.wr(), reads=[srowb, c.constb])
        c.psDb.w = trep; c.psDb.wl = []; c.psDb.r = []
        tmc = em.s(lambda e: e.copy(stat2[:, lo:lo + 1], c.psD[:, 0:1]),
                   deps=[trep], reads=[c.psDb], writes_also=[stat2b])
        t_thr = em.v(lambda e: e.tensor_scalar_mul(
            stat2[:, lo + 1:lo + 2], stat2[:, lo:lo + 1], 0.5),
            deps=[tmc], risky=True, writes_also=[stat2b])
        t_nthr = em.v(lambda e: e.tensor_scalar_mul(
            stat2[:, lo + 2:lo + 3], stat2[:, lo:lo + 1], -0.5),
            deps=[tmc], risky=True, writes_also=[stat2b])
        t_im = em.v(lambda e: e.reciprocal(
            stat2[:, lo + 3:lo + 4], stat2[:, lo:lo + 1]),
            deps=[tmc], risky=True, writes_also=[stat2b])
        for ch in range(nchunk):
            kc = ch % 2
            for sub in range(0, chw, E):
                em.dma("scalar", lambda e, ch=ch, sub=sub, kc=kc: e.dma_start(
                    out=wch[kc][:, sub:sub + E],
                    in_=wT_ap[ch * 128:(ch + 1) * 128, sub:sub + E]),
                    writes=([wchb[kc]] if sub == 0 else []),
                    writes_also=([wchb[kc]] if sub else []))
            for sub in range(0, chw, E):
                tg_ = em.v(lambda e, kc=kc, sub=sub: e.tensor_scalar(
                    c.tmpA[kc][:, 0:E], wch[kc][:, sub:sub + E],
                    stat2[:, lo + 1:lo + 2], None, ALU.is_gt),
                    deps=[t_thr], reads=[wchb[kc], stat2b],
                    writes=[c.tmpAb[kc]])
                tl_ = em.v(lambda e, kc=kc, sub=sub: e.tensor_scalar(
                    c.tmpB[kc][:, 0:E], wch[kc][:, sub:sub + E],
                    stat2[:, lo + 2:lo + 3], None, ALU.is_lt),
                    deps=[t_nthr], reads=[wchb[kc], stat2b],
                    writes=[c.tmpBb[kc]])
                em.v(lambda e, ch=ch, kc=kc, sub=sub, chw=chw: e.tensor_tensor(
                    tgt_sb[:, ch * chw + sub:ch * chw + sub + E],
                    c.tmpA[kc][:, 0:E], c.tmpB[kc][:, 0:E], ALU.subtract),
                    deps=[tg_, tl_], reads=[c.tmpAb[kc], c.tmpBb[kc]],
                    writes_also=[tgt_b])

    quantize(W["l1w1T"], NC, 4096, t1sb, t1b, 0)
    quantize(W["l1w2T"], 32, E, t2sb, t2b, 8)

    for jb in range(8):
        tb_ = []
        for cc in range(NC):
            tb_.append(em.t(lambda e, jb=jb, cc=cc: e.matmul(
                c.psC[0:1, 0:512], b1lnc[:, cc:cc + 1],
                t1sb[:, cc * 4096 + jb * 512:cc * 4096 + (jb + 1) * 512],
                start=(cc == 0), stop=(cc == NC - 1)),
                reads=[b1lncb, t1b],
                deps=(c.psCb.wr() if cc == 0 else [])))
        c.psCb.w = tb_[-1]; c.psCb.wl = []; c.psCb.r = []
        em.dma("gpsimd", lambda e, jb=jb: e.dma_start(
            out=bsl[:], in_=W["l1b1row"][0:1, jb * 512:(jb + 1) * 512]),
            writes=[bslb])
        tsc = em.v(lambda e, jb=jb: e.tensor_scalar_mul(
            c.sq[0][0:1, 0:512], bsl[0:1, :], stat2[0:1, 3:4]),
            reads=[bslb, stat2b], writes=[c.sqb[0]], risky=True)
        em.v(lambda e, jb=jb: e.tensor_tensor(
            bv1bf[0:1, jb * 512:(jb + 1) * 512], c.psC[0:1, 0:512],
            c.sq[0][0:1, 0:512], ALU.add),
            deps=[tsc], reads=[c.psCb, c.sqb[0]], writes_also=[bv1bfb], risky=True)
    for half in range(2):
        tb_ = []
        for js in range(32):
            tb_.append(em.t(lambda e, js=js, half=half: e.matmul(
                c.psC[0:1, 0:512], b2lnc[:, js:js + 1],
                t2sb[:, js * E + half * 512:js * E + (half + 1) * 512],
                start=(js == 0), stop=(js == 31)),
                reads=[b2lncb, t2b],
                deps=(c.psCb.wr() if js == 0 else [])))
        c.psCb.w = tb_[-1]; c.psCb.wl = []; c.psCb.r = []
        em.dma("gpsimd", lambda e, half=half: e.dma_start(
            out=bsl[:], in_=W["l1b2row"][0:1, half * 512:(half + 1) * 512]),
            writes=[bslb])
        tsc = em.v(lambda e, half=half: e.tensor_scalar_mul(
            c.sq[0][0:1, 0:512], bsl[0:1, :], stat2[0:1, 11:12]),
            reads=[bslb, stat2b], writes=[c.sqb[0]], risky=True)
        em.v(lambda e, half=half: e.tensor_tensor(
            bv2bf[0:1, half * 512:(half + 1) * 512], c.psC[0:1, 0:512],
            c.sq[0][0:1, 0:512], ALU.add),
            deps=[tsc], reads=[c.psCb, c.sqb[0]], writes_also=[bv2bfb], risky=True)
    for cc in range(NC):
        em.v(lambda e, cc=cc: e.tensor_scalar_mul(
            t1sb[:, cc * 4096:(cc + 1) * 4096], t1sb[:, cc * 4096:(cc + 1) * 4096],
            g1c[:, cc:cc + 1]), reads=[g1cb, t1b], writes_also=[t1b])
    for js in range(32):
        em.v(lambda e, js=js: e.tensor_scalar_mul(
            t2sb[:, js * E:(js + 1) * E], t2sb[:, js * E:(js + 1) * E],
            g2c[:, js:js + 1]), reads=[g2cb, t2b], writes_also=[t2b])

    em.barrier()
    prep.close()

    n1bf = [ent(nc.sbuf_tensor(f"p3_n1{k}", [128, E], BF16)) for k in range(2)]
    n1bfb = [Buf(), Buf()]
    n1T = [ent(nc.sbuf_tensor(f"p3_n1T{k}", [128, NC * 128], BF16))
           for k in range(2)]
    n1Tb = [Buf(), Buf()]
    hbf_l = [ent(nc.sbuf_tensor(f"p3_h{k}", [128, 4096], BF16))
             for k in range(2)]
    hbfb_l = [Buf(), Buf()]
    n2hT_s = ent(nc.sbuf_tensor("p3_n2hT", [128, 32 * 128], BF16))
    n2hTb_s = Buf()
    ysb = c.tmpB; ysbb = c.tmpBb
    lno = c.tmpA; lnob = c.tmpAb

    m1col = stat2[:, 0:1]
    m2col = stat2[:, 8:9]

    def ln4k(x_ap, x_buf, out_ap, out_buf, k):
        stat, stb = c.stat[k], c.statb[k]
        s1 = stat[:, 0:1]; s2 = stat[:, 1:2]; rstd = stat[:, 2:3]
        negmr = stat[:, 3:4]; tmp = stat[:, 4:5]
        p1 = stat[:, 8:9]; p2 = stat[:, 9:10]
        tz = None
        for q in range(4):
            tsq = em.s(lambda e, q=q: e.activation(
                c.sq[k][:, 0:E], x_ap[:, q * E:(q + 1) * E], AF.Square,
                bias=0.0, scale=1.0, accum_out=(s2 if q == 0 else p2)),
                risky=True, reads=[x_buf], writes=[c.sqb[k]], writes_also=[stb],
                deps=([tz] if tz else []))
            trs = em.v(lambda e, q=q: e.reduce_sum(
                (s1 if q == 0 else p1), x_ap[:, q * E:(q + 1) * E], axis=AX.X),
                deps=[tsq], risky=True, reads=[x_buf], writes_also=[stb])
            if q > 0:
                t1_ = em.v(lambda e: e.tensor_tensor(s1, s1, p1, ALU.add),
                           deps=[trs], risky=True, writes_also=[stb])
                tz = em.v(lambda e: e.tensor_tensor(s2, s2, p2, ALU.add),
                          deps=[t1_, tsq], risky=True, writes_also=[stb])
            else:
                tz = trs
        t_a = em.v(lambda e: e.tensor_scalar(s1, s1, 1.0 / 4096.0, None, ALU.mult),
                   deps=[tz], risky=True, writes_also=[stb])
        t_b = em.v(lambda e: e.tensor_scalar(s2, s2, 1.0 / 4096.0, None, ALU.mult),
                   deps=[t_a], risky=True, writes_also=[stb])
        t_c = em.v(lambda e: e.tensor_tensor(tmp, s1, s1, ALU.mult),
                   deps=[t_b], risky=True, writes_also=[stb])
        t_d = em.v(lambda e: e.tensor_tensor(s2, s2, tmp, ALU.subtract),
                   deps=[t_c], risky=True, writes_also=[stb])
        t_ln = em.s(lambda e: e.activation(tmp, s2, AF.Ln, bias=c.eps5[:, 0:1],
                                           scale=1.0),
                    deps=[t_d], risky=True, reads=[c.constb], writes_also=[stb])
        t_ex = em.s(lambda e: e.activation(rstd, tmp, AF.Exp, bias=0.0,
                                           scale=-0.5),
                    deps=[t_ln], risky=True, writes_also=[stb])
        t_nm = em.v(lambda e: e.tensor_scalar(negmr, s1, -1.0, None, ALU.mult),
                    deps=[t_ex], risky=True, writes_also=[stb])
        toks = []
        for q in range(4):
            toks.append(em.v(lambda e, q=q: e.tensor_scalar(
                out_ap[:, q * E:(q + 1) * E], x_ap[:, q * E:(q + 1) * E],
                negmr[:, 0:1], rstd[:, 0:1], ALU.add, ALU.mult),
                deps=[t_nm, t_ex], risky=True, reads=[x_buf, stb],
                writes=([out_buf] if q == 3 else []),
                writes_also=([out_buf] if q < 3 else [])))
        return toks[-1]

    def emitA(i):
        k = i % 2
        hbf = hbf_l[k]; hbfb = hbfb_l[k]
        st = dict(sq=(c.sq[k][:, 0:E], c.sqb[k]), stat=(c.stat[k][:], c.statb[k]))
        em.dma("sync", lambda e, i=i, k=k: e.dma_start(
            out=c.xt[k][:], in_=Xin[i * 128:(i + 1) * 128, :]), writes=[c.xtb[k]])
        ln_compose(em, c, c.xt[k][:], c.xtb[k], n1bf[k][:], n1bfb[k], st, E)
        for cc in range(NC):
            transpose_128(em, c, n1bf[k][:, cc * 128:(cc + 1) * 128], [n1bfb[k]],
                          n1T[k][:, cc * 128:(cc + 1) * 128],
                          [n1Tb[k]], cc % 2,
                          deps=(n1Tb[k].wr() if cc == 0 else []), final=(cc == NC - 1))
        for jb in range(8):
            psH, psHb = (c.psC, c.psCb) if jb % 2 == 0 else (c.psD, c.psDb)
            th = []
            for cc in range(NC):
                th.append(em.t(lambda e, jb=jb, cc=cc, k=k, psH=psH: e.matmul(
                    psH[:, 0:512], n1T[k][:, cc * 128:(cc + 1) * 128],
                    t1sb[:, cc * 4096 + jb * 512:cc * 4096 + (jb + 1) * 512],
                    start=(cc == 0), stop=False),
                    reads=[n1Tb[k], t1b],
                    deps=(psHb.wr() if cc == 0 else [])))
            th.append(em.t(lambda e, jb=jb, psH=psH: e.matmul(
                psH[:, 0:512], c.ones_bf[0:1, :],
                bv1bf[0:1, jb * 512:(jb + 1) * 512], start=False, stop=True),
                reads=[bv1bfb, c.constb]))
            psHb.w = th[-1]; psHb.wl = []; psHb.r = []
            em.s(lambda e, jb=jb, psH=psH, hbf=hbf: e.activation(
                hbf[:, jb * 512:(jb + 1) * 512], psH[:, 0:512], AF.Gelu,
                bias=0.0, scale=m1col),
                reads=[psHb, stat2b],
                writes=([hbfb] if jb == 0 else []),
                writes_also=([hbfb] if jb > 0 else []),
                deps=(hbfb.wr() if jb == 0 else []))
    def emitL(i):
        k = i % 2
        hbf = hbf_l[k]; hbfb = hbfb_l[k]
        ln4k(hbf[:], hbfb, hbf[:], hbfb, k)

    def emitB(i):
        k = i % 2
        hbf = hbf_l[k]; hbfb = hbfb_l[k]
        n2hT = n2hT_s; n2hTb = n2hTb_s
        psY, psYb = (c.psA, c.psAb) if k == 0 else (c.psB, c.psBb)
        st = dict(sq=(c.sq[k][:, 0:E], c.sqb[k]), stat=(c.stat[k][:], c.statb[k]))
        for js in range(32):
            transpose_128(em, c, hbf[:, js * 128:(js + 1) * 128], [hbfb],
                          n2hT[:, js * 128:(js + 1) * 128],
                          [n2hTb], js % 2,
                          deps=(n2hTb.wr() if js == 0 else []), final=(js == 31))
        ty = []
        for js in range(32):
            for half in range(2):
                ty.append(em.t(lambda e, js=js, half=half, n2hT=n2hT,
                               psY=psY: e.matmul(
                    psY[:, half * 512:(half + 1) * 512],
                    n2hT[:, js * 128:(js + 1) * 128],
                    t2sb[:, js * E + half * 512:js * E + (half + 1) * 512],
                    start=(js == 0), stop=False),
                    reads=[n2hTb, t2b],
                    deps=(psYb.wr() if js == 0 and half == 0 else [])))
        for half in range(2):
            ty.append(em.t(lambda e, half=half, psY=psY: e.matmul(
                psY[:, half * 512:(half + 1) * 512], c.ones_bf[0:1, :],
                bv2bf[0:1, half * 512:(half + 1) * 512], start=False, stop=True),
                reads=[bv2bfb, c.constb]))
        psYb.w = ty[-1]; psYb.wl = []; psYb.r = []
        tev = em.s(lambda e, k=k, psY=psY: e.activation(
            ysb[k][:], psY[:], AF.Identity, bias=0.0, scale=m2col),
            reads=[psYb, stat2b], writes=[ysbb[k]])
        tlno = ln_to(em, c, ysb[k][:], ysbb[k], lno[k][:], lnob[k], st, E)
        em.v(lambda e, k=k: e.tensor_tensor(lno[k][:], lno[k][:],
                                            c.xt[k][:], ALU.add),
             deps=[tlno], reads=[c.xtb[k]], writes=[lnob[k]])
        em.dma("gpsimd", lambda e, i=i, k=k: e.dma_start(
            out=out[i * 128:(i + 1) * 128, :], in_=lno[k][:]), reads=[lnob[k]])

    for i in range(NT):
        emitA(i)
        emitL(i)
        emitB(i)



def kernel(**inputs):
    x = np.asarray(inputs["L0_attn_x"], dtype=np.float32)
    prep = host_prep(inputs)
    nc = build(phases=PHASES)
    base = {k: np.ascontiguousarray(prep[k], dtype=np.float32) for k in HOST_KEYS}
    in_maps = []
    for b in range(8):
        m = dict(base)
        m["x"] = np.ascontiguousarray(x[b])
        in_maps.append(m)
    res = run_bass_kernel_spmd(nc, in_maps, core_ids=list(range(8)))
    return np.stack([res.results[b]["out"] for b in range(8)], axis=0)
